# version 11
# baseline (speedup 1.0000x reference)
import math
from contextlib import ExitStack

import numpy as np
import ml_dtypes

import concourse.bass as bass
import concourse.mybir as mybir
from concourse.bass_utils import run_bass_kernel_spmd

F32 = mybir.dt.float32
BF16 = mybir.dt.bfloat16
I8 = mybir.dt.int8
AF = mybir.ActivationFunctionType
ALU = mybir.AluOpType
AX = mybir.AxisListType
BF = ml_dtypes.bfloat16

D = 2048
T = 2048
SEQ = 4096
NB = 16
NT = 16
H0, HD0 = 16, 128
H1, HD1 = 32, 64
RANK = 16
LORA_SCALE = 0.5
ATT_SCALE = HD0 ** (-0.5)
FMOE = 1408
NFC = 11
FSH = 2816
NEXP = 8
FF1 = 8192
EPS = 1e-5

_NPDT = {np.dtype(np.float32): F32, np.dtype(BF): BF16, np.dtype(np.int8): I8}

FUSED = True


def _c(a):
    return np.ascontiguousarray(a)



def host_prep(inp):
    sh = {}
    sh["wqT"] = _c(inp["L0_attn_Wq"].T)
    sh["wkT"] = _c(inp["L0_attn_Wk"].T)
    sh["wvT"] = _c(inp["L0_attn_Wv"].T)
    sh["woT"] = _c(inp["L0_attn_Wo"].T)
    for p in "qkvo":
        sh[f"B{p}T"] = _c(inp[f"L0_attn_B{p}"].T)
        sh[f"A{p}T"] = _c(inp[f"L0_attn_A{p}"].T)
    sh["bq_row"] = _c(inp["L0_attn_bq"].reshape(1, D))
    sh["bk_row"] = _c(inp["L0_attn_bk"].reshape(1, D))
    sh["bv_row"] = _c(inp["L0_attn_bv"].reshape(1, D))
    sh["bo_row"] = _c(inp["L0_attn_bo"].reshape(1, D))
    sh["gwT"] = _c(inp["L0_ffn_gate_weight"].T)
    sh["wgTe"] = _c(inp["L0_ffn_gate_w"].transpose(0, 2, 1))
    sh["wuTe"] = _c(inp["L0_ffn_up_w"].transpose(0, 2, 1))
    sh["wdTe"] = _c(inp["L0_ffn_down_w"].transpose(0, 2, 1))
    sh["shgT"] = _c(inp["L0_ffn_sh_gate_w"].T)
    sh["shuT"] = _c(inp["L0_ffn_sh_up_w"].T)
    sh["shdT"] = _c(inp["L0_ffn_sh_down_w"].T)
    sh["wqkvT"] = _c(inp["L1_attn_wqkv"].T)
    sh["wg1T"] = _c(inp["L1_attn_wg"].T)
    sh["wo1T"] = _c(inp["L1_attn_wo"].T)
    sh["gnw2"] = _c(inp["L1_attn_gn_weight"].reshape(16, 2).T)
    sh["gnb2"] = _c(inp["L1_attn_gn_bias"].reshape(16, 2).T)
    sh["f1g"] = inp["L1_ffn_w_gate"]
    sh["f1u"] = inp["L1_ffn_w_up"]
    sh["f1d"] = inp["L1_ffn_w_down"]
    sh["sgc"] = _c(inp["L1_ffn_s_gate"].reshape(FF1 // 128, 128).T)
    sh["suc"] = _c(inp["L1_ffn_s_up"].reshape(FF1 // 128, 128).T)
    sh["sdc"] = _c(inp["L1_ffn_s_down"].reshape(NB, 128).T)

    x = inp["L0_attn_x"]
    percore = []
    for c in range(8):
        b, hf = c // 2, c % 2
        pc = {}
        pc["xT"] = _c(x[b, hf * T:(hf + 1) * T, :].T)
        pc["x_tm"] = _c(x[b, hf * T:(hf + 1) * T, :])
        pc.update(host_consts(hf))
        percore.append(pc)
    return sh, percore


def host_consts(half):
    c = {}
    c["c_eye_bf"] = np.eye(128, dtype=BF)
    c["c_eye_f32"] = np.eye(128, dtype=np.float32)
    c["c_ones_bf"] = np.ones((128, 512), dtype=BF)
    c["c_ones_f32"] = np.ones((128, 512), dtype=np.float32)
    hp = np.zeros((128, 2), dtype=np.float32)
    hp[0:64, 0] = 1.0
    hp[64:128, 1] = 1.0
    c["c_halfpick"] = hp
    c["c_hpT"] = _c(hp.T)

    dim = 32
    t0 = half * T
    pos = np.arange(t0, t0 + T, dtype=np.float64)
    inv_freq = 1.0 / 10000.0 ** (np.arange(0, dim, 2, dtype=np.float64) / dim)
    freqs = np.repeat(pos[:, None] * inv_freq[None, :], 2, axis=-1)
    base_scale = (np.arange(0, dim, 2, dtype=np.float64) + 0.4 * dim) / (1.4 * dim)
    power = (pos - SEQ // 2) / 512.0
    scale = base_scale[None, :] ** power[:, None]
    scale = np.concatenate([scale, scale], axis=-1)
    c["c_cq"] = (np.cos(freqs) * scale).astype(np.float32)
    c["c_sq"] = (np.sin(freqs) * scale).astype(np.float32)
    c["c_ck"] = (np.cos(freqs) / scale).astype(np.float32)
    c["c_sk"] = (np.sin(freqs) / scale).astype(np.float32)

    gam = 1.0 - 2.0 ** (-5.0 - np.arange(H1, dtype=np.float64))
    i = np.arange(128, dtype=np.float64)
    mask = (i[None, :] >= i[:, None]).astype(np.float64)
    d0 = mask[None] * (gam[:, None, None] ** (-i)[:, None])
    c["c_d0"] = _c(d0.transpose(1, 0, 2).reshape(128, H1 * 128).astype(np.float32))
    qdec = gam[None, :, None] ** i[:, None, None]
    c["c_qdec"] = _c(np.broadcast_to(qdec, (128, H1, HD1)).reshape(128, D).astype(np.float32))
    kdec = gam[None, :, None] ** (128.0 - i)[:, None, None]
    c["c_kdec"] = _c(np.broadcast_to(kdec, (128, H1, HD1)).reshape(128, D).astype(np.float32))
    c["c_kdec_prev"] = c["c_kdec"] * np.float32(1.0 if half == 1 else 0.0)
    sdec = np.broadcast_to((gam ** 128.0)[None, :, None], (64, H1, HD1))
    c["c_sdec"] = _c(sdec.reshape(64, H1 * HD1).astype(np.float32))
    return c



class MK:
    def __init__(self, name, ext_in=(), ext_out=()):
        self.nc = bass.Bass("TRN2", target_bir_lowering=False, num_devices=8,
                            name=name)
        self.ext_in = set(ext_in)
        self.ext_out = set(ext_out)
        self.input_names = set()
        self.t = {}
        self.sem = {}
        self.semn = {}
        self.phase_i = 0
        self.block = None
        self._es = ExitStack()
        for s in ("qS", "qG", "qA", "bar", "pe0", "pe1", "pe2", "ev0", "ev1",
                  "ev2", "dv0", "dv1", "sc0", "sc1", "ld0", "ld1", "cc",
                  "ch0", "ch1", "ch2", "ch3", "ch4", "ch5", "ch6", "ch7",
                  "wb0", "wb1", "st0", "st1", "bg0"):
            self.sem[s] = self.nc.alloc_semaphore(s)
            self.semn[s] = 0

    def dram(self, name, shape, dtype):
        if name in self.t:
            return self.t[name]
        if name in self.ext_in:
            kind = "ExternalInput"
        elif name in self.ext_out:
            kind = "ExternalOutput"
        else:
            kind = "Internal"
        ap = self.nc.dram_tensor(name, list(shape), dtype, kind=kind).ap()
        if kind == "ExternalInput":
            self.input_names.add(name)
        self.t[name] = ap
        return ap

    def ein(self, name, shape, dtype):
        if name in self.t:
            return self.t[name]
        ap = self.nc.dram_tensor(name, list(shape), dtype, kind="ExternalInput").ap()
        self.input_names.add(name)
        self.t[name] = ap
        return ap

    def inc(self, instr, s, k=1):
        instr.then_inc(self.sem[s], k)
        self.semn[s] += k
        return instr

    def val(self, s):
        return self.semn[s]

    def wait(self, eng, s, val=None):
        eng.wait_ge(self.sem[s], self.semn[s] if val is None else val)

    def begin(self):
        self.block = self._es.enter_context(self.nc.Block("mk"))

    def finish(self):
        self._es.close()
        self.block = None

    DMA_SEMS_SYNC = ("qS", "ch0", "ch1", "ch2", "ch3", "ch4", "ch5", "ch6", "ch7")
    DMA_SEMS_GP = ("qG", "wb0", "wb1", "st0", "st1")

    def barrier(self):
        self.phase_i += 1
        tgt = 5 * self.phase_i
        b = self.sem["bar"]

        def mk_body(qsems):
            vals = [(self.sem[q], self.semn[q]) for q in qsems]

            def f(e):
                for h, v in vals:
                    if v:
                        e.wait_ge(h, v)
                e.drain().then_inc(b, 1)
                e.wait_ge(b, tgt)
            return f
        self.block.sync(mk_body(self.DMA_SEMS_SYNC))
        self.block.gpsimd(mk_body(self.DMA_SEMS_GP))
        self.block.scalar(mk_body(("qA",)))
        self.block.vector(mk_body(()))
        self.block.tensor(mk_body(()))
        self.semn["bar"] += 5


def run_groups(mk, groups, psums, pe_sem="pe0", ev_sem="ev0", evac_eng="scalar",
               pre=None, post_t=None):
    blk = mk.block
    n = len(psums)
    base_pe = mk.val(pe_sem)
    base_ev = mk.val(ev_sem)
    ngr = len(groups)

    def tbody(e):
        if pre is not None:
            pre(e)
        for i, (mm, _) in enumerate(groups):
            v = base_ev + i - n + 1
            if 0 < v:
                e.wait_ge(mk.sem[ev_sem], v)
            mm(e, psums[i % n]).then_inc(mk.sem[pe_sem], 1)
        if post_t is not None:
            post_t(e)

    fin = lambda instr: instr.then_inc(mk.sem[ev_sem], 1)

    def ebody(e):
        for i, (_, ev) in enumerate(groups):
            e.wait_ge(mk.sem[pe_sem], base_pe + i + 1)
            ev(e, psums[i % n], fin)

    blk.tensor(tbody)
    mk.semn[pe_sem] += ngr
    getattr(blk, evac_eng)(ebody)
    mk.semn[ev_sem] += ngr


class Sched:
    ENGS = ("sync", "gpsimd", "scalar", "vector", "tensor")

    def __init__(self, mk):
        self.mk = mk
        self.ops = {e: [] for e in self.ENGS}

    def on(self, eng, fn):
        self.ops[eng].append(fn)

    def wait(self, eng, s, val=None):
        v = self.mk.val(s) if val is None else val
        h = self.mk.sem[s]
        self.ops[eng].append(lambda e: e.wait_ge(h, v))

    def add(self, eng, fn, inc=None, inck=1):
        if inc is None:
            self.ops[eng].append(fn)
        else:
            h = self.mk.sem[inc]
            self.ops[eng].append(lambda e: fn(e).then_inc(h, inck))
            self.mk.semn[inc] += inck

    def emit(self):
        blk = self.mk.block
        for eng in self.ENGS:
            lst = self.ops[eng]
            if not lst:
                continue
            getattr(blk, eng)(lambda e, lst=lst: [f(e) for f in lst])



def ph_qkv0(mk):
    nc, blk = mk.nc, mk.block
    xT = mk.ein("xT", [D, T], F32)
    for w in ("wqT", "wkT", "wvT"):
        mk.ein(w, [D, D], F32)
    for p in "qkv":
        mk.ein(f"B{p}T", [D, RANK], F32)
        mk.ein(f"A{p}T", [RANK, D], F32)
    mk.ein("bq_row", [1, D], F32)
    mk.ein("bk_row", [1, D], F32)
    mk.ein("bv_row", [1, D], F32)
    mk.ein("c_ones_bf", [128, 512], BF16)
    q0f = mk.dram("q0f", [D, T], BF16)
    k0f = mk.dram("k0f", [D, T], BF16)
    v0t = mk.dram("v0t", [T, D], BF16)

    es = ExitStack()
    sb = lambda n_, sh, dt: es.enter_context(nc.sbuf_tensor(n_, sh, dt))
    pst = lambda n_, sh, dt: es.enter_context(nc.psum_tensor(n_, sh, dt))
    with es:
        xfm = sb("a1_xfm", [128, NB * T], BF16)
        wt = sb("a1_wt", [128, NB * D], BF16)
        aT = sb("a1_aT", [RANK, 3 * D], BF16)
        bT = sb("a1_bT", [128, 3 * NB * RANK], BF16)
        zz = sb("a1_z", [RANK, 3 * T], BF16)
        brow = sb("a1_brow", [1, 3 * D], BF16)
        ones = sb("a1_ones", [128, 512], BF16)
        rows = sb("a1_rows", [128, 2 * D], BF16)
        p_mm = [pst(f"a1_pmm{i}", [128, 512], F32) for i in range(4)]
        p_z = [pst(f"a1_pz{i}", [RANK, 512], F32) for i in range(2)]

        def g0(e):
            for dc in range(NB):
                mk.inc(e.dma_start(out=xfm[:, dc * T:(dc + 1) * T],
                                   in_=xT[dc * 128:(dc + 1) * 128, :]), "qG", 16)
            for pi, p in enumerate("qkv"):
                mk.inc(e.dma_start(out=aT[0:RANK, pi * D:(pi + 1) * D],
                                   in_=mk.t[f"A{p}T"]), "qG", 16)
                mk.inc(e.dma_start(
                    out=bT[:, pi * NB * RANK:(pi + 1) * NB * RANK]
                        .rearrange("p (dc r) -> p dc r", dc=NB),
                    in_=mk.t[f"B{p}T"].rearrange("(dc p) r -> p dc r", p=128)),
                    "qG", 16)
            mk.inc(e.dma_start(out=brow[0:1, 0:D], in_=mk.t["bk_row"]), "qG", 16)
            mk.inc(e.dma_start(out=brow[0:1, D:2 * D], in_=mk.t["bv_row"]), "qG", 16)
            mk.inc(e.dma_start(out=brow[0:1, 2 * D:3 * D], in_=mk.t["bq_row"]), "qG", 16)
        blk.gpsimd(g0)
        small_done = mk.val("qG")

        def s0(e):
            mk.inc(e.dma_start(out=ones[:], in_=mk.t["c_ones_bf"]), "qS", 16)
        blk.sync(s0)
        sync_done = mk.val("qS")

        zgroups = []
        for pi in range(3):
            for tt in range(4):
                def mm(e, psz, pi=pi, tt=tt):
                    for dc in range(NB):
                        last = e.matmul(
                            psz[:],
                            bT[:, (pi * NB + dc) * RANK:(pi * NB + dc + 1) * RANK],
                            xfm[:, dc * T + tt * 512: dc * T + (tt + 1) * 512],
                            start=(dc == 0), stop=(dc == NB - 1))
                    return last

                def ev(e, psz, fin, pi=pi, tt=tt):
                    fin(e.activation(
                        out=zz[0:RANK, pi * T + tt * 512: pi * T + (tt + 1) * 512],
                        in_=psz[:], func=AF.Copy, scale=LORA_SCALE))
                zgroups.append((mm, ev))

        def pre_z(e):
            e.wait_ge(mk.sem["qG"], small_done)
        run_groups(mk, zgroups, p_z, pre=pre_z)

        last_store = {0: None, 1: None}
        for p in ("k", "v", "q"):
            pi = "qkv".index(p)
            pe_before = mk.val("pe0")

            def gw(e, p=p, pe_before=pe_before):
                e.wait_ge(mk.sem["pe0"], pe_before)
                for dc in range(NB):
                    mk.inc(e.dma_start(out=wt[:, dc * D:(dc + 1) * D],
                                       in_=mk.t[f"w{p}T"][dc * 128:(dc + 1) * 128, :]),
                           "qG", 16)
            blk.gpsimd(gw)
            w_done = mk.val("qG")

            groups = []
            if p in ("q", "k"):
                fdst = q0f if p == "q" else k0f
                fboff = 2 * D if p == "q" else 0
                for oc in range(NB):
                    for tt in range(4):
                        def mm(e, ps_, oc=oc, tt=tt, pi=pi, fboff=fboff):
                            for dc in range(NB):
                                e.matmul(ps_[:],
                                         wt[:, dc * D + oc * 128: dc * D + (oc + 1) * 128],
                                         xfm[:, dc * T + tt * 512: dc * T + (tt + 1) * 512],
                                         start=(dc == 0), stop=False)
                            e.matmul(ps_[:],
                                     aT[0:RANK, pi * D + oc * 128: pi * D + (oc + 1) * 128],
                                     zz[0:RANK, pi * T + tt * 512: pi * T + (tt + 1) * 512],
                                     start=False, stop=False)
                            return e.matmul(ps_[:],
                                            brow[0:1, fboff + oc * 128: fboff + (oc + 1) * 128],
                                            ones[0:1, 0:512],
                                            start=False, stop=True)

                        par = oc % 2
                        warv = last_store[par] if tt == 0 else None

                        def ev(e, ps_, fin, oc=oc, tt=tt, par=par, warv=warv, fdst=fdst):
                            if warv is not None:
                                e.wait_ge(mk.sem[f"ch{par}"], warv)
                            act = e.activation(out=rows[:, par * D + tt * 512: par * D + (tt + 1) * 512],
                                               in_=ps_[:], func=AF.Copy)
                            fin(act)
                            if tt == 3:
                                e.drain()
                                e.dma_start(
                                    out=fdst[oc * 128:(oc + 1) * 128, :],
                                    in_=rows[:, par * D:(par + 1) * D]).then_inc(
                                        mk.sem[f"ch{par}"], 16)
                        groups.append((mm, ev))
                        if tt == 3:
                            mk.semn[f"ch{par}"] += 16
                            last_store[par] = mk.val(f"ch{par}")
            else:
                dst = v0t
                boff = D
                for tc in range(NT):
                    for ot in range(4):
                        def mm(e, ps_, tc=tc, ot=ot, pi=pi, boff=boff):
                            for dc in range(NB):
                                e.matmul(ps_[:],
                                         xfm[:, dc * T + tc * 128: dc * T + (tc + 1) * 128],
                                         wt[:, dc * D + ot * 512: dc * D + (ot + 1) * 512],
                                         start=(dc == 0), stop=False)
                            e.matmul(ps_[:],
                                     zz[0:RANK, pi * T + tc * 128: pi * T + (tc + 1) * 128],
                                     aT[0:RANK, pi * D + ot * 512: pi * D + (ot + 1) * 512],
                                     start=False, stop=False)
                            return e.matmul(ps_[:],
                                            ones[0:1, 0:128],
                                            brow[0:1, boff + ot * 512: boff + (ot + 1) * 512],
                                            start=False, stop=True)

                        par = tc % 2
                        warv = last_store[par] if ot == 0 else None

                        def ev(e, ps_, fin, tc=tc, ot=ot, p=p, dst=dst, par=par, warv=warv):
                            if warv is not None:
                                e.wait_ge(mk.sem[f"ch{par}"], warv)
                            act = e.activation(out=rows[:, par * D + ot * 512: par * D + (ot + 1) * 512],
                                               in_=ps_[:], func=AF.Copy)
                            fin(act)
                            if ot == 3:
                                e.drain()
                                e.dma_start(
                                    out=dst[tc * 128:(tc + 1) * 128, :],
                                    in_=rows[:, par * D:(par + 1) * D]).then_inc(
                                        mk.sem[f"ch{par}"], 16)
                        groups.append((mm, ev))
                        if ot == 3:
                            mk.semn[f"ch{par}"] += 16
                            last_store[par] = mk.val(f"ch{par}")

            def pre(e, w_done=w_done):
                e.wait_ge(mk.sem["qG"], w_done)
                e.wait_ge(mk.sem["qS"], sync_done)
            run_groups(mk, groups, p_mm, pre=pre)
            if p == "v" and getattr(mk, "fused_mode", False):
                k0a = mk.dram("k0a", [2 * D, T], BF16)
                v0a = mk.dram("v0a", [SEQ, D], BF16)
                if not hasattr(mk, "exch_mark"):
                    mk.exch_mark = {}
                st0 = mk.val("ch0")
                st1 = mk.val("ch1")

                def gcoll(e):
                    e.wait_ge(mk.sem["ch0"], st0)
                    e.wait_ge(mk.sem["ch1"], st1)
                    mk.inc(e.collective_compute(
                        "AllGather", ALU.bypass,
                        replica_groups=[[0, 1], [2, 3], [4, 5], [6, 7]],
                        ins=[k0f.opt()], outs=[k0a.opt()]), "cc", 1)
                    mk.inc(e.collective_compute(
                        "AllGather", ALU.bypass,
                        replica_groups=[[0, 1], [2, 3], [4, 5], [6, 7]],
                        ins=[v0t.opt()], outs=[v0a.opt()]), "cc", 1)
                    mk.exch_mark["k0a"] = mk.exch_mark["v0a"] = mk.val("cc")
                blk.gpsimd(gcoll)

    mk.barrier()



def ph_attn0(mk):
    nc = mk.nc
    q0f = mk.dram("q0f", [D, T], BF16)
    k0a = mk.dram("k0a", [2 * D, T], BF16)
    v0a = mk.dram("v0a", [SEQ, D], BF16)
    ao_f = mk.dram("ao_f", [D, T], BF16)
    mk.ein("c_ones_bf", [128, 512], BF16)
    mk.ein("c_eye_f32", [128, 128], F32)
    NKC = SEQ // 128
    NKB = SEQ // 512

    es = ExitStack()
    sb = lambda n_, sh, dt: es.enter_context(nc.sbuf_tensor(n_, sh, dt))
    pst = lambda n_, sh, dt: es.enter_context(nc.psum_tensor(n_, sh, dt))
    with es:
        qh = sb("a2_qh", [128, 3 * T], BF16)
        kh = sb("a2_kh", [128, 3 * SEQ], BF16)
        vh4 = [sb(f"a2_vh{i}", [128, NKC * 512], BF16) for i in range(2)]
        pt = sb("a2_pt", [128, 2 * NKC * 512], BF16)
        mtmp = sb("a2_mtmp", [128, NKB], F32)
        mneg1 = sb("a2_mneg1", [128, 2], F32)
        m1sb = sb("a2_m1", [1, 2], BF16)
        mbias = sb("a2_mb", [128, 2], F32)
        lrec_f = sb("a2_lrecf", [1, 2 * 512], F32)
        lrec_b = sb("a2_lrecb", [1, 2 * 512], BF16)
        lb_f = sb("a2_lbf", [128, 2 * 512], F32)
        ao_t = sb("a2_aot", [128, 2 * 512], BF16)
        ones = sb("a2_ones", [128, 512], BF16)
        eye_f = sb("a2_eyef", [128, 128], F32)
        p_s = [pst(f"a2_ps{i}", [128, 512], F32) for i in range(2)]
        p_st = [pst(f"a2_pst{i}", [128, 512], F32) for i in range(2)]
        p_sm = pst("a2_psm", [1, 512], F32)
        p_mi = pst("a2_pmi", [128, 512], F32)
        p_lb = pst("a2_plb", [128, 512], F32)
        p_o = pst("a2_po", [128, 512], F32)

        sc = Sched(mk)
        sc.add("sync", lambda e: e.dma_start(out=ones[:], in_=mk.t["c_ones_bf"]), "qS", 16)
        sc.add("sync", lambda e: e.dma_start(out=eye_f[:], in_=mk.t["c_eye_f32"]), "qS", 16)
        sc.wait("tensor", "qS")

        iters = [(h, qt) for h in range(H0) for qt in range(4)]
        ld_marks = {}
        st_marks = {}

        ccm = getattr(mk, "exch_mark", {}).get("v0a")

        vld = {}

        def emit_vload(g):
            buf = g % 2
            vch = f"ch{5 + buf}"
            if g == 0 and ccm is not None:
                sc.wait("sync", "cc", ccm)
            if g >= 2:
                sc.wait("sync", "ld1", 4 * g - 4)
            sc.add("sync", lambda e, g=g, buf=buf: e.dma_start(
                out=vh4[buf][:].rearrange("p (c f) -> p c f", c=NKC),
                in_=v0a.rearrange("(c p) d -> p c d", p=128)
                    [:, :, g * 512:(g + 1) * 512]), vch, 16)
            vld[g] = mk.val(vch)

        def emit_loads(h):
            par = h % 3
            lch = f"ch{2 + par}"
            if h % 4 == 0:
                emit_vload(h // 4)
            if h == 0 and ccm is not None:
                sc.wait("sync", "cc", ccm)
            if h >= 3:
                sc.wait("sync", "ld1", h - 2)
            sc.add("sync", lambda e, h=h, par=par: e.dma_start(
                out=qh[:, par * T:(par + 1) * T],
                in_=q0f[h * 128:(h + 1) * 128, :]), lch, 16)
            sc.add("sync", lambda e, h=h, par=par: e.dma_start(
                out=kh[:, par * SEQ: par * SEQ + T],
                in_=k0a[h * 128:(h + 1) * 128, :]), lch, 16)
            sc.add("sync", lambda e, h=h, par=par: e.dma_start(
                out=kh[:, par * SEQ + T:(par + 1) * SEQ],
                in_=k0a[D + h * 128: D + (h + 1) * 128, :]), lch, 16)
            ld_marks[h] = mk.val(lch)

        def stage_A(i):
            h, qt = iters[i]
            par, ip = h % 3, i % 2
            if qt == 0:
                sc.wait("tensor", f"ch{2 + par}", ld_marks[h])
            dvb = mk.val("dv0")
            for blk in range(NKB):
                if blk >= 2:
                    sc.wait("tensor", "dv0", dvb + blk - 1)
                sc.add("tensor", lambda e, h=h, par=par, qt=qt, blk=blk: e.matmul(
                    p_s[blk % 2][:],
                    qh[:, par * T + qt * 512: par * T + (qt + 1) * 512].rearrange(
                        "p (a b) -> p (a b)") if False else
                    qh[:, par * T + qt * 512: par * T + qt * 512 + 128],
                    kh[:, par * SEQ + blk * 512: par * SEQ + (blk + 1) * 512],
                    start=True, stop=True), "pe0", 1)
                sc.wait("vector", "pe0")
                sc.add("vector", lambda e, blk=blk: e.reduce_max(
                    out=mtmp[:, blk:blk + 1], in_=p_s[blk % 2][:],
                    axis=AX.X, negate=True), "dv0", 1)
            sc.on("vector", lambda e: e.drain())
            sc.add("vector", lambda e, ip=ip: e.tensor_reduce(
                out=mneg1[:, ip:ip + 1], in_=mtmp[:], axis=AX.X, op=ALU.min), None)
            sc.add("vector", lambda e: e.drain(), "dv1", 1)
            return mk.val("dv1")

        def stage_C(i, bmark):
            h, qt = iters[i]
            ip = i % 2
            sc.wait("tensor", "dv1", bmark)
            sc.wait("tensor", "sc1")
            sc.add("tensor", lambda e, ip=ip: e.transpose(
                p_mi[0:1, 0:128], mneg1[:, ip:ip + 1], eye_f[:]), "pe1", 1)
            sc.wait("vector", "pe1")
            sc.add("vector", lambda e, ip=ip: e.tensor_reduce(
                out=m1sb[0:1, ip:ip + 1], in_=p_mi[0:1, 0:128], axis=AX.X,
                op=ALU.min), None)
            sc.add("vector", lambda e: e.drain(), "dv1", 1)
            sc.wait("tensor", "dv1")
            sc.add("tensor", lambda e, ip=ip: e.matmul(
                p_mi[0:128, 256:257], ones[0:1, 0:128], m1sb[0:1, ip:ip + 1],
                start=True, stop=True), "pe1", 1)
            sc.wait("scalar", "pe1")
            sc.add("scalar", lambda e, ip=ip: e.activation(
                out=mbias[:, ip:ip + 1], in_=p_mi[0:128, 256:257], func=AF.Copy,
                scale=ATT_SCALE), None)
            sc.on("scalar", lambda e: e.drain())
            sc.add("scalar", lambda e: e.nop(), "sc1", 1)
            return mk.val("sc1")

        prev = None

        def emit_slot(i, cmark):
            nonlocal prev
            h, qt = iters[i]
            par, ip = h % 3, i % 2
            sc.wait("tensor", "sc1", cmark)
            ev_base = mk.val("ev0")
            if prev is not None:
                hp, qtp = iters[prev["i"]]
                parp, ipp = hp % 3, prev["i"] % 2
                sc.wait("tensor", "ev0", prev["evdone"])
                sc.wait("tensor", "dv0", prev["podone"])
            for k in range(NKC):
                if k >= 2:
                    sc.wait("tensor", "ev0", ev_base + k - 1)
                sc.add("tensor", lambda e, h=h, par=par, qt=qt, k=k: e.matmul(
                    p_st[k % 2][:],
                    kh[:, par * SEQ + k * 128: par * SEQ + (k + 1) * 128],
                    qh[:, par * T + qt * 512: par * T + (qt + 1) * 512],
                    start=True, stop=True), "pe0", 1)
                sc.wait("scalar", "pe0")
                sc.add("scalar", lambda e, k=k, ip=ip: e.activation(
                    out=pt[:, ip * NKC * 512 + k * 512: ip * NKC * 512 + (k + 1) * 512],
                    in_=p_st[k % 2][:], func=AF.Exp, scale=ATT_SCALE,
                    bias=mbias[:, ip:ip + 1]), "ev0", 1)
                if prev is not None:
                    if k < 16:
                        for j in (2 * k, 2 * k + 1):
                            sc.add("tensor", lambda e, j=j, ipp=ipp: e.matmul(
                                p_sm[0:1, 0:512], ones[0:128, 0:1],
                                pt[:, ipp * NKC * 512 + j * 512: ipp * NKC * 512 + (j + 1) * 512],
                                start=(j == 0), stop=(j == NKC - 1)),
                                "pe1" if j == NKC - 1 else None, 1)
                    else:
                        if k == 16:
                            sc.wait("vector", "pe1")
                            sc.add("vector", lambda e, ipp=ipp: e.reciprocal(
                                out=lrec_f[0:1, ipp * 512:(ipp + 1) * 512],
                                in_=p_sm[0:1, 0:512]), None)
                            sc.on("vector", lambda e: e.drain())
                            sc.add("vector", lambda e, ipp=ipp: e.tensor_copy(
                                out=lrec_b[0:1, ipp * 512:(ipp + 1) * 512],
                                in_=lrec_f[0:1, ipp * 512:(ipp + 1) * 512]), None)
                            sc.add("vector", lambda e: e.drain(), "dv1", 1)
                        for j in (2 * (k - 16), 2 * (k - 16) + 1):
                            gb = (hp // 4) % 2
                            if j == 0:
                                sc.wait("tensor", f"ch{5 + gb}", vld[hp // 4])
                            sc.add("tensor", lambda e, j=j, hp=hp, gb=gb, ipp=ipp: e.matmul(
                                p_o[:],
                                vh4[gb][:].rearrange("p (c f) -> p c f", c=NKC)
                                    [:, j, (hp % 4) * 128:(hp % 4) * 128 + 128],
                                pt[:, ipp * NKC * 512 + j * 512: ipp * NKC * 512 + (j + 1) * 512],
                                start=(j == 0), stop=(j == NKC - 1)),
                                "pe1" if j == NKC - 1 else None, 1)
            if prev is not None:
                sc.wait("tensor", "dv1")
                sc.add("tensor", lambda e, ipp=ipp: e.matmul(
                    p_lb[:], ones[0:1, 0:128],
                    lrec_b[0:1, ipp * 512:(ipp + 1) * 512],
                    start=True, stop=True), "pe2", 1)
                sc.wait("scalar", "pe2")
                sc.add("scalar", lambda e, ipp=ipp: e.activation(
                    out=lb_f[:, ipp * 512:(ipp + 1) * 512],
                    in_=p_lb[:], func=AF.Copy), "ev1", 1)
                io = prev["i"]
                opar = io % 2
                sc.wait("vector", "pe1")
                sc.wait("vector", "ev1")
                if io >= 2:
                    sc.wait("vector", f"st{opar}", st_marks[io - 2])
                sc.add("vector", lambda e, opar=opar: e.tensor_tensor(
                    out=ao_t[:, opar * 512:(opar + 1) * 512], in0=p_o[:],
                    in1=lb_f[:, opar * 512:(opar + 1) * 512], op=ALU.mult), None)
                if qtp == 3:
                    sc.add("vector", lambda e: e.drain(), "ld1", 1)
                else:
                    sc.on("vector", lambda e: e.drain())
                sc.add("vector", lambda e: e.engine_nop(), "dv0", 1)
                podone = mk.val("dv0")
                sc.wait("gpsimd", "dv0", podone)
                sc.add("gpsimd", lambda e, hp=hp, qtp=qtp, opar=opar: e.dma_start(
                    out=ao_f[hp * 128:(hp + 1) * 128, qtp * 512:(qtp + 1) * 512],
                    in_=ao_t[:, opar * 512:(opar + 1) * 512]), f"st{opar}", 16)
                st_marks[io] = mk.val(f"st{opar}")
            else:
                podone = 0
            prev_new = {"i": i, "evdone": mk.val("ev0"), "podone": podone}
            prev = prev_new

        emit_loads(0)
        emit_loads(1)
        emit_loads(2)
        bm = stage_A(0)
        cm = stage_C(0, bm)
        for i in range(len(iters)):
            emit_slot(i, cm)
            if i + 1 < len(iters):
                hn, qtn = iters[i + 1]
                if qtn == 0:
                    if hn + 2 < H0 and hn >= 1:
                        emit_loads(hn + 2)
                bm = stage_A(i + 1)
                cm = stage_C(i + 1, bm)
        i = len(iters)
        hp, qtp = iters[-1]
        ipp = (len(iters) - 1) % 2
        parp = hp % 3
        sc.wait("tensor", "ev0")
        sc.wait("tensor", "dv0")
        for j in range(NKC):
            sc.add("tensor", lambda e, j=j, ipp=ipp: e.matmul(
                p_sm[0:1, 0:512], ones[0:128, 0:1],
                pt[:, ipp * NKC * 512 + j * 512: ipp * NKC * 512 + (j + 1) * 512],
                start=(j == 0), stop=(j == NKC - 1)),
                "pe1" if j == NKC - 1 else None, 1)
        sc.wait("vector", "pe1")
        sc.add("vector", lambda e, ipp=ipp: e.reciprocal(
            out=lrec_f[0:1, ipp * 512:(ipp + 1) * 512], in_=p_sm[0:1, 0:512]), None)
        sc.on("vector", lambda e: e.drain())
        sc.add("vector", lambda e, ipp=ipp: e.tensor_copy(
            out=lrec_b[0:1, ipp * 512:(ipp + 1) * 512],
            in_=lrec_f[0:1, ipp * 512:(ipp + 1) * 512]), None)
        sc.add("vector", lambda e: e.drain(), "dv1", 1)
        sc.wait("tensor", "dv1")
        sc.add("tensor", lambda e, ipp=ipp: e.matmul(
            p_lb[:], ones[0:1, 0:128], lrec_b[0:1, ipp * 512:(ipp + 1) * 512],
            start=True, stop=True), "pe2", 1)
        sc.wait("scalar", "pe2")
        sc.add("scalar", lambda e, ipp=ipp: e.activation(
            out=lb_f[:, ipp * 512:(ipp + 1) * 512], in_=p_lb[:], func=AF.Copy),
            "ev1", 1)
        for j in range(NKC):
            gb = (hp // 4) % 2
            sc.add("tensor", lambda e, j=j, hp=hp, gb=gb, ipp=ipp: e.matmul(
                p_o[:],
                vh4[gb][:].rearrange("p (c f) -> p c f", c=NKC)
                    [:, j, (hp % 4) * 128:(hp % 4) * 128 + 128],
                pt[:, ipp * NKC * 512 + j * 512: ipp * NKC * 512 + (j + 1) * 512],
                start=(j == 0), stop=(j == NKC - 1)),
                "pe1" if j == NKC - 1 else None, 1)
        io = len(iters) - 1
        opar = io % 2
        sc.wait("vector", "pe1")
        sc.wait("vector", "ev1")
        sc.wait("vector", f"st{opar}", st_marks[io - 2])
        sc.add("vector", lambda e, opar=opar: e.tensor_tensor(
            out=ao_t[:, opar * 512:(opar + 1) * 512], in0=p_o[:],
            in1=lb_f[:, opar * 512:(opar + 1) * 512], op=ALU.mult), None)
        sc.add("vector", lambda e: e.drain(), "dv0", 1)
        sc.wait("gpsimd", "dv0")
        sc.add("gpsimd", lambda e, hp=hp, qtp=qtp, opar=opar: e.dma_start(
            out=ao_f[hp * 128:(hp + 1) * 128, qtp * 512:(qtp + 1) * 512],
            in_=ao_t[:, opar * 512:(opar + 1) * 512]), f"st{opar}", 16)
        sc.emit()
    mk.barrier()


def ln_row(sc, mk, row_ap, out_ap, sq, st1, st2, st3, n_free):
    inv_n = 1.0 / n_free
    sc.add("scalar", lambda e: e.activation(out=sq[:], in_=row_ap, func=AF.Square,
                                            accum_out=st1[:]), "sc0", 1)
    sc.wait("vector", "sc0")
    sc.add("vector", lambda e: e.reduce_sum(out=st2[:], in_=row_ap, axis=AX.X,
                                            negate=True), None)
    sc.on("vector", lambda e: e.drain())
    sc.on("vector", lambda e: e.tensor_scalar_mul(out=st2[:], in0=st2[:], scalar1=inv_n))
    sc.on("vector", lambda e: e.drain())
    sc.on("vector", lambda e: e.tensor_mul(out=st3[:], in0=st2[:], in1=st2[:]))
    sc.on("vector", lambda e: e.drain())
    sc.on("vector", lambda e: e.tensor_scalar(out=st1[:], in0=st1[:], scalar1=inv_n,
                                              scalar2=None, op0=ALU.mult))
    sc.on("vector", lambda e: e.drain())
    sc.on("vector", lambda e: e.tensor_sub(out=st1[:], in0=st1[:], in1=st3[:]))
    sc.on("vector", lambda e: e.drain())
    sc.on("vector", lambda e: e.tensor_scalar_add(out=st1[:], in0=st1[:], scalar1=EPS))
    sc.add("vector", lambda e: e.drain(), "dv1", 1)
    sc.wait("scalar", "dv1")
    sc.add("scalar", lambda e: e.activation(out=st3[:], in_=st1[:], func=AF.Sqrt), None)
    sc.on("scalar", lambda e: e.drain())
    sc.add("scalar", lambda e: e.nop(), "sc0", 1)
    sc.wait("vector", "sc0")
    sc.on("vector", lambda e: e.reciprocal(out=st3[:], in_=st3[:]))
    sc.on("vector", lambda e: e.drain())
    sc.add("vector", lambda e: e.tensor_scalar(out=out_ap, in0=row_ap, scalar1=st2[:],
                                               scalar2=st3[:], op0=ALU.add,
                                               op1=ALU.mult), None)
    sc.add("vector", lambda e: e.drain(), "dv0", 1)



def ph_wo0ln(mk):
    mk.ein("woT", [D, D], F32)
    mk.ein("BoT", [D, RANK], F32)
    mk.ein("AoT", [RANK, D], F32)
    mk.ein("bo_row", [1, D], F32)
    mk.ein("x_tm", [T, D], F32)
    mk.ein("c_ones_bf", [128, 512], BF16)
    mk.ein("c_eye_f32", [128, 128], F32)
    emit_oproj_ln(mk, "a3", mk.dram("ao_f", [D, T], BF16), "woT",
                  ("BoT", "AoT", "bo_row"), mk.t["x_tm"],
                  mk.dram("x1t", [T, D], F32), mk.dram("x1f", [D, T], F32))


def emit_oproj_ln(mk, tag, src_fm, w_name, lora, res_tm, xt, xf):
    nc = mk.nc
    x1t, x1f = xt, xf

    es = ExitStack()
    sb = lambda n_, sh, dt: es.enter_context(nc.sbuf_tensor(n_, sh, dt))
    pst = lambda n_, sh, dt: es.enter_context(nc.psum_tensor(n_, sh, dt))
    with es:
        afm = sb(f"{tag}_afm", [128, NB * T], BF16)
        wt = sb(f"{tag}_wt", [128, NB * D], BF16)
        ones = sb(f"{tag}_ones", [128, 512], BF16)
        eye_f = sb(f"{tag}_eyef", [128, 128], F32)
        xres = sb(f"{tag}_xres", [128, 2 * D], F32)
        prerow = sb(f"{tag}_pre", [128, 2 * D], F32)
        lnrow = sb(f"{tag}_ln", [128, 2 * D], F32)
        sq = sb(f"{tag}_sq", [128, D], F32)
        st1 = sb(f"{tag}_st1", [128, 1], F32)
        st2 = sb(f"{tag}_st2", [128, 1], F32)
        st3 = sb(f"{tag}_st3", [128, 1], F32)
        colb = sb(f"{tag}_colb", [128, D], F32)
        p_mm = [pst(f"{tag}_pmm{i}", [128, 512], F32) for i in range(3)]
        p_tr = [pst(f"{tag}_ptr{i}", [128, 128], F32) for i in range(2)]
        if lora is not None:
            aT = sb(f"{tag}_aT", [RANK, D], BF16)
            bT = sb(f"{tag}_bT", [128, NB * RANK], BF16)
            zz = sb(f"{tag}_z", [RANK, T], BF16)
            brow = sb(f"{tag}_brow", [1, D], BF16)
            p_z = [pst(f"{tag}_pz{i}", [RANK, 512], F32) for i in range(2)]

        sc = Sched(mk)
        for dc in range(NB):
            sc.add("sync", lambda e, dc=dc: e.dma_start(
                out=afm[:, dc * T:(dc + 1) * T],
                in_=src_fm[dc * 128:(dc + 1) * 128, :]), "qS", 16)
        sc.add("sync", lambda e: e.dma_start(out=ones[:], in_=mk.t["c_ones_bf"]), "qS", 16)
        sc.add("sync", lambda e: e.dma_start(out=eye_f[:], in_=mk.t["c_eye_f32"]), "qS", 16)
        for dc in range(NB):
            sc.add("gpsimd", lambda e, dc=dc: e.dma_start(
                out=wt[:, dc * D:(dc + 1) * D],
                in_=mk.t[w_name][dc * 128:(dc + 1) * 128, :]), "qG", 16)
        if lora is not None:
            bT_name, aT_name, brow_name = lora
            sc.add("gpsimd", lambda e: e.dma_start(out=aT[0:RANK, :], in_=mk.t[aT_name]), "qG", 16)
            sc.add("gpsimd", lambda e: e.dma_start(
                out=bT[:].rearrange("p (dc r) -> p dc r", dc=NB),
                in_=mk.t[bT_name].rearrange("(dc p) r -> p dc r", p=128)), "qG", 16)
            sc.add("gpsimd", lambda e: e.dma_start(out=brow[0:1, :], in_=mk.t[brow_name]), "qG", 16)
        lds = mk.val("qS")
        ldg = mk.val("qG")
        sc.wait("tensor", "qS", lds)
        sc.wait("tensor", "qG", ldg)

        if lora is not None:
            ev_base = mk.val("ev0")
            for tt in range(4):
                if tt >= 2:
                    sc.wait("tensor", "ev0", ev_base + tt - 1)

                def mmz(e, tt=tt):
                    for dc in range(NB):
                        last = e.matmul(p_z[tt % 2][:],
                                        bT[:, dc * RANK:(dc + 1) * RANK],
                                        afm[:, dc * T + tt * 512: dc * T + (tt + 1) * 512],
                                        start=(dc == 0), stop=(dc == NB - 1))
                    return last
                sc.add("tensor", mmz, "pe0", 1)
                sc.wait("scalar", "pe0")
                sc.add("scalar", lambda e, tt=tt: e.activation(
                    out=zz[0:RANK, tt * 512:(tt + 1) * 512], in_=p_z[tt % 2][:],
                    func=AF.Copy, scale=LORA_SCALE), "ev0", 1)

        xld = {}
        st_x1t = {}
        st_x1f = {}
        ttmark = {}

        def mmw(e, tc, ot):
            last = None
            for dc in range(NB):
                last = e.matmul(p_mm[(tc * 4 + ot) % 3][:],
                                afm[:, dc * T + tc * 128: dc * T + (tc + 1) * 128],
                                wt[:, dc * D + ot * 512: dc * D + (ot + 1) * 512],
                                start=(dc == 0),
                                stop=(lora is None and dc == NB - 1))
            if lora is not None:
                e.matmul(p_mm[(tc * 4 + ot) % 3][:],
                         zz[0:RANK, tc * 128:(tc + 1) * 128],
                         aT[0:RANK, ot * 512:(ot + 1) * 512],
                         start=False, stop=False)
                last = e.matmul(p_mm[(tc * 4 + ot) % 3][:], ones[0:1, 0:128],
                                brow[0:1, ot * 512:(ot + 1) * 512],
                                start=False, stop=True)
            return last

        tadd_mark = {}
        row_mark = {}
        for tc in range(NT):
            par = tc % 2
            lch = f"ch{2 + par}"
            if tc >= 2:
                sc.wait("sync", "dv0", row_mark[tc - 2])
            sc.add("sync", lambda e, tc=tc, par=par: e.dma_start(
                out=xres[:, par * D:(par + 1) * D],
                in_=res_tm[tc * 128:(tc + 1) * 128, :]), lch, 16)
            xld[tc] = mk.val(lch)
            for ot in range(4):
                i = tc * 4 + ot
                if i >= 3:
                    sc.wait("tensor", "dv0", tadd_mark[i - 3])
                sc.add("tensor", lambda e, tc=tc, ot=ot: mmw(e, tc, ot), "pe0", 1)
                sc.wait("vector", "pe0")
                sc.wait("vector", lch, xld[tc])
                sc.add("vector", lambda e, tc=tc, ot=ot, par=par, i=i: e.tensor_add(
                    out=prerow[:, par * D + ot * 512: par * D + (ot + 1) * 512],
                    in0=p_mm[i % 3][:],
                    in1=xres[:, par * D + ot * 512: par * D + (ot + 1) * 512]), "dv0", 1)
                tadd_mark[i] = mk.val("dv0")
            row_mark[tc] = mk.val("dv0")
            sc.on("vector", lambda e: e.drain())
            sc.wait("scalar", "dv0")
            ln_row(sc, mk, prerow[:, par * D:(par + 1) * D],
                   lnrow[:, par * D:(par + 1) * D], sq, st1, st2, st3, D)
            och = f"ch{4 + par}"
            sc.wait("scalar", "dv0")
            sc.add("scalar", lambda e, tc=tc, par=par, och=och: e.dma_start(
                out=x1t[tc * 128:(tc + 1) * 128, :],
                in_=lnrow[:, par * D:(par + 1) * D]), och, 16)
            st_x1t[tc] = mk.val(och)
            if tc >= 1:
                sc.wait("scalar", "ch6", st_x1f[tc - 1])
            ev1_base = mk.val("ev1")
            for dc in range(NB):
                if ev1_base + dc - 1 > 0:
                    sc.wait("tensor", "ev1", ev1_base + dc - 1)
                sc.wait("tensor", "dv0")
                sc.add("tensor", lambda e, tc=tc, par=par, dc=dc: e.transpose(
                    p_tr[dc % 2][:],
                    lnrow[:, par * D + dc * 128: par * D + (dc + 1) * 128],
                    eye_f[:]), "pe1", 1)
                sc.wait("scalar", "pe1")
                sc.add("scalar", lambda e, tc=tc, par=par, dc=dc: e.activation(
                    out=colb[:, dc * 128:(dc + 1) * 128],
                    in_=p_tr[dc % 2][:], func=AF.Copy), "ev1", 1)
            sc.on("scalar", lambda e: e.drain())
            sc.add("scalar", lambda e, tc=tc: e.dma_start(
                out=x1f.rearrange("(fc p) t -> p fc t", p=128)[:, :, tc * 128:(tc + 1) * 128],
                in_=colb[:].rearrange("p (fc q) -> p fc q", fc=NB)),
                "ch6", 16)
            st_x1f[tc] = mk.val("ch6")
            if tc >= 1:
                sc.wait("vector", f"ch{4 + (tc - 1) % 2}", st_x1t[tc - 1])

        sc.emit()
    mk.barrier()



def ph_moeprep(mk):
    nc = mk.nc
    mk.ein("wgTe", [NEXP, D, FMOE], F32)
    mk.ein("wuTe", [NEXP, D, FMOE], F32)
    mk.ein("wdTe", [NEXP, FMOE, D], F32)
    mk.ein("shgT", [D, FSH], F32)
    mk.ein("shuT", [D, FSH], F32)
    mk.ein("shdT", [FSH, D], F32)
    w8g = mk.dram("w8g", [NEXP, D, FMOE], BF16)
    w8u = mk.dram("w8u", [NEXP, D, FMOE], BF16)
    w8d = mk.dram("w8d", [NEXP, FMOE, D], BF16)
    w8sg = mk.dram("w8sg", [D, FSH], BF16)
    w8su = mk.dram("w8su", [D, FSH], BF16)
    w8sd = mk.dram("w8sd", [FSH, D], BF16)
    sc = Sched(mk)
    if hasattr(mk, "exch_mark") and "v0a" in mk.exch_mark:
        sc.wait("gpsimd", "cc", mk.exch_mark["v0a"])
        sc.wait("gpsimd", "ch2", mk.val("ch2") + 48)
        sc.wait("gpsimd", "ch3", mk.val("ch3") + 48)
        sc.wait("gpsimd", "ch4", mk.val("ch4") + 48)
        sc.wait("gpsimd", "ch5", mk.val("ch5") + 16)
    jobs = []
    for src, dst in (("wgTe", w8g), ("wuTe", w8u), ("wdTe", w8d)):
        for n in range(NEXP):
            rows = dst.shape[1]
            for r0 in range(0, rows, 1024):
                r1 = min(rows, r0 + 1024)
                jobs.append((dst[n][r0:r1], mk.t[src][n][r0:r1]))
    for src, dst in (("shgT", w8sg), ("shuT", w8su), ("shdT", w8sd)):
        rows = dst.shape[0]
        for r0 in range(0, rows, 1024):
            r1 = min(rows, r0 + 1024)
            jobs.append((dst[r0:r1], mk.t[src][r0:r1]))
    base = mk.val("bg0")
    for i, (dst_ap, src_ap) in enumerate(jobs):
        if i >= 2:
            sc.wait("gpsimd", "bg0", base + 16 * (i - 1))
        sc.add("gpsimd", lambda e, dst_ap=dst_ap, src_ap=src_ap:
               e.dma_start(out=dst_ap, in_=src_ap), "bg0", 16)
    sc.emit()


def ph_moe(mk):
    nc = mk.nc
    x1f = mk.dram("x1f", [D, T], F32)
    x1t = mk.dram("x1t", [T, D], F32)
    mk.ein("gwT", [D, NEXP], F32)
    mk.ein("wgTe", [NEXP, D, FMOE], F32)
    mk.ein("wuTe", [NEXP, D, FMOE], F32)
    mk.ein("wdTe", [NEXP, FMOE, D], F32)
    mk.ein("shgT", [D, FSH], F32)
    mk.ein("shuT", [D, FSH], F32)
    mk.ein("shdT", [FSH, D], F32)
    mk.ein("c_ones_bf", [128, 512], BF16)
    mk.ein("c_eye_f32", [128, 128], F32)
    x2t = mk.dram("x2t", [T, D], F32)
    x2f = mk.dram("x2f", [D, T], F32)

    es0 = ExitStack()
    sb0 = lambda n_, sh, dt: es0.enter_context(nc.sbuf_tensor(n_, sh, dt))
    with es0:
        wmat = sb0("b_wmat", [128, NT * NEXP], F32)
        acc = sb0("b_acc", [128, 4 * D], F32)
        eye_f = sb0("b_eyef", [128, 128], F32)

        esR = ExitStack()
        sbR = lambda n_, sh, dt: esR.enter_context(nc.sbuf_tensor(n_, sh, dt))
        with esR:
            gw_sb = sbR("b_gw", [128, NB * NEXP], F32)
            xcol = sbR("b_xcol", [128, 2 * D], F32)
            sm = sbR("b_sm", [128, NEXP], F32)
            g4 = sbR("b_g4", [128, 4], F32)
            pr2 = sbR("b_pr2", [128, 2], F32)
            pr2b = sbR("b_pr2b", [128, 2], F32)
            s1 = sbR("b_s1", [128, 1], F32)
            s2 = sbR("b_s2", [128, 1], F32)
            s3 = sbR("b_s3", [128, 1], F32)
            msk = sbR("b_msk", [128, NEXP], F32)
            p_log = esR.enter_context(nc.psum_tensor("b_plog", [128, NEXP], F32))

            sc = Sched(mk)
            sc.add("sync", lambda e: e.dma_start(
                out=gw_sb[:].rearrange("p (dc x) -> p dc x", dc=NB),
                in_=mk.t["gwT"].rearrange("(dc p) x -> p dc x", p=128)), "qS", 16)
            sc.add("sync", lambda e: e.dma_start(out=eye_f[:], in_=mk.t["c_eye_f32"]),
                   "qS", 16)
            xc_mark = {}
            dvR = {}
            gw_mark = mk.val("qS")
            sc.wait("tensor", "qS", gw_mark)
            for tc in range(NT):
                par = tc % 2
                lch = f"ch{2 + par}"
                if tc >= 2:
                    sc.wait("sync", "pe0", dvR[tc - 2])
                sc.add("sync", lambda e, tc=tc, par=par: e.dma_start(
                    out=xcol[:, par * D:(par + 1) * D].rearrange("p (dc q) -> p dc q", dc=NB),
                    in_=x1f.rearrange("(dc p) t -> p dc t", p=128)[:, :, tc * 128:(tc + 1) * 128]),
                    lch, 16)
                xc_mark[tc] = mk.val(lch)
                sc.wait("tensor", lch, xc_mark[tc])
                if tc >= 1:
                    sc.wait("tensor", "dv1")

                def mml(e, tc=tc, par=par):
                    for dc in range(NB):
                        last = e.matmul(p_log[:],
                                        xcol[:, par * D + dc * 128: par * D + (dc + 1) * 128],
                                        gw_sb[:, dc * NEXP:(dc + 1) * NEXP],
                                        start=(dc == 0), stop=(dc == NB - 1))
                    return last
                sc.add("tensor", mml, "pe0", 1)
                dvR[tc] = mk.val("pe0")
                sc.wait("vector", "pe0")
                sc.add("vector", lambda e: e.reduce_max(out=s1[:], in_=p_log[:],
                                                        axis=AX.X, negate=True), None)
                sc.add("vector", lambda e: e.drain(), "dv0", 1)
                sc.wait("scalar", "dv0")
                sc.add("scalar", lambda e: e.activation(out=sm[:], in_=p_log[:],
                                                        func=AF.Exp, bias=s1[:],
                                                        accum_out=s2[:]), None)
                sc.on("scalar", lambda e: e.drain())
                sc.add("scalar", lambda e: e.nop(), "sc0", 1)
                sc.wait("vector", "sc0")
                sc.add("vector", lambda e: e.drain(), "dv1", 1)
                V = lambda f: (sc.on("vector", f), sc.on("vector", lambda e: e.drain()))
                V(lambda e: e.reciprocal(out=s3[:], in_=s2[:]))
                V(lambda e: e.tensor_scalar_mul(out=sm[:], in0=sm[:], scalar1=s3[:]))
                smv = sm[:].rearrange("p (g u) -> p g u", g=4)
                V(lambda e: e.tensor_tensor(out=g4[:], in0=smv[:, :, 0], in1=smv[:, :, 1],
                                            op=ALU.max))
                g4v = g4[:].rearrange("p (u v) -> p u v", u=2)
                V(lambda e: e.tensor_tensor(out=pr2[:], in0=g4v[:, :, 0], in1=g4v[:, :, 1],
                                            op=ALU.max))
                V(lambda e: e.tensor_tensor(out=pr2b[:], in0=g4v[:, :, 0], in1=g4v[:, :, 1],
                                            op=ALU.min))
                V(lambda e: e.tensor_tensor(out=s1[:], in0=pr2[:, 0:1], in1=pr2[:, 1:2],
                                            op=ALU.min))
                V(lambda e: e.tensor_tensor(out=s2[:], in0=pr2b[:, 0:1], in1=pr2b[:, 1:2],
                                            op=ALU.max))
                V(lambda e: e.tensor_tensor(out=s1[:], in0=s1[:], in1=s2[:], op=ALU.max))
                V(lambda e: e.tensor_single_scalar(out=g4[:], in_=g4[:], scalar=s1[:],
                                                   op=ALU.is_ge))
                V(lambda e: e.tensor_tensor(
                    out=smv, in0=smv,
                    in1=g4[:].unsqueeze(2).broadcast_to([128, 4, 2]), op=ALU.mult))
                V(lambda e: e.reduce_max(out=s1[:], in_=sm[:], axis=AX.X))
                V(lambda e: e.tensor_single_scalar(out=msk[:], in_=sm[:], scalar=s1[:],
                                                   op=ALU.is_lt))
                V(lambda e: e.tensor_tensor(out=msk[:], in0=sm[:], in1=msk[:], op=ALU.mult))
                V(lambda e: e.reduce_max(out=s2[:], in_=msk[:], axis=AX.X))
                V(lambda e: e.tensor_single_scalar(out=msk[:], in_=sm[:], scalar=s2[:],
                                                   op=ALU.is_ge))
                sc.add("vector", lambda e, tc=tc: e.tensor_tensor(
                    out=wmat[:, tc * NEXP:(tc + 1) * NEXP], in0=sm[:], in1=msk[:],
                    op=ALU.mult), None)
                sc.on("vector", lambda e: e.drain())
            sc.emit()
        mk.barrier()

        for tt in range(4):
            esE = ExitStack()
            sbE = lambda n_, sh, dt: esE.enter_context(nc.sbuf_tensor(n_, sh, dt))
            with esE:
                xtt = sbE(f"bE_x{tt}", [128, NB * 512], BF16)
                wbuf = [sbE(f"bE_w{tt}_{i}", [128, NB * FMOE], BF16) for i in range(2)]
                g_sb = sbE(f"bE_g{tt}", [128, 22 * 512], BF16)
                gated = sbE(f"bE_gt{tt}", [128, 22 * 512], BF16)
                u_t = sbE(f"bE_u{tt}", [128, 2 * 512], BF16)
                p_g = [esE.enter_context(nc.psum_tensor(f"bE_pg{tt}_{i}", [128, 512], F32))
                       for i in range(3)]
                p_d = [esE.enter_context(nc.psum_tensor(f"bE_pd{tt}_{i}", [128, 512], F32))
                       for i in range(3)]

                sc = Sched(mk)
                for dc in range(NB):
                    sc.add("gpsimd", lambda e, dc=dc, tt=tt: e.dma_start(
                        out=xtt[:, dc * 512:(dc + 1) * 512],
                        in_=x1f[dc * 128:(dc + 1) * 128, tt * 512:(tt + 1) * 512]),
                        "qG", 16)
                x_mark = mk.val("qG")
                sc.wait("tensor", "qG", x_mark)
                sc.wait("gpsimd", "bg0", mk.val("bg0"))

                buf_free = [0, 0]
                pg_last = [None, None, None]
                pd_last = [None, None, None]
                u_last = [None, None]
                gidx = [0]

                def load_stage(buf, src_slabs):
                    wch = f"wb{buf}"
                    sc.wait("gpsimd", "pe1", buf_free[buf])
                    for nch, chw, src in src_slabs:
                        sc.add("gpsimd", lambda e, nch=nch, chw=chw, src=src, buf=buf:
                               e.dma_start(
                                   out=wbuf[buf][:, 0:nch * chw]
                                       .rearrange("p (c w) -> p c w", c=nch),
                                   in_=src), wch, 16)
                    return (wch, mk.val(wch))

                def wg_slabs(w, n):
                    w8 = {"wgTe": "w8g", "wuTe": "w8u"}[w]
                    return [(NB, FMOE,
                             mk.t[w8][n].rearrange("(dc p) f -> p dc f", p=128))]

                def wd_slabs(n):
                    return [(NFC, D,
                             mk.t["w8d"][n].rearrange("(fc p) h -> p fc h", p=128))]

                def sh_slabs(w, half):
                    w8 = {"shgT": "w8sg", "shuT": "w8su"}[w]
                    return [(NB, FMOE,
                             mk.t[w8].rearrange("(dc p) f -> p dc f", p=128)
                                 [:, :, half * FMOE:(half + 1) * FMOE])]

                def shd_slabs(half):
                    return [(NFC, D,
                             mk.t["w8sd"][half * FMOE:(half + 1) * FMOE]
                                 .rearrange("(fc p) h -> p fc h", p=128))]

                def gate_stage(buf, mark, nfc, dst, func):
                    sc.wait("tensor", mark[0], mark[1])
                    for fc in range(nfc):
                        if pg_last[fc % 3] is not None:
                            sc.wait("tensor", *pg_last[fc % 3])

                        def mmg(e, buf=buf, fc=fc):
                            boff = (fc % NFC) * 128 if nfc == NFC else 0
                            for dc in range(NB):
                                last = e.matmul(
                                    p_g[fc % 3][:],
                                    wbuf[buf][:, dc * FMOE + (fc % NFC) * 128: dc * FMOE + (fc % NFC) * 128 + 128]
                                    if nfc == NFC else
                                    wbuf[buf][:, dc * FMOE + fc * 128: dc * FMOE + fc * 128 + 128],
                                    xtt[:, dc * 512:(dc + 1) * 512],
                                    start=(dc == 0), stop=(dc == NB - 1))
                            return last
                        sc.add("tensor", mmg, "pe1", 1)
                        sc.wait("scalar", "pe1")
                        sc.add("scalar", lambda e, fc=fc, dst=dst, func=func: e.activation(
                            out=dst[:, fc * 512:(fc + 1) * 512], in_=p_g[fc % 3][:],
                            func=func), "ev0", 1)
                        pg_last[fc % 3] = ("ev0", mk.val("ev0"))
                    buf_free[buf] = mk.val("pe1")
                    return mk.val("ev0")

                def up_fuse_stage(buf, mark, nfc, goff):
                    sc.wait("tensor", mark[0], mark[1])
                    for fc in range(nfc):
                        if pg_last[fc % 3] is not None:
                            sc.wait("tensor", *pg_last[fc % 3])
                        if u_last[fc % 2] is not None:
                            sc.wait("tensor", *u_last[fc % 2])

                        def mmu(e, buf=buf, fc=fc):
                            for dc in range(NB):
                                last = e.matmul(
                                    p_g[fc % 3][:],
                                    wbuf[buf][:, dc * FMOE + fc * 128: dc * FMOE + fc * 128 + 128],
                                    xtt[:, dc * 512:(dc + 1) * 512],
                                    start=(dc == 0), stop=(dc == NB - 1))
                            return last
                        sc.add("tensor", mmu, "pe1", 1)
                        sc.wait("scalar", "pe1")
                        sc.add("scalar", lambda e, fc=fc: e.activation(
                            out=u_t[:, (fc % 2) * 512:((fc % 2) + 1) * 512],
                            in_=p_g[fc % 3][:], func=AF.Copy), "ev0", 1)
                        sc.wait("vector", "ev0")
                        sc.add("vector", lambda e, fc=fc, goff=goff: e.tensor_tensor(
                            out=gated[:, (goff + fc) * 512:(goff + fc + 1) * 512],
                            in0=g_sb[:, (goff + fc) * 512:(goff + fc + 1) * 512],
                            in1=u_t[:, (fc % 2) * 512:((fc % 2) + 1) * 512],
                            op=ALU.mult), "dv0", 1)
                        pg_last[fc % 3] = ("ev0", mk.val("ev0"))
                        u_last[fc % 2] = ("dv0", mk.val("dv0"))
                    buf_free[buf] = mk.val("pe1")
                    return mk.val("dv0")

                def down_stage(bufs, mark, nfc, goff, n, first):
                    sc.wait("tensor", mark[0], mark[1])
                    sc.wait("tensor", "dv0")
                    i = 0
                    for tc4 in range(4):
                        tcg = tt * 4 + tc4
                        for ht in range(4):
                            if pd_last[i % 3] is not None:
                                sc.wait("tensor", *pd_last[i % 3])

                            def mmd(e, tc4=tc4, ht=ht, i=i, bufs=bufs, nfc=nfc, goff=goff):
                                for fc in range(nfc):
                                    buf = bufs[fc // NFC]
                                    last = e.matmul(
                                        p_d[i % 3][:],
                                        gated[:, (goff + fc) * 512 + tc4 * 128:
                                              (goff + fc) * 512 + tc4 * 128 + 128],
                                        wbuf[buf][:, (fc % NFC) * D + ht * 512:
                                                  (fc % NFC) * D + ht * 512 + 512],
                                        start=(fc == 0), stop=(fc == nfc - 1))
                                return last
                            sc.add("tensor", mmd, "pe1", 1)
                            sc.wait("vector", "pe1")
                            if n is None:
                                sc.add("vector", lambda e, tc4=tc4, ht=ht, i=i: e.scalar_tensor_tensor(
                                    out=acc[:, tc4 * D + ht * 512: tc4 * D + (ht + 1) * 512],
                                    in0=p_d[i % 3][:], scalar=1.0,
                                    in1=acc[:, tc4 * D + ht * 512: tc4 * D + (ht + 1) * 512],
                                    op0=ALU.mult, op1=ALU.add), "dv1", 1)
                            elif first:
                                sc.add("vector", lambda e, tc4=tc4, ht=ht, i=i, tcg=tcg, n=n:
                                       e.tensor_scalar_mul(
                                           out=acc[:, tc4 * D + ht * 512: tc4 * D + (ht + 1) * 512],
                                           in0=p_d[i % 3][:],
                                           scalar1=wmat[:, tcg * NEXP + n: tcg * NEXP + n + 1]),
                                       "dv1", 1)
                            else:
                                sc.add("vector", lambda e, tc4=tc4, ht=ht, i=i, tcg=tcg, n=n:
                                       e.scalar_tensor_tensor(
                                           out=acc[:, tc4 * D + ht * 512: tc4 * D + (ht + 1) * 512],
                                           in0=p_d[i % 3][:],
                                           scalar=wmat[:, tcg * NEXP + n: tcg * NEXP + n + 1],
                                           in1=acc[:, tc4 * D + ht * 512: tc4 * D + (ht + 1) * 512],
                                           op0=ALU.mult, op1=ALU.add), "dv1", 1)
                            pd_last[i % 3] = ("dv1", mk.val("dv1"))
                            i += 1
                    for b in bufs:
                        buf_free[b] = mk.val("pe1")

                for n in range(NEXP):
                    mkg = load_stage(n % 2, wg_slabs("wgTe", n))
                    gate_stage(n % 2, mkg, NFC, g_sb, AF.Silu)
                    mku = load_stage((n + 1) % 2, wg_slabs("wuTe", n))
                    up_fuse_stage((n + 1) % 2, mku, NFC, 0)
                    mkd = load_stage(n % 2, wd_slabs(n))
                    down_stage([n % 2], mkd, NFC, 0, n, first=(n == 0))
                m0 = load_stage(0, sh_slabs("shgT", 0))
                m1 = load_stage(1, sh_slabs("shgT", 1))
                gate_stage(0, m0, NFC, g_sb, AF.Silu)
                sc.wait("tensor", m1[0], m1[1])
                for fc in range(NFC):
                    if pg_last[fc % 3] is not None:
                        sc.wait("tensor", *pg_last[fc % 3])

                    def mmg2(e, fc=fc):
                        for dc in range(NB):
                            last = e.matmul(
                                p_g[fc % 3][:],
                                wbuf[1][:, dc * FMOE + fc * 128: dc * FMOE + fc * 128 + 128],
                                xtt[:, dc * 512:(dc + 1) * 512],
                                start=(dc == 0), stop=(dc == NB - 1))
                        return last
                    sc.add("tensor", mmg2, "pe1", 1)
                    sc.wait("scalar", "pe1")
                    sc.add("scalar", lambda e, fc=fc: e.activation(
                        out=g_sb[:, (NFC + fc) * 512:(NFC + fc + 1) * 512],
                        in_=p_g[fc % 3][:], func=AF.Silu), "ev0", 1)
                    pg_last[fc % 3] = ("ev0", mk.val("ev0"))
                buf_free[1] = mk.val("pe1")
                m0 = load_stage(0, sh_slabs("shuT", 0))
                m1 = load_stage(1, sh_slabs("shuT", 1))
                up_fuse_stage(0, m0, NFC, 0)
                sc.wait("tensor", m1[0], m1[1])
                for fc in range(NFC):
                    if pg_last[fc % 3] is not None:
                        sc.wait("tensor", *pg_last[fc % 3])
                    if u_last[fc % 2] is not None:
                        sc.wait("tensor", *u_last[fc % 2])

                    def mmu2(e, fc=fc):
                        for dc in range(NB):
                            last = e.matmul(
                                p_g[fc % 3][:],
                                wbuf[1][:, dc * FMOE + fc * 128: dc * FMOE + fc * 128 + 128],
                                xtt[:, dc * 512:(dc + 1) * 512],
                                start=(dc == 0), stop=(dc == NB - 1))
                        return last
                    sc.add("tensor", mmu2, "pe1", 1)
                    sc.wait("scalar", "pe1")
                    sc.add("scalar", lambda e, fc=fc: e.activation(
                        out=u_t[:, (fc % 2) * 512:((fc % 2) + 1) * 512],
                        in_=p_g[fc % 3][:], func=AF.Copy), "ev0", 1)
                    sc.wait("vector", "ev0")
                    sc.add("vector", lambda e, fc=fc: e.tensor_tensor(
                        out=gated[:, (NFC + fc) * 512:(NFC + fc + 1) * 512],
                        in0=g_sb[:, (NFC + fc) * 512:(NFC + fc + 1) * 512],
                        in1=u_t[:, (fc % 2) * 512:((fc % 2) + 1) * 512],
                        op=ALU.mult), "dv0", 1)
                    pg_last[fc % 3] = ("ev0", mk.val("ev0"))
                    u_last[fc % 2] = ("dv0", mk.val("dv0"))
                buf_free[1] = mk.val("pe1")
                m0 = load_stage(0, shd_slabs(0))
                m1 = load_stage(1, shd_slabs(1))
                sc.wait("tensor", m1[0], m1[1])
                down_stage([0, 1], m0, 22, 0, None, first=False)
                sc.emit()
            mk.barrier()

            esL = ExitStack()
            sbL = lambda n_, sh, dt: esL.enter_context(nc.sbuf_tensor(n_, sh, dt))
            with esL:
                xres = sbL(f"bL_xres{tt}", [128, 2 * D], F32)
                lnrow = sbL(f"bL_ln{tt}", [128, 2 * D], F32)
                sq = sbL(f"bL_sq{tt}", [128, D], F32)
                s1 = sbL(f"bL_s1{tt}", [128, 1], F32)
                s2 = sbL(f"bL_s2{tt}", [128, 1], F32)
                s3 = sbL(f"bL_s3{tt}", [128, 1], F32)
                colb = sbL(f"bL_colb{tt}", [128, D], F32)
                p_tr = [esL.enter_context(nc.psum_tensor(f"bL_ptr{tt}_{i}", [128, 128], F32))
                        for i in range(2)]
                sc = Sched(mk)
                stf = {}
                stt_m = {}
                ev1_m = {}
                ln_mark = {}
                for tc4 in range(4):
                    tcg = tt * 4 + tc4
                    par = tc4 % 2
                    lch = f"ch{2 + par}"
                    if tc4 >= 2:
                        sc.wait("sync", "dv0", ln_mark[tc4 - 2])
                    sc.add("sync", lambda e, tcg=tcg, par=par: e.dma_start(
                        out=xres[:, par * D:(par + 1) * D],
                        in_=x1t[tcg * 128:(tcg + 1) * 128, :]), lch, 16)
                    sc.wait("vector", lch)
                    if tc4 >= 2:
                        sc.wait("vector", "ch6", stf[tc4 - 2])
                        sc.wait("vector", f"ch{4 + par}", stt_m[tc4 - 2])
                        sc.wait("vector", "ev1", ev1_m[tc4 - 2])
                    sc.add("vector", lambda e, tc4=tc4, par=par: e.tensor_add(
                        out=xres[:, par * D:(par + 1) * D],
                        in0=acc[:, tc4 * D:(tc4 + 1) * D],
                        in1=xres[:, par * D:(par + 1) * D]), "dv0", 1)
                    sc.on("vector", lambda e: e.drain())
                    sc.wait("scalar", "dv0")
                    ln_row(sc, mk, xres[:, par * D:(par + 1) * D],
                           lnrow[:, par * D:(par + 1) * D], sq, s1, s2, s3, D)
                    ln_mark[tc4] = mk.val("dv0")
                    sc.wait("scalar", "dv0")
                    sc.add("scalar", lambda e, tcg=tcg, par=par: e.dma_start(
                        out=x2t[tcg * 128:(tcg + 1) * 128, :],
                        in_=lnrow[:, par * D:(par + 1) * D]), f"ch{4 + par}", 16)
                    stt_m[tc4] = mk.val(f"ch{4 + par}")
                    if tc4 >= 1:
                        sc.wait("scalar", "ch6", stf[tc4 - 1])
                    ev1b = mk.val("ev1")
                    for dc in range(NB):
                        if ev1b + dc - 1 > 0:
                            sc.wait("tensor", "ev1", ev1b + dc - 1)
                        sc.wait("tensor", "dv0")
                        sc.add("tensor", lambda e, par=par, dc=dc: e.transpose(
                            p_tr[dc % 2][:],
                            lnrow[:, par * D + dc * 128: par * D + (dc + 1) * 128],
                            eye_f[:]), "pe0", 1)
                        sc.wait("scalar", "pe0")
                        sc.add("scalar", lambda e, dc=dc: e.activation(
                            out=colb[:, dc * 128:(dc + 1) * 128],
                            in_=p_tr[dc % 2][:], func=AF.Copy), "ev1", 1)
                    sc.on("scalar", lambda e: e.drain())
                    sc.add("scalar", lambda e, tcg=tcg: e.dma_start(
                        out=x2f.rearrange("(fc p) t -> p fc t", p=128)[:, :, tcg * 128:(tcg + 1) * 128],
                        in_=colb[:].rearrange("p (fc q) -> p fc q", fc=NB)), "ch6", 16)
                    stf[tc4] = mk.val("ch6")
                    ev1_m[tc4] = mk.val("ev1")
                sc.emit()
            mk.barrier()



def ph_qkv1(mk):
    nc = mk.nc
    x2f = mk.dram("x2f", [D, T], F32)
    mk.ein("wqkvT", [D, 3 * D], F32)
    for t_ in ("c_cq", "c_sq", "c_ck", "c_sk"):
        mk.ein(t_, [T, 32], F32)
    q1t = mk.dram("q1t", [T, D], BF16)
    k1t = mk.dram("k1t", [T, D], BF16)
    v1t = mk.dram("v1t", [T, D], BF16)
    SEC = 1536

    es = ExitStack()
    sb = lambda n_, sh, dt: es.enter_context(nc.sbuf_tensor(n_, sh, dt))
    with es:
        xfm = sb("c1_xfm", [128, NB * T], BF16)
        wsec = [sb(f"c1_w{i}", [128, NB * SEC], BF16) for i in range(2)]
        rows = sb("c1_rows", [128, 2 * SEC], BF16)
        rot = {t_: sb(f"c1_{t_}", [128, NT * 32], F32)
               for t_ in ("c_cq", "c_sq", "c_ck", "c_sk")}
        qa = sb("c1_qa", [128, 256], BF16)
        rh = sb("c1_rh", [128, 256], BF16)
        p_mm = [es.enter_context(nc.psum_tensor(f"c1_pmm{i}", [128, 512], F32))
                for i in range(3)]

        sc = Sched(mk)
        for dc in range(NB):
            sc.add("gpsimd", lambda e, dc=dc: e.dma_start(
                out=xfm[:, dc * T:(dc + 1) * T],
                in_=x2f[dc * 128:(dc + 1) * 128, :]), "qG", 16)
        x_mark = mk.val("qG")
        for t_ in ("c_cq", "c_sq", "c_ck", "c_sk"):
            sc.add("sync", lambda e, t_=t_: e.dma_start(
                out=rot[t_][:].rearrange("p (tc r) -> p tc r", tc=NT),
                in_=mk.t[t_].rearrange("(tc p) r -> p tc r", p=128)), "qS", 16)
        rot_mark = mk.val("qS")
        sc.wait("tensor", "qG", x_mark)
        sc.wait("vector", "qS", rot_mark)

        w_free = [0, 0]
        st_mark = {0: None, 1: None}
        dvm = {}
        evm = {}
        for sec in range(4):
            par = sec % 2
            wch = f"wb{par}"
            sc.wait("gpsimd", "pe0", w_free[par])
            for dc in range(NB):
                sc.add("gpsimd", lambda e, sec=sec, dc=dc, par=par: e.dma_start(
                    out=wsec[par][:, dc * SEC:(dc + 1) * SEC],
                    in_=mk.t["wqkvT"][dc * 128:(dc + 1) * 128, sec * SEC:(sec + 1) * SEC]),
                    wch, 16)
            sc.wait("tensor", wch)
            for tc in range(NT):
                rpar = tc % 2
                evb = mk.val("ev0")
                for ot in range(3):
                    i = sec * 48 + tc * 3 + ot
                    if i >= 3:
                        sc.wait("tensor", "ev0", evm[i - 3])
                    sc.add("tensor", lambda e, sec=sec, tc=tc, ot=ot, par=par, i=i: [
                        e.matmul(p_mm[i % 3][:],
                                 xfm[:, dc_ * T + tc * 128: dc_ * T + (tc + 1) * 128],
                                 wsec[par][:, dc_ * SEC + ot * 512: dc_ * SEC + (ot + 1) * 512],
                                 start=(dc_ == 0), stop=(dc_ == NB - 1))
                        for dc_ in range(NB)][-1], "pe0", 1)
                    sc.wait("scalar", "pe0")
                    if ot == 0 and st_mark[rpar] is not None:
                        sc.wait("scalar", f"ch{4 + rpar}", st_mark[rpar])
                    sc.add("scalar", lambda e, ot=ot, rpar=rpar, i=i: e.activation(
                        out=rows[:, rpar * SEC + ot * 512: rpar * SEC + (ot + 1) * 512],
                        in_=p_mm[i % 3][:], func=AF.Copy), "ev0", 1)
                    evm[i] = mk.val("ev0")
                sc.wait("vector", "ev0")
                rv = rows[:, rpar * SEC:(rpar + 1) * SEC].rearrange(
                    "p (h three hd) -> p h three hd", h=8, three=3)
                qav = qa[:].rearrange("p (h r) -> p h r", h=8)
                rhv = rh[:].rearrange("p (h r) -> p h r", h=8)
                for part, ct, st_ in ((0, "c_cq", "c_sq"), (1, "c_ck", "c_sk")):
                    cb = rot[ct][:, tc * 32:(tc + 1) * 32].unsqueeze(1).broadcast_to([128, 8, 32])
                    sb_ = rot[st_][:, tc * 32:(tc + 1) * 32].unsqueeze(1).broadcast_to([128, 8, 32])
                    src = rv[:, :, part, 0:32]
                    sc.on("vector", lambda e, src=src, cb=cb: e.tensor_tensor(
                        out=qav, in0=src, in1=cb, op=ALU.mult))
                    sc.on("vector", lambda e, src=src: e.tensor_scalar_mul(
                        out=rhv[:, :, 0:32:2], in0=src[:, :, 1:32:2], scalar1=-1.0))
                    sc.on("vector", lambda e, src=src: e.tensor_copy(
                        out=rhv[:, :, 1:32:2], in_=src[:, :, 0:32:2]))
                    sc.on("vector", lambda e: e.drain())
                    sc.on("vector", lambda e, sb_=sb_: e.tensor_tensor(
                        out=rhv, in0=rhv, in1=sb_, op=ALU.mult))
                    sc.on("vector", lambda e: e.drain())
                    sc.on("vector", lambda e, src=src: e.tensor_tensor(
                        out=src, in0=qav, in1=rhv, op=ALU.add))
                    sc.on("vector", lambda e: e.drain())
                sc.add("vector", lambda e: e.engine_nop(), "dv0", 1)
                dvm[(sec, tc)] = mk.val("dv0")
                sc.wait("scalar", "dv0", dvm[(sec, tc)])
                for three, dst in ((0, q1t), (1, k1t), (2, v1t)):
                    sc.add("scalar", lambda e, sec=sec, tc=tc, rpar=rpar, three=three, dst=dst:
                           e.dma_start(
                               out=dst[tc * 128:(tc + 1) * 128, sec * 512:(sec + 1) * 512],
                               in_=rows[:, rpar * SEC:(rpar + 1) * SEC].rearrange(
                                   "p (h three hd) -> p h three hd", h=8, three=3)[:, :, three, :]),
                           f"ch{4 + rpar}", 16)
                st_mark[rpar] = mk.val(f"ch{4 + rpar}")
            w_free[par] = mk.val("pe0")
        sc.emit()
    mk.barrier()



def ph_ret(mk):
    nc = mk.nc
    q1t = mk.dram("q1t", [T, D], BF16)
    k1t = mk.dram("k1t", [T, D], BF16)
    v1t = mk.dram("v1t", [T, D], BF16)
    k1a = mk.dram("k1a", [SEQ, D], BF16)
    v1a = mk.dram("v1a", [SEQ, D], BF16)
    for t_ in ("c_d0", "c_qdec", "c_kdec", "c_kdec_prev"):
        mk.ein(t_, [128, H1 * 128] if t_ == "c_d0" else [128, D], F32)
    mk.ein("c_sdec", [64, H1 * HD1], F32)
    mk.ein("c_halfpick", [128, 2], F32)
    mk.ein("c_eye_bf", [128, 128], BF16)
    ret_bf = mk.dram("ret_bf", [D, T], BF16)
    st4 = mk.dram("st4", [2, 32], F32)

    es = ExitStack()
    sb = lambda n_, sh, dt: es.enter_context(nc.sbuf_tensor(n_, sh, dt))
    with es:
        krow = sb("c3_k", [128, 2 * D], BF16)
        vrow = sb("c3_v", [128, 2 * D], BF16)
        qrow = sb("c3_q", [128, 2 * D], BF16)
        k1s = sb("c3_ks", [128, D], BF16)
        q1s = sb("c3_qs", [128, D], BF16)
        kfm = sb("c3_kfm", [64, H1 * 128], BF16)
        qfm = sb("c3_qfm", [64, H1 * 128], BF16)
        pt = [sb(f"c3_pt{i}", [128, 128], BF16) for i in range(2)]
        state = sb("c3_S", [64, H1 * HD1], F32)
        state_bf = sb("c3_Sb", [64, H1 * HD1], BF16)
        retc = sb("c3_retc", [128, 2 * NB * 128], BF16)
        cd0 = sb("c3_cd0", [128, H1 * 128], F32)
        cqd = sb("c3_cqd", [128, D], F32)
        ckd = sb("c3_ckd", [128, D], F32)
        ckp = sb("c3_ckp", [128, D], F32)
        csd = sb("c3_csd", [64, H1 * HD1], F32)
        hp = sb("c3_hp", [128, 2], F32)
        eye = sb("c3_eye", [128, 128], BF16)
        sums = sb("c3_sums", [128, 2 * NB], F32)
        sqs = sb("c3_sq", [128, D], F32)
        slab = sb("c3_slab", [128, D], BF16)
        stats_sb = sb("c3_stats", [2, 32], F32)
        p_tr = es.enter_context(nc.psum_tensor("c3_ptr", [64, 16 * 128], BF16))
        p_s = [es.enter_context(nc.psum_tensor(f"c3_ps{i}", [128, 128], F32))
               for i in range(2)]
        p_big = es.enter_context(nc.psum_tensor("c3_pbig", [64, 2048], F32))

        sc = Sched(mk)
        for dst, src in ((cd0, "c_d0"), (cqd, "c_qdec"), (ckd, "c_kdec"),
                         (ckp, "c_kdec_prev"), (csd, "c_sdec"), (hp, "c_halfpick"),
                         (eye, "c_eye_bf")):
            sc.add("sync", lambda e, dst=dst, src=src: e.dma_start(
                out=dst[0:dst.shape[0], :], in_=mk.t[src]), "qS", 16)
        const_mark = mk.val("qS")
        sc.wait("vector", "qS", const_mark)
        sc.on("vector", lambda e: e.memset(state[:], 0.0))
        sc.on("vector", lambda e: e.memset(state_bf[:], 0.0))
        sc.add("vector", lambda e: e.drain(), "dv0", 1)
        sc.wait("tensor", "qS", const_mark)

        ccm = getattr(mk, "exch_mark", {}).get("v1a")
        if ccm is not None:
            sc.wait("sync", "cc", ccm)
        ld_m = {0: None, 1: None}
        up_m = {}
        for c in range(NT):
            par = c % 2
            lch = f"ch{2 + par}"
            if c >= 2:
                sc.wait("sync", "dv0", up_m[c - 2])
            sc.add("sync", lambda e, c=c, par=par: e.dma_start(
                out=krow[:, par * D:(par + 1) * D], in_=k1a[c * 128:(c + 1) * 128, :]),
                lch, 16)
            sc.add("sync", lambda e, c=c, par=par: e.dma_start(
                out=vrow[:, par * D:(par + 1) * D], in_=v1a[c * 128:(c + 1) * 128, :]),
                lch, 16)
            ld = mk.val(lch)
            sc.wait("vector", lch, ld)
            sc.add("vector", lambda e, par=par: e.tensor_tensor(
                out=k1s[:], in0=krow[:, par * D:(par + 1) * D], in1=ckp[:],
                op=ALU.mult), None)
            sc.add("vector", lambda e: e.drain(), "dv1", 1)
            sc.wait("tensor", "dv1")
            sc.wait("tensor", "dv0")
            for h in range(H1):
                sc.add("tensor", lambda e, h=h, par=par: e.matmul(
                    p_big[0:64, h * HD1:(h + 1) * HD1],
                    k1s[:, h * HD1:(h + 1) * HD1],
                    vrow[:, par * D + h * HD1: par * D + (h + 1) * HD1],
                    start=True, stop=True), "pe0" if h == H1 - 1 else None, 1)
            sc.wait("vector", "pe0")
            sc.on("vector", lambda e: e.tensor_tensor(out=state[:], in0=state[:],
                                                      in1=csd[:], op=ALU.mult))
            sc.on("vector", lambda e: e.drain())
            sc.add("vector", lambda e: e.tensor_add(out=state[:], in0=state[:],
                                                    in1=p_big[0:64, :]), None)
            sc.on("vector", lambda e: e.drain())
            sc.add("vector", lambda e: e.tensor_copy(out=state_bf[:], in_=state[:]), None)
            sc.add("vector", lambda e: e.drain(), "dv0", 1)
            up_m[c] = mk.val("dv0")

        st_m = {0: None, 1: None}
        for c in range(NT):
            par = c % 2
            lch = f"ch{2 + par}"
            sc.wait("sync", "dv0", up_m[NT + c - 2])
            for buf, src in ((qrow, q1t), (krow, k1t), (vrow, v1t)):
                sc.add("sync", lambda e, c=c, par=par, buf=buf, src=src: e.dma_start(
                    out=buf[:, par * D:(par + 1) * D], in_=src[c * 128:(c + 1) * 128, :]),
                    lch, 16)
            ld = mk.val(lch)
            sc.wait("vector", lch, ld)
            sc.add("vector", lambda e, par=par: e.tensor_tensor(
                out=k1s[:], in0=krow[:, par * D:(par + 1) * D], in1=ckd[:],
                op=ALU.mult), None)
            sc.add("vector", lambda e, par=par: e.tensor_tensor(
                out=q1s[:], in0=qrow[:, par * D:(par + 1) * D], in1=cqd[:],
                op=ALU.mult), None)
            sc.add("vector", lambda e: e.drain(), "dv1", 1)
            sc.wait("tensor", "dv1")
            sc.wait("tensor", "sc1")
            for gi, (src_sb, dst_sb) in enumerate(
                    ((q1s, qfm), (q1s, qfm), (krow, kfm), (krow, kfm))):
                half = gi % 2
                for hh in range(16):
                    h = half * 16 + hh
                    off = par * D if src_sb is krow else 0
                    sc.add("tensor", lambda e, h=h, hh=hh, src_sb=src_sb, off=off: e.transpose(
                        p_tr[0:64, hh * 128:(hh + 1) * 128],
                        src_sb[:, off + h * HD1: off + (h + 1) * HD1],
                        eye[:]), "pe1" if hh == 15 else None, 1)
                sc.wait("scalar", "pe1")
                sc.add("scalar", lambda e, half=half, dst_sb=dst_sb: e.activation(
                    out=dst_sb[0:64, half * 2048:(half + 1) * 2048],
                    in_=p_tr[0:64, :], func=AF.Copy), "sc1", 1)
                sc.wait("tensor", "sc1")
            tr_done = mk.val("sc1")
            sc.wait("tensor", "sc1", tr_done)
            if st_m[par] is not None:
                sc.wait("scalar", f"ch{4 + par}", st_m[par])
            dvb = mk.val("dv1")
            for grp in range(2):
                sc.wait("tensor", "ev1")
                sc.wait("tensor", "dv0")
                for hh in range(16):
                    h = grp * 16 + hh
                    if hh >= 2:
                        sc.wait("tensor", "dv1", dvb + grp * 16 + hh - 1)
                    sc.add("tensor", lambda e, h=h: e.matmul(
                        p_s[h % 2][:],
                        kfm[0:64, h * 128:(h + 1) * 128],
                        qfm[0:64, h * 128:(h + 1) * 128],
                        start=True, stop=True), "pe1", 1)
                    sc.wait("vector", "pe1")
                    sc.add("vector", lambda e, h=h: e.tensor_tensor(
                        out=pt[h % 2][:], in0=p_s[h % 2][:],
                        in1=cd0[:, h * 128:(h + 1) * 128], op=ALU.mult), "dv1", 1)
                    sc.wait("tensor", "dv1")
                    sc.add("tensor", lambda e, h=h, hh=hh, par=par: [
                        e.matmul(p_big[0:64, hh * 128:(hh + 1) * 128],
                                 vrow[:, par * D + h * HD1: par * D + (h + 1) * HD1],
                                 pt[h % 2][:], start=True, stop=False),
                        e.matmul(p_big[0:64, hh * 128:(hh + 1) * 128],
                                 state_bf[0:64, h * HD1:(h + 1) * HD1],
                                 qfm[0:64, h * 128:(h + 1) * 128],
                                 start=False, stop=True)][-1], "pe1", 1)
                sc.wait("scalar", "pe1")
                rv = retc[:, par * NB * 128:(par + 1) * NB * 128].rearrange(
                    "p (fc q) -> p fc q", fc=NB)
                pv = p_big[0:64, :].rearrange("p (hh q) -> p hh q", hh=16)
                sc.add("scalar", lambda e, grp=grp, rv=rv, pv=pv: e.activation(
                    out=rv[0:64, grp * 8:(grp + 1) * 8, :], in_=pv[:, 0::2, :],
                    func=AF.Copy), None)
                sc.add("scalar", lambda e, grp=grp, rv=rv, pv=pv: e.activation(
                    out=rv[64:128, grp * 8:(grp + 1) * 8, :], in_=pv[:, 1::2, :],
                    func=AF.Copy), "ev1", 1)
            sc.on("scalar", lambda e: e.drain())
            sc.add("scalar", lambda e, c=c, par=par: e.dma_start(
                out=ret_bf.rearrange("(fc p) t -> p fc t", p=128)[:, :, c * 128:(c + 1) * 128],
                in_=retc[:, par * NB * 128:(par + 1) * NB * 128]
                    .rearrange("p (fc q) -> p fc q", fc=NB)), f"ch{4 + par}", 16)
            st_m[par] = mk.val(f"ch{4 + par}")
            sc.wait("tensor", "dv1")
            sc.wait("tensor", "dv0")
            sc.wait("tensor", "ev1")
            for h in range(H1):
                sc.add("tensor", lambda e, h=h, par=par: e.matmul(
                    p_big[0:64, h * HD1:(h + 1) * HD1],
                    k1s[:, h * HD1:(h + 1) * HD1],
                    vrow[:, par * D + h * HD1: par * D + (h + 1) * HD1],
                    start=True, stop=True), "pe0" if h == H1 - 1 else None, 1)
            sc.wait("vector", "pe0")
            sc.on("vector", lambda e: e.tensor_tensor(out=state[:], in0=state[:],
                                                      in1=csd[:], op=ALU.mult))
            sc.on("vector", lambda e: e.drain())
            sc.add("vector", lambda e: e.tensor_add(out=state[:], in0=state[:],
                                                    in1=p_big[0:64, :]), None)
            sc.on("vector", lambda e: e.drain())
            sc.add("vector", lambda e: e.tensor_copy(out=state_bf[:], in_=state[:]), None)
            sc.add("vector", lambda e: e.drain(), "dv0", 1)
            up_m[NT + c] = mk.val("dv0")

        sc.wait("sync", "ch4")
        sc.wait("sync", "ch5")
        for fc in range(NB):
            lch = f"ch{2 + fc % 2}"
            if fc >= 1:
                sc.wait("sync", "sc0")
            sc.add("sync", lambda e, fc=fc: e.dma_start(
                out=slab[:], in_=ret_bf[fc * 128:(fc + 1) * 128, :]), lch, 16)
            sc.wait("vector", lch)
            sc.add("vector", lambda e, fc=fc: e.reduce_sum(
                out=sums[:, fc:fc + 1], in_=slab[:], axis=AX.X), None)
            sc.add("vector", lambda e: e.drain(), "dv1", 1)
            sc.wait("scalar", "dv1")
            sc.add("scalar", lambda e, fc=fc: e.activation(
                out=sqs[:], in_=slab[:], func=AF.Square,
                accum_out=sums[:, NB + fc:NB + fc + 1]), None)
            sc.on("scalar", lambda e: e.drain())
            sc.add("scalar", lambda e: e.nop(), "sc0", 1)
        sc.wait("tensor", "sc0")
        sc.wait("tensor", "dv1")
        sc.add("tensor", lambda e: e.matmul(
            p_s[0][0:2, 0:16], hp[:, 0:2], sums[:, 0:NB], start=True, stop=True), None)
        sc.add("tensor", lambda e: e.matmul(
            p_s[0][0:2, 16:32], hp[:, 0:2], sums[:, NB:2 * NB], start=True, stop=True),
            "pe1", 1)
        sc.wait("scalar", "pe1")
        sc.add("scalar", lambda e: e.activation(out=stats_sb[0:2, :], in_=p_s[0][0:2, 0:32],
                                                func=AF.Copy), None)
        sc.on("scalar", lambda e: e.drain())
        sc.add("scalar", lambda e: e.dma_start(out=st4, in_=stats_sb[0:2, :]), "qA", 16)
        sc.emit()
    mk.barrier()



def ph_gw1(mk):
    nc = mk.nc
    ret_bf = mk.dram("ret_bf", [D, T], BF16)
    st4a = mk.dram("st4a", [4, 32], F32)
    x2f = mk.dram("x2f", [D, T], F32)
    x2t = mk.dram("x2t", [T, D], F32)
    mk.ein("gnw2", [2, 16], F32)
    mk.ein("gnb2", [2, 16], F32)
    mk.ein("wg1T", [D, D], F32)
    mk.ein("wo1T", [D, D], F32)
    mk.ein("c_hpT", [2, 128], F32)
    mk.ein("c_ones_bf", [128, 512], BF16)
    mk.ein("c_eye_f32", [128, 128], F32)
    gated_b = mk.dram("gated_b", [D, T], BF16)
    x3t = mk.dram("x3t", [T, D], F32)
    x3f = mk.dram("x3f", [D, T], F32)
    NTOK = float(SEQ * HD1)

    es = ExitStack()
    sb = lambda n_, sh, dt: es.enter_context(nc.sbuf_tensor(n_, sh, dt))
    with es:
        xfm = sb("c5_xfm", [128, NB * T], BF16)
        wgt = sb("c5_wgt", [128, NB * D], BF16)
        retslab = sb("c5_ret", [128, 2 * T], BF16)
        grow = sb("c5_g", [128, 2 * T], F32)
        t1 = sb("c5_t1", [128, T], F32)
        t2 = sb("c5_t2", [128, T], F32)
        spb = sb("c5_sp", [128, T], F32)
        yrow = sb("c5_y", [128, T], F32)
        gb = sb("c5_gb", [128, 2 * T], BF16)
        slo = sb("c5_slo", [2, 32], F32)
        shi = sb("c5_shi", [2, 32], F32)
        a2 = sb("c5_a2", [2, 16], F32)
        b2 = sb("c5_b2", [2, 16], F32)
        gw2 = sb("c5_gw2", [2, 16], F32)
        gb2 = sb("c5_gb2", [2, 16], F32)
        hpT = sb("c5_hpT", [2, 128], F32)
        acol = sb("c5_acol", [128, 16], F32)
        bcol = sb("c5_bcol", [128, 16], F32)
        p_g = [es.enter_context(nc.psum_tensor(f"c5_pg{i}", [128, 512], F32))
               for i in range(3)]
        p_ab = es.enter_context(nc.psum_tensor("c5_pab", [128, 32], F32))

        sc = Sched(mk)
        for dc in range(NB):
            sc.add("gpsimd", lambda e, dc=dc: e.dma_start(
                out=xfm[:, dc * T:(dc + 1) * T],
                in_=x2f[dc * 128:(dc + 1) * 128, :]), "qG", 16)
            sc.add("gpsimd", lambda e, dc=dc: e.dma_start(
                out=wgt[:, dc * D:(dc + 1) * D],
                in_=mk.t["wg1T"][dc * 128:(dc + 1) * 128, :]), "qG", 16)
        ldg = mk.val("qG")
        ccm = getattr(mk, "exch_mark", {}).get("st4a")
        if ccm is not None:
            sc.wait("sync", "cc", ccm)
        sc.add("sync", lambda e: e.dma_start(out=slo[0:2, :], in_=st4a[0:2, :]), "qS", 16)
        sc.add("sync", lambda e: e.dma_start(out=shi[0:2, :], in_=st4a[2:4, :]), "qS", 16)
        sc.add("sync", lambda e: e.dma_start(out=gw2[0:2, :], in_=mk.t["gnw2"]), "qS", 16)
        sc.add("sync", lambda e: e.dma_start(out=gb2[0:2, :], in_=mk.t["gnb2"]), "qS", 16)
        sc.add("sync", lambda e: e.dma_start(out=hpT[0:2, :], in_=mk.t["c_hpT"]), "qS", 16)
        lds = mk.val("qS")

        sc.wait("vector", "qS", lds)
        V = lambda f: (sc.on("vector", f), sc.on("vector", lambda e: e.drain()))
        V(lambda e: e.tensor_add(out=slo[0:2, :], in0=slo[0:2, :], in1=shi[0:2, :]))
        V(lambda e: e.tensor_scalar_mul(out=slo[0:2, :], in0=slo[0:2, :],
                                        scalar1=1.0 / NTOK))
        V(lambda e: e.tensor_mul(out=a2[0:2, :], in0=slo[0:2, 0:16], in1=slo[0:2, 0:16]))
        V(lambda e: e.tensor_sub(out=b2[0:2, :], in0=slo[0:2, 16:32], in1=a2[0:2, :]))
        V(lambda e: e.tensor_scalar_add(out=b2[0:2, :], in0=b2[0:2, :], scalar1=EPS))
        sc.add("vector", lambda e: e.engine_nop(), "dv1", 1)
        sc.wait("scalar", "dv1")
        sc.add("scalar", lambda e: e.activation(out=b2[0:2, :], in_=b2[0:2, :],
                                                func=AF.Sqrt), None)
        sc.on("scalar", lambda e: e.drain())
        sc.add("scalar", lambda e: e.nop(), "sc0", 1)
        sc.wait("vector", "sc0")
        V(lambda e: e.reciprocal(out=b2[0:2, :], in_=b2[0:2, :]))
        V(lambda e: e.tensor_mul(out=a2[0:2, :], in0=gw2[0:2, :], in1=b2[0:2, :]))
        V(lambda e: e.tensor_mul(out=b2[0:2, :], in0=slo[0:2, 0:16], in1=a2[0:2, :]))
        V(lambda e: e.tensor_sub(out=b2[0:2, :], in0=gb2[0:2, :], in1=b2[0:2, :]))
        sc.add("vector", lambda e: e.engine_nop(), "dv1", 1)
        sc.wait("tensor", "dv1")
        sc.add("tensor", lambda e: e.matmul(p_ab[:, 0:16], hpT[0:2, :], a2[0:2, :],
                                            start=True, stop=True), None)
        sc.add("tensor", lambda e: e.matmul(p_ab[:, 16:32], hpT[0:2, :], b2[0:2, :],
                                            start=True, stop=True), "pe1", 1)
        sc.wait("scalar", "pe1")
        sc.add("scalar", lambda e: e.activation(out=acol[:], in_=p_ab[:, 0:16],
                                                func=AF.Copy), None)
        sc.add("scalar", lambda e: e.activation(out=bcol[:], in_=p_ab[:, 16:32],
                                                func=AF.Copy), None)
        sc.on("scalar", lambda e: e.drain())
        sc.add("scalar", lambda e: e.nop(), "sc1", 1)

        sc.wait("tensor", "qG", ldg)
        gdone = {}
        for oc in range(NB):
            par = oc % 2
            lch = f"ch{2 + par}"
            if oc >= 2:
                sc.wait("sync", "dv0", gdone[oc - 2])
            sc.add("sync", lambda e, oc=oc, par=par: e.dma_start(
                out=retslab[:, par * T:(par + 1) * T],
                in_=ret_bf[oc * 128:(oc + 1) * 128, :]), lch, 16)
            rld = mk.val(lch)
            evb = mk.val("ev0")
            for tt in range(4):
                i = oc * 4 + tt
                if i >= 3:
                    sc.wait("tensor", "ev0", evb + tt - 3 if tt >= 3 else None) \
                        if False else sc.wait("tensor", "ev0",
                                              mk.val("ev0") - 2 if mk.val("ev0") >= 2 else 0)
                sc.add("tensor", lambda e, oc=oc, tt=tt, i=i: [
                    e.matmul(p_g[i % 3][:],
                             wgt[:, dc_ * D + oc * 128: dc_ * D + (oc + 1) * 128],
                             xfm[:, dc_ * T + tt * 512: dc_ * T + (tt + 1) * 512],
                             start=(dc_ == 0), stop=(dc_ == NB - 1))
                    for dc_ in range(NB)][-1], "pe0", 1)
                sc.wait("scalar", "pe0")
                if tt == 0 and oc >= 2:
                    sc.wait("scalar", "dv0", gdone[oc - 2])
                sc.add("scalar", lambda e, tt=tt, par=par, i=i: e.activation(
                    out=grow[:, par * T + tt * 512: par * T + (tt + 1) * 512],
                    in_=p_g[i % 3][:], func=AF.Copy), "ev0", 1)
            gslice = lambda par=par: grow[:, par * T:(par + 1) * T]
            sc.on("scalar", lambda e: e.drain())
            if oc >= 1:
                sc.wait("scalar", "dv0", gdone.get(oc - 1, 0))
            sc.add("scalar", lambda e, par=par: e.activation(
                out=t1[:], in_=grow[:, par * T:(par + 1) * T], func=AF.Abs), None)
            sc.on("scalar", lambda e: e.drain())
            sc.add("scalar", lambda e: e.activation(out=t1[:], in_=t1[:], func=AF.Exp,
                                                    scale=-1.0), None)
            sc.on("scalar", lambda e: e.drain())
            sc.add("scalar", lambda e: e.activation(out=t1[:], in_=t1[:], func=AF.Ln,
                                                    bias=1.0), None)
            sc.add("scalar", lambda e, par=par: e.activation(
                out=t2[:], in_=grow[:, par * T:(par + 1) * T], func=AF.Relu), None)
            sc.on("scalar", lambda e: e.drain())
            sc.add("scalar", lambda e: e.nop(), "sc0", 1)
            sc.wait("vector", "sc0")
            sc.add("vector", lambda e: e.tensor_add(out=spb[:], in0=t1[:], in1=t2[:]),
                   None)
            sc.add("vector", lambda e: e.drain(), "dv1", 1)
            sc.wait("scalar", "dv1")
            sc.add("scalar", lambda e: e.activation(out=spb[:], in_=spb[:],
                                                    func=AF.Tanh), None)
            sc.on("scalar", lambda e: e.drain())
            sc.add("scalar", lambda e: e.nop(), "sc0", 1)
            sc.wait("vector", "sc0")
            sc.wait("vector", "sc1")
            sc.wait("vector", lch, rld)
            sc.add("vector", lambda e, oc=oc, par=par: e.tensor_scalar(
                out=yrow[:], in0=retslab[:, par * T:(par + 1) * T],
                scalar1=acol[:, oc:oc + 1], scalar2=bcol[:, oc:oc + 1],
                op0=ALU.mult, op1=ALU.add), None)
            sc.add("vector", lambda e, par=par: e.tensor_mul(
                out=t2[:], in0=grow[:, par * T:(par + 1) * T], in1=spb[:]), None)
            sc.on("vector", lambda e: e.drain())
            if oc >= 2:
                sc.wait("vector", f"ch{4 + par}")
            sc.add("vector", lambda e, par=par: e.tensor_mul(
                out=gb[:, par * T:(par + 1) * T], in0=t2[:], in1=yrow[:]), None)
            sc.add("vector", lambda e: e.drain(), "dv0", 1)
            gdone[oc] = mk.val("dv0")
            sc.wait("scalar", "dv0", gdone[oc])
            sc.add("scalar", lambda e, oc=oc, par=par: e.dma_start(
                out=gated_b[oc * 128:(oc + 1) * 128, :],
                in_=gb[:, par * T:(par + 1) * T]), f"ch{4 + par}", 16)
        sc.emit()
    mk.barrier()
    emit_oproj_ln(mk, "c5b", gated_b, "wo1T", None, x2t, x3t, x3f)



def ph_ffn1(mk):
    nc = mk.nc
    x3f = mk.dram("x3f", [D, T], F32)
    x3t = mk.dram("x3t", [T, D], F32)
    mk.ein("f1g", [D, FF1], I8)
    mk.ein("f1u", [D, FF1], I8)
    mk.ein("f1d", [FF1, D], I8)
    mk.ein("sgc", [128, 64], F32)
    mk.ein("suc", [128, 64], F32)
    mk.ein("sdc", [128, 16], F32)
    mk.ein("c_eye_f32", [128, 128], F32)
    out_t = mk.dram("out_t", [T, D], F32)
    NSEC = 8
    OSEC = FF1 // NSEC

    es0 = ExitStack()
    sb0 = lambda n_, sh, dt: es0.enter_context(nc.sbuf_tensor(n_, sh, dt))
    with es0:
        x4pre = sb0("d_x4", [128, NB * 512], F32)
        sg_sb = sb0("d_sg", [128, 64], F32)
        su_sb = sb0("d_su", [128, 64], F32)
        sd_sb = sb0("d_sd", [128, 16], F32)
        eye_f = sb0("d_eye", [128, 128], F32)

        sc = Sched(mk)
        sc.add("sync", lambda e: e.dma_start(out=sg_sb[:], in_=mk.t["sgc"]), "qS", 16)
        sc.add("sync", lambda e: e.dma_start(out=su_sb[:], in_=mk.t["suc"]), "qS", 16)
        sc.add("sync", lambda e: e.dma_start(out=sd_sb[:], in_=mk.t["sdc"]), "qS", 16)
        sc.add("sync", lambda e: e.dma_start(out=eye_f[:], in_=mk.t["c_eye_f32"]), "qS", 16)
        sc.wait("scalar", "qS")
        sc.wait("vector", "qS")
        sc.wait("tensor", "qS")
        sc.emit()
        mk.barrier()

        for tt in range(4):
            esT = ExitStack()
            sbT = lambda n_, sh, dt: esT.enter_context(nc.sbuf_tensor(n_, sh, dt))
            with esT:
                fuse = sbT(f"d_fu{tt}", [128, 64 * 512], BF16)
                xtt = sbT(f"d_x{tt}", [128, NB * 512], BF16)

                esG = ExitStack()
                sbG = lambda n_, sh, dt: esG.enter_context(nc.sbuf_tensor(n_, sh, dt))
                with esG:
                    wsec = [sbG(f"d_w{tt}_{i}", [128, NB * OSEC], BF16) for i in range(2)]
                    utmp = sbG(f"d_ut{tt}", [128, 2 * 512], F32)
                    p_g = [esG.enter_context(nc.psum_tensor(f"d_pg{tt}_{i}", [128, 512], F32))
                           for i in range(3)]
                    sc = Sched(mk)
                    for dc in range(NB):
                        sc.add("gpsimd", lambda e, dc=dc, tt=tt: e.dma_start(
                            out=xtt[:, dc * 512:(dc + 1) * 512],
                            in_=x3f[dc * 128:(dc + 1) * 128, tt * 512:(tt + 1) * 512]),
                            "qG", 16)
                    x_mark = mk.val("qG")
                    sc.wait("tensor", "qG", x_mark)
                    w_free = [0, 0]
                    stage = 0
                    pvm = {}
                    for mat, wname in ((0, "f1g"), (1, "f1u")):
                        for sec in range(NSEC):
                            buf = stage % 2
                            wch = f"wb{buf}"
                            sc.wait("gpsimd", "pe0", w_free[buf])
                            sc.add("gpsimd", lambda e, wname=wname, sec=sec, buf=buf:
                                   e.dma_start(
                                       out=wsec[buf][:].rearrange("p (hc o) -> p hc o", hc=NB),
                                       in_=mk.t[wname].rearrange("(hc p) o -> p hc o", p=128)
                                           [:, :, sec * OSEC:(sec + 1) * OSEC]), wch, 16)
                            wmk = mk.val(wch)
                            sc.wait("tensor", wch, wmk)
                            for oi in range(8):
                                ocg = sec * 8 + oi
                                i = stage * 8 + oi
                                if i >= 3:
                                    psem, pval = pvm[i - 3]
                                    sc.wait("tensor", psem, pval)
                                sc.add("tensor", lambda e, oi=oi, buf=buf, i=i: [
                                    e.matmul(p_g[i % 3][:],
                                             wsec[buf][:, dc_ * OSEC + oi * 128:
                                                       dc_ * OSEC + oi * 128 + 128],
                                             xtt[:, dc_ * 512:(dc_ + 1) * 512],
                                             start=(dc_ == 0), stop=(dc_ == NB - 1))
                                    for dc_ in range(NB)][-1], "pe0", 1)
                                if mat == 0:
                                    sc.wait("scalar", "pe0")
                                    sc.add("scalar", lambda e, ocg=ocg, i=i: e.activation(
                                        out=fuse[:, ocg * 512:(ocg + 1) * 512],
                                        in_=p_g[i % 3][:], func=AF.Silu,
                                        scale=sg_sb[:, ocg:ocg + 1]), "sc1", 1)
                                else:
                                    sc.wait("vector", "pe0")
                                    sc.add("vector", lambda e, ocg=ocg, i=i, oi=oi: e.tensor_scalar(
                                        out=utmp[:, (oi % 2) * 512:((oi % 2) + 1) * 512],
                                        in0=p_g[i % 3][:], scalar1=su_sb[:, ocg:ocg + 1],
                                        scalar2=None, op0=ALU.mult), None)
                                    sc.on("vector", lambda e: e.drain())
                                    sc.add("vector", lambda e, ocg=ocg, oi=oi: e.tensor_mul(
                                        out=fuse[:, ocg * 512:(ocg + 1) * 512],
                                        in0=fuse[:, ocg * 512:(ocg + 1) * 512],
                                        in1=utmp[:, (oi % 2) * 512:((oi % 2) + 1) * 512]),
                                        "dv1", 1)
                                pvm[i] = (("sc1", mk.val("sc1")) if mat == 0
                                          else ("dv1", mk.val("dv1")))
                            w_free[buf] = mk.val("pe0")
                            stage += 1
                    sc.emit()
                mk.barrier()

                esD = ExitStack()
                sbD = lambda n_, sh, dt: esD.enter_context(nc.sbuf_tensor(n_, sh, dt))
                with esD:
                    wdn = [sbD(f"d_wd{tt}_{i}", [128, 8 * D], BF16) for i in range(2)]
                    p_d = [esD.enter_context(nc.psum_tensor(f"d_pd{tt}_{i}", [128, 512], F32))
                           for i in range(8)]
                    sc = Sched(mk)
                    for hg in range(2):
                        dvb = mk.val("dv1")
                        w_free = [mk.val("pe0")] * 2
                        for og in range(8):
                            buf = og % 2
                            wch = f"wb{buf}"
                            sc.wait("gpsimd", "pe0", w_free[buf])
                            sc.add("gpsimd", lambda e, og=og, buf=buf: e.dma_start(
                                out=wdn[buf][:].rearrange("p (oc h) -> p oc h", oc=8),
                                in_=mk.t["f1d"].rearrange("(oc p) h -> p oc h", p=128)
                                    [:, og * 8:(og + 1) * 8, :]), wch, 16)
                            wmk = mk.val(wch)
                            sc.wait("tensor", wch, wmk)
                            if og == 0 and hg == 0:
                                sc.wait("tensor", "dv1", dvb)
                            if og == 0 and hg == 1:
                                sc.wait("tensor", "dv1", dvb)
                            for oi in range(8):
                                oc = og * 8 + oi

                                def mmd(e, oc=oc, oi=oi, buf=buf, hg=hg):
                                    for hh in range(8):
                                        last = e.matmul(
                                            p_d[hh][:],
                                            wdn[buf][:, oi * D + (hg * 8 + hh) * 128:
                                                     oi * D + (hg * 8 + hh) * 128 + 128],
                                            fuse[:, oc * 512:(oc + 1) * 512],
                                            start=(oc == 0), stop=(oc == 63))
                                    return last
                                sc.add("tensor", mmd, "pe0" if oi == 7 else None, 1)
                            w_free[buf] = mk.val("pe0")
                        sc.wait("vector", "pe0")
                        for hh in range(8):
                            hc = hg * 8 + hh
                            sc.add("vector", lambda e, hh=hh, hc=hc: e.tensor_scalar(
                                out=x4pre[:, hc * 512:(hc + 1) * 512], in0=p_d[hh][:],
                                scalar1=sd_sb[:, hc:hc + 1], scalar2=None,
                                op0=ALU.mult), None)
                        sc.add("vector", lambda e: e.drain(), "dv1", 1)
                    sc.emit()
                mk.barrier()

            esL = ExitStack()
            sbL = lambda n_, sh, dt: esL.enter_context(nc.sbuf_tensor(n_, sh, dt))
            with esL:
                trow = sbL(f"dL_tr{tt}", [128, 2 * D], F32)
                xres = sbL(f"dL_xr{tt}", [128, 2 * D], F32)
                lnrow = sbL(f"dL_ln{tt}", [128, 2 * D], F32)
                sq = sbL(f"dL_sq{tt}", [128, D], F32)
                s1 = sbL(f"dL_s1{tt}", [128, 1], F32)
                s2 = sbL(f"dL_s2{tt}", [128, 1], F32)
                s3 = sbL(f"dL_s3{tt}", [128, 1], F32)
                p_tr = [esL.enter_context(nc.psum_tensor(f"dL_pt{tt}_{i}", [128, 128], F32))
                        for i in range(2)]
                sc = Sched(mk)
                st_m = {0: None, 1: None}
                dv_m = {}
                for tc4 in range(4):
                    tcg = tt * 4 + tc4
                    par = tc4 % 2
                    lch = f"ch{2 + par}"
                    if tc4 >= 2:
                        sc.wait("sync", "dv0", dv_m[tc4 - 2])
                    sc.add("sync", lambda e, tcg=tcg, par=par: e.dma_start(
                        out=xres[:, par * D:(par + 1) * D],
                        in_=x3t[tcg * 128:(tcg + 1) * 128, :]), lch, 16)
                    xmk = mk.val(lch)
                    ev1b = mk.val("ev1")
                    if st_m[par] is not None:
                        sc.wait("scalar", f"ch{4 + par}", st_m[par])
                    for hc in range(NB):
                        if ev1b + hc - 1 > 0:
                            sc.wait("tensor", "ev1", ev1b + hc - 1)
                        sc.add("tensor", lambda e, hc=hc, tc4=tc4: e.transpose(
                            p_tr[hc % 2][:],
                            x4pre[:, hc * 512 + tc4 * 128: hc * 512 + (tc4 + 1) * 128],
                            eye_f[:]), "pe1", 1)
                        sc.wait("scalar", "pe1")
                        sc.add("scalar", lambda e, hc=hc, par=par: e.activation(
                            out=trow[:, par * D + hc * 128: par * D + (hc + 1) * 128],
                            in_=p_tr[hc % 2][:], func=AF.Copy), "ev1", 1)
                    sc.on("scalar", lambda e: e.drain())
                    sc.add("scalar", lambda e: e.nop(), "sc1", 1)
                    sc.wait("vector", "sc1")
                    sc.wait("vector", lch, xmk)
                    sc.add("vector", lambda e, par=par: e.tensor_add(
                        out=xres[:, par * D:(par + 1) * D],
                        in0=trow[:, par * D:(par + 1) * D],
                        in1=xres[:, par * D:(par + 1) * D]), "dv0", 1)
                    sc.on("vector", lambda e: e.drain())
                    sc.wait("scalar", "dv0")
                    ln_row(sc, mk, xres[:, par * D:(par + 1) * D],
                           lnrow[:, par * D:(par + 1) * D], sq, s1, s2, s3, D)
                    dv_m[tc4] = mk.val("dv0")
                    sc.wait("scalar", "dv0")
                    sc.add("scalar", lambda e, tcg=tcg, par=par: e.dma_start(
                        out=out_t[tcg * 128:(tcg + 1) * 128, :],
                        in_=lnrow[:, par * D:(par + 1) * D]), f"ch{4 + par}", 16)
                    st_m[par] = mk.val(f"ch{4 + par}")
                sc.emit()
            mk.barrier()



PHASES = {
    "qkv0": (ph_qkv0, [], ["q0f", "k0f", "v0t"]),
    "attn0": (ph_attn0, ["q0f", "k0a", "v0a"], ["ao_f"]),
    "wo0ln": (ph_wo0ln, ["ao_f"], ["x1t", "x1f"]),
    "moe": (lambda mk: (ph_moeprep(mk), ph_moe(mk)), ["x1f", "x1t"], ["x2t", "x2f"]),
    "qkv1": (ph_qkv1, ["x2f"], ["q1t", "k1t", "v1t"]),
    "ret": (ph_ret, ["q1t", "k1t", "v1t", "k1a", "v1a"], ["ret_bf", "st4"]),
    "gw1": (ph_gw1, ["ret_bf", "st4a", "x2f", "x2t"], ["x3t", "x3f"]),
    "ffn1": (ph_ffn1, ["x3f", "x3t"], ["out_t"]),
}


def build_phase(name):
    fn, pins, pouts = PHASES[name]
    mk = MK(f"mk_{name}", ext_in=set(pins), ext_out=set(pouts))
    mk.begin()
    fn(mk)
    mk.finish()
    return mk


LAUNCH_ORDER = ["qkv0", "attn0", "wo0ln", "moe", "qkv1", "ret", "gw1", "ffn1"]


def ph_exchange(mk, pairs):
    if not hasattr(mk, "exch_mark"):
        mk.exch_mark = {}
    for src, dst in pairs:
        sap = mk.t[src]
        mk.dram(dst, [2 * sap.shape[0]] + list(sap.shape[1:]), sap.dtype)
    sc = Sched(mk)
    for src, dst in pairs:
        sc.add("gpsimd", lambda e, src=src, dst=dst: e.collective_compute(
            "AllGather", ALU.bypass,
            replica_groups=[[0, 1], [2, 3], [4, 5], [6, 7]],
            ins=[mk.t[src].opt()], outs=[mk.t[dst].opt()]), "cc", 1)
        mk.exch_mark[dst] = mk.val("cc")
    sc.emit()


def build_fused():
    mk = MK("mk_fused", ext_in=(), ext_out=("out_t",))
    mk.fused_mode = True
    mk.begin()
    steps = [("qkv0", ph_qkv0),
             ("moeprep", ph_moeprep),
             ("attn0", ph_attn0),
             ("wo0ln", ph_wo0ln),
             ("moe", ph_moe),
             ("qkv1", ph_qkv1),
             ("exch1", lambda m: ph_exchange(m, [("k1t", "k1a"), ("v1t", "v1a")])),
             ("ret", ph_ret),
             ("exch2", lambda m: ph_exchange(m, [("st4", "st4a")])),
             ("gw1", ph_gw1),
             ("ffn1", ph_ffn1)]
    for nm, fn in steps:
        with mk.nc.named_scope(nm):
            fn(mk)
    mk.finish()
    return mk


def kernel_fused(inp):
    sh, pc = host_prep(inp)
    mk = build_fused()
    in_maps = []
    for c in range(8):
        feed = {}
        for tn in mk.input_names:
            if tn in sh:
                feed[tn] = sh[tn]
            elif tn in pc[c]:
                feed[tn] = pc[c][tn]
            else:
                raise KeyError(f"fused: unfed input {tn}")
        in_maps.append(feed)
    res = run_bass_kernel_spmd(mk.nc, in_maps, core_ids=list(range(8)))
    out = np.empty((4, 4096, 2048), np.float32)
    for c in range(8):
        b, hf = c // 2, c % 2
        out[b, hf * T:(hf + 1) * T, :] = res.results[c]["out_t"]
    return out


def kernel(**inp):
    inp = {k: np.asarray(v) for k, v in inp.items()}
    if FUSED:
        return kernel_fused(inp)
    sh, pc = host_prep(inp)
    inter = [dict() for _ in range(8)]

    for name in LAUNCH_ORDER:
        mk = build_phase(name)
        _, pins, pouts = PHASES[name]
        in_maps = []
        for c in range(8):
            feed = {}
            for tn in mk.input_names:
                if tn in inter[c]:
                    feed[tn] = inter[c][tn]
                elif tn in sh:
                    feed[tn] = sh[tn]
                elif tn in pc[c]:
                    feed[tn] = pc[c][tn]
                else:
                    raise KeyError(f"{name}: unfed input {tn}")
            in_maps.append(feed)
        res = run_bass_kernel_spmd(mk.nc, in_maps, core_ids=list(range(8)))
        for c in range(8):
            for po in pouts:
                inter[c][po] = res.results[c][po]
        if name == "qkv0":
            for c in range(8):
                p0, p1 = (c // 2) * 2, (c // 2) * 2 + 1
                inter[c]["k0a"] = np.concatenate(
                    [inter[p0]["k0f"], inter[p1]["k0f"]], axis=0)
                inter[c]["v0a"] = np.concatenate(
                    [inter[p0]["v0t"], inter[p1]["v0t"]], axis=0)
        if name == "qkv1":
            for c in range(8):
                p0, p1 = (c // 2) * 2, (c // 2) * 2 + 1
                inter[c]["k1a"] = np.concatenate(
                    [inter[p0]["k1t"], inter[p1]["k1t"]], axis=0)
                inter[c]["v1a"] = np.concatenate(
                    [inter[p0]["v1t"], inter[p1]["v1t"]], axis=0)
        if name == "ret":
            for c in range(8):
                p0, p1 = (c // 2) * 2, (c // 2) * 2 + 1
                inter[c]["st4a"] = np.concatenate(
                    [inter[p0]["st4"], inter[p1]["st4"]], axis=0)

    out = np.empty((4, 4096, 2048), np.float32)
    for c in range(8):
        b, hf = c // 2, c % 2
        out[b, hf * T:(hf + 1) * T, :] = inter[c]["out_t"]
    return out


# version 13
# speedup vs baseline: 1.1250x; 1.1250x over previous
import math
from contextlib import ExitStack

import numpy as np
import ml_dtypes

import concourse.bass as bass
import concourse.mybir as mybir
from concourse.bass_utils import run_bass_kernel_spmd

F32 = mybir.dt.float32
BF16 = mybir.dt.bfloat16
I8 = mybir.dt.int8
AF = mybir.ActivationFunctionType
ALU = mybir.AluOpType
AX = mybir.AxisListType
BF = ml_dtypes.bfloat16

D = 2048
T = 2048
SEQ = 4096
NB = 16
NT = 16
H0, HD0 = 16, 128
H1, HD1 = 32, 64
RANK = 16
LORA_SCALE = 0.5
ATT_SCALE = HD0 ** (-0.5)
FMOE = 1408
NFC = 11
FSH = 2816
NEXP = 8
FF1 = 8192
EPS = 1e-5

_NPDT = {np.dtype(np.float32): F32, np.dtype(BF): BF16, np.dtype(np.int8): I8}

FUSED = True


def _c(a):
    return np.ascontiguousarray(a)



def host_prep(inp):
    sh = {}
    sh["wqT"] = _c(inp["L0_attn_Wq"].T)
    sh["wkT"] = _c(inp["L0_attn_Wk"].T)
    sh["wvT"] = _c(inp["L0_attn_Wv"].T)
    sh["woT"] = _c(inp["L0_attn_Wo"].T)
    for p in "qkvo":
        sh[f"B{p}T"] = _c(inp[f"L0_attn_B{p}"].T)
        sh[f"A{p}T"] = _c(inp[f"L0_attn_A{p}"].T)
    sh["bq_row"] = _c(inp["L0_attn_bq"].reshape(1, D))
    sh["bk_row"] = _c(inp["L0_attn_bk"].reshape(1, D))
    sh["bv_row"] = _c(inp["L0_attn_bv"].reshape(1, D))
    sh["bo_row"] = _c(inp["L0_attn_bo"].reshape(1, D))
    sh["gwT"] = _c(inp["L0_ffn_gate_weight"].T)
    sh["wgTe"] = _c(inp["L0_ffn_gate_w"].transpose(0, 2, 1))
    sh["wuTe"] = _c(inp["L0_ffn_up_w"].transpose(0, 2, 1))
    sh["wdTe"] = _c(inp["L0_ffn_down_w"].transpose(0, 2, 1))
    sh["shgT"] = _c(inp["L0_ffn_sh_gate_w"].T)
    sh["shuT"] = _c(inp["L0_ffn_sh_up_w"].T)
    sh["shdT"] = _c(inp["L0_ffn_sh_down_w"].T)
    sh["wqkvT"] = _c(inp["L1_attn_wqkv"].T)
    sh["wg1T"] = _c(inp["L1_attn_wg"].T)
    sh["wo1T"] = _c(inp["L1_attn_wo"].T)
    sh["gnw2"] = _c(inp["L1_attn_gn_weight"].reshape(16, 2).T)
    sh["gnb2"] = _c(inp["L1_attn_gn_bias"].reshape(16, 2).T)
    sh["f1g"] = inp["L1_ffn_w_gate"]
    sh["f1u"] = inp["L1_ffn_w_up"]
    sh["f1d"] = inp["L1_ffn_w_down"]
    sh["sgc"] = _c(inp["L1_ffn_s_gate"].reshape(FF1 // 128, 128).T)
    sh["suc"] = _c(inp["L1_ffn_s_up"].reshape(FF1 // 128, 128).T)
    sh["sdc"] = _c(inp["L1_ffn_s_down"].reshape(NB, 128).T)

    x = inp["L0_attn_x"]
    percore = []
    for c in range(8):
        b, hf = c // 2, c % 2
        pc = {}
        pc["xT"] = _c(x[b, hf * T:(hf + 1) * T, :].T)
        pc["x_tm"] = _c(x[b, hf * T:(hf + 1) * T, :])
        pc.update(host_consts(hf))
        percore.append(pc)
    return sh, percore


def host_consts(half):
    c = {}
    c["c_eye_bf"] = np.eye(128, dtype=BF)
    c["c_eye_f32"] = np.eye(128, dtype=np.float32)
    c["c_ones_bf"] = np.ones((128, 512), dtype=BF)
    c["c_ones_f32"] = np.ones((128, 512), dtype=np.float32)
    hp = np.zeros((128, 2), dtype=np.float32)
    hp[0:64, 0] = 1.0
    hp[64:128, 1] = 1.0
    c["c_halfpick"] = hp
    c["c_hpT"] = _c(hp.T)

    dim = 32
    t0 = half * T
    pos = np.arange(t0, t0 + T, dtype=np.float64)
    inv_freq = 1.0 / 10000.0 ** (np.arange(0, dim, 2, dtype=np.float64) / dim)
    freqs = np.repeat(pos[:, None] * inv_freq[None, :], 2, axis=-1)
    base_scale = (np.arange(0, dim, 2, dtype=np.float64) + 0.4 * dim) / (1.4 * dim)
    power = (pos - SEQ // 2) / 512.0
    scale = base_scale[None, :] ** power[:, None]
    scale = np.concatenate([scale, scale], axis=-1)
    c["c_cq"] = (np.cos(freqs) * scale).astype(np.float32)
    c["c_sq"] = (np.sin(freqs) * scale).astype(np.float32)
    c["c_ck"] = (np.cos(freqs) / scale).astype(np.float32)
    c["c_sk"] = (np.sin(freqs) / scale).astype(np.float32)

    gam = 1.0 - 2.0 ** (-5.0 - np.arange(H1, dtype=np.float64))
    i = np.arange(128, dtype=np.float64)
    mask = (i[None, :] >= i[:, None]).astype(np.float64)
    d0 = mask[None] * (gam[:, None, None] ** (-i)[:, None])
    c["c_d0"] = _c(d0.transpose(1, 0, 2).reshape(128, H1 * 128).astype(np.float32))
    qdec = gam[None, :, None] ** i[:, None, None]
    c["c_qdec"] = _c(np.broadcast_to(qdec, (128, H1, HD1)).reshape(128, D).astype(np.float32))
    kdec = gam[None, :, None] ** (128.0 - i)[:, None, None]
    c["c_kdec"] = _c(np.broadcast_to(kdec, (128, H1, HD1)).reshape(128, D).astype(np.float32))
    c["c_kdec_prev"] = c["c_kdec"] * np.float32(1.0 if half == 1 else 0.0)
    sdec = np.broadcast_to((gam ** 128.0)[None, :, None], (64, H1, HD1))
    c["c_sdec"] = _c(sdec.reshape(64, H1 * HD1).astype(np.float32))
    return c



class MK:
    def __init__(self, name, ext_in=(), ext_out=()):
        self.nc = bass.Bass("TRN2", target_bir_lowering=False, num_devices=8,
                            name=name)
        self.ext_in = set(ext_in)
        self.ext_out = set(ext_out)
        self.input_names = set()
        self.t = {}
        self.sem = {}
        self.semn = {}
        self.phase_i = 0
        self.block = None
        self._es = ExitStack()
        for s in ("qS", "qG", "qA", "bar", "pe0", "pe1", "pe2", "ev0", "ev1",
                  "ev2", "dv0", "dv1", "sc0", "sc1", "ld0", "ld1", "cc",
                  "ch0", "ch1", "ch2", "ch3", "ch4", "ch5", "ch6", "ch7",
                  "wb0", "wb1", "st0", "st1", "bg0"):
            self.sem[s] = self.nc.alloc_semaphore(s)
            self.semn[s] = 0

    def dram(self, name, shape, dtype):
        if name in self.t:
            return self.t[name]
        if name in self.ext_in:
            kind = "ExternalInput"
        elif name in self.ext_out:
            kind = "ExternalOutput"
        else:
            kind = "Internal"
        ap = self.nc.dram_tensor(name, list(shape), dtype, kind=kind).ap()
        if kind == "ExternalInput":
            self.input_names.add(name)
        self.t[name] = ap
        return ap

    def ein(self, name, shape, dtype):
        if name in self.t:
            return self.t[name]
        ap = self.nc.dram_tensor(name, list(shape), dtype, kind="ExternalInput").ap()
        self.input_names.add(name)
        self.t[name] = ap
        return ap

    def inc(self, instr, s, k=1):
        instr.then_inc(self.sem[s], k)
        self.semn[s] += k
        return instr

    def val(self, s):
        return self.semn[s]

    def wait(self, eng, s, val=None):
        eng.wait_ge(self.sem[s], self.semn[s] if val is None else val)

    def begin(self):
        self.block = self._es.enter_context(self.nc.Block("mk"))

    def finish(self):
        self._es.close()
        self.block = None

    DMA_SEMS_SYNC = ("qS", "ch0", "ch1", "ch2", "ch3", "ch4", "ch5", "ch6", "ch7")
    DMA_SEMS_GP = ("qG", "wb0", "wb1", "st0", "st1")

    def barrier(self):
        self.phase_i += 1
        tgt = 5 * self.phase_i
        b = self.sem["bar"]

        def mk_body(qsems):
            vals = [(self.sem[q], self.semn[q]) for q in qsems]

            def f(e):
                for h, v in vals:
                    if v:
                        e.wait_ge(h, v)
                e.drain().then_inc(b, 1)
                e.wait_ge(b, tgt)
            return f
        self.block.sync(mk_body(self.DMA_SEMS_SYNC))
        self.block.gpsimd(mk_body(self.DMA_SEMS_GP))
        self.block.scalar(mk_body(("qA",)))
        self.block.vector(mk_body(()))
        self.block.tensor(mk_body(()))
        self.semn["bar"] += 5


def run_groups(mk, groups, psums, pe_sem="pe0", ev_sem="ev0", evac_eng="scalar",
               pre=None, post_t=None):
    blk = mk.block
    n = len(psums)
    base_pe = mk.val(pe_sem)
    base_ev = mk.val(ev_sem)
    ngr = len(groups)

    def tbody(e):
        if pre is not None:
            pre(e)
        for i, (mm, _) in enumerate(groups):
            v = base_ev + i - n + 1
            if 0 < v:
                e.wait_ge(mk.sem[ev_sem], v)
            mm(e, psums[i % n]).then_inc(mk.sem[pe_sem], 1)
        if post_t is not None:
            post_t(e)

    fin = lambda instr: instr.then_inc(mk.sem[ev_sem], 1)

    def ebody(e):
        for i, (_, ev) in enumerate(groups):
            e.wait_ge(mk.sem[pe_sem], base_pe + i + 1)
            ev(e, psums[i % n], fin)

    blk.tensor(tbody)
    mk.semn[pe_sem] += ngr
    getattr(blk, evac_eng)(ebody)
    mk.semn[ev_sem] += ngr


class Sched:
    ENGS = ("sync", "gpsimd", "scalar", "vector", "tensor")

    def __init__(self, mk):
        self.mk = mk
        self.ops = {e: [] for e in self.ENGS}

    def on(self, eng, fn):
        self.ops[eng].append(fn)

    def wait(self, eng, s, val=None):
        v = self.mk.val(s) if val is None else val
        h = self.mk.sem[s]
        self.ops[eng].append(lambda e: e.wait_ge(h, v))

    def add(self, eng, fn, inc=None, inck=1):
        if inc is None:
            self.ops[eng].append(fn)
        else:
            h = self.mk.sem[inc]
            self.ops[eng].append(lambda e: fn(e).then_inc(h, inck))
            self.mk.semn[inc] += inck

    def emit(self):
        blk = self.mk.block
        for eng in self.ENGS:
            lst = self.ops[eng]
            if not lst:
                continue
            getattr(blk, eng)(lambda e, lst=lst: [f(e) for f in lst])



def ph_qkv0(mk):
    nc, blk = mk.nc, mk.block
    xT = mk.ein("xT", [D, T], F32)
    for w in ("wqT", "wkT", "wvT"):
        mk.ein(w, [D, D], F32)
    for p in "qkv":
        mk.ein(f"B{p}T", [D, RANK], F32)
        mk.ein(f"A{p}T", [RANK, D], F32)
    mk.ein("bq_row", [1, D], F32)
    mk.ein("bk_row", [1, D], F32)
    mk.ein("bv_row", [1, D], F32)
    mk.ein("c_ones_bf", [128, 512], BF16)
    q0f = mk.dram("q0f", [D, T], BF16)
    k0f = mk.dram("k0f", [D, T], BF16)
    v0t = mk.dram("v0t", [T, D], BF16)

    es = ExitStack()
    sb = lambda n_, sh, dt: es.enter_context(nc.sbuf_tensor(n_, sh, dt))
    pst = lambda n_, sh, dt: es.enter_context(nc.psum_tensor(n_, sh, dt))
    with es:
        xfm = sb("a1_xfm", [128, NB * T], BF16)
        wt = sb("a1_wt", [128, NB * D], BF16)
        aT = sb("a1_aT", [RANK, 3 * D], BF16)
        bT = sb("a1_bT", [128, 3 * NB * RANK], BF16)
        zz = sb("a1_z", [RANK, 3 * T], BF16)
        brow = sb("a1_brow", [1, 3 * D], BF16)
        ones = sb("a1_ones", [128, 512], BF16)
        rows = sb("a1_rows", [128, 2 * D], BF16)
        p_mm = [pst(f"a1_pmm{i}", [128, 512], F32) for i in range(4)]
        p_z = [pst(f"a1_pz{i}", [RANK, 512], F32) for i in range(2)]

        def g0(e):
            for dc in range(NB):
                mk.inc(e.dma_start(out=xfm[:, dc * T:(dc + 1) * T],
                                   in_=xT[dc * 128:(dc + 1) * 128, :]), "qG", 16)
            for pi, p in enumerate("qkv"):
                mk.inc(e.dma_start(out=aT[0:RANK, pi * D:(pi + 1) * D],
                                   in_=mk.t[f"A{p}T"]), "qG", 16)
                mk.inc(e.dma_start(
                    out=bT[:, pi * NB * RANK:(pi + 1) * NB * RANK]
                        .rearrange("p (dc r) -> p dc r", dc=NB),
                    in_=mk.t[f"B{p}T"].rearrange("(dc p) r -> p dc r", p=128)),
                    "qG", 16)
            mk.inc(e.dma_start(out=brow[0:1, 0:D], in_=mk.t["bk_row"]), "qG", 16)
            mk.inc(e.dma_start(out=brow[0:1, D:2 * D], in_=mk.t["bv_row"]), "qG", 16)
            mk.inc(e.dma_start(out=brow[0:1, 2 * D:3 * D], in_=mk.t["bq_row"]), "qG", 16)
        blk.gpsimd(g0)
        small_done = mk.val("qG")

        def s0(e):
            mk.inc(e.dma_start(out=ones[:], in_=mk.t["c_ones_bf"]), "qS", 16)
        blk.sync(s0)
        sync_done = mk.val("qS")

        zgroups = []
        for pi in range(3):
            for tt in range(4):
                def mm(e, psz, pi=pi, tt=tt):
                    for dc in range(NB):
                        last = e.matmul(
                            psz[:],
                            bT[:, (pi * NB + dc) * RANK:(pi * NB + dc + 1) * RANK],
                            xfm[:, dc * T + tt * 512: dc * T + (tt + 1) * 512],
                            start=(dc == 0), stop=(dc == NB - 1))
                    return last

                def ev(e, psz, fin, pi=pi, tt=tt):
                    fin(e.activation(
                        out=zz[0:RANK, pi * T + tt * 512: pi * T + (tt + 1) * 512],
                        in_=psz[:], func=AF.Copy, scale=LORA_SCALE))
                zgroups.append((mm, ev))

        def pre_z(e):
            e.wait_ge(mk.sem["qG"], small_done)
        run_groups(mk, zgroups, p_z, pre=pre_z)

        last_store = {0: None, 1: None}
        for p in ("k", "v", "q"):
            pi = "qkv".index(p)
            pe_before = mk.val("pe0")

            def gw(e, p=p, pe_before=pe_before):
                e.wait_ge(mk.sem["pe0"], pe_before)
                for dc in range(NB):
                    mk.inc(e.dma_start(out=wt[:, dc * D:(dc + 1) * D],
                                       in_=mk.t[f"w{p}T"][dc * 128:(dc + 1) * 128, :]),
                           "qG", 16)
            blk.gpsimd(gw)
            w_done = mk.val("qG")

            groups = []
            if p in ("q", "k"):
                fdst = q0f if p == "q" else k0f
                fboff = 2 * D if p == "q" else 0
                for oc in range(NB):
                    for tt in range(4):
                        def mm(e, ps_, oc=oc, tt=tt, pi=pi, fboff=fboff):
                            for dc in range(NB):
                                e.matmul(ps_[:],
                                         wt[:, dc * D + oc * 128: dc * D + (oc + 1) * 128],
                                         xfm[:, dc * T + tt * 512: dc * T + (tt + 1) * 512],
                                         start=(dc == 0), stop=False)
                            e.matmul(ps_[:],
                                     aT[0:RANK, pi * D + oc * 128: pi * D + (oc + 1) * 128],
                                     zz[0:RANK, pi * T + tt * 512: pi * T + (tt + 1) * 512],
                                     start=False, stop=False)
                            return e.matmul(ps_[:],
                                            brow[0:1, fboff + oc * 128: fboff + (oc + 1) * 128],
                                            ones[0:1, 0:512],
                                            start=False, stop=True)

                        par = oc % 2
                        warv = last_store[par] if tt == 0 else None

                        def ev(e, ps_, fin, oc=oc, tt=tt, par=par, warv=warv, fdst=fdst):
                            if warv is not None:
                                e.wait_ge(mk.sem[f"ch{par}"], warv)
                            act = e.activation(out=rows[:, par * D + tt * 512: par * D + (tt + 1) * 512],
                                               in_=ps_[:], func=AF.Copy)
                            fin(act)
                            if tt == 3:
                                e.drain()
                                e.dma_start(
                                    out=fdst[oc * 128:(oc + 1) * 128, :],
                                    in_=rows[:, par * D:(par + 1) * D]).then_inc(
                                        mk.sem[f"ch{par}"], 16)
                        groups.append((mm, ev))
                        if tt == 3:
                            mk.semn[f"ch{par}"] += 16
                            last_store[par] = mk.val(f"ch{par}")
            else:
                dst = v0t
                boff = D
                for tc in range(NT):
                    for ot in range(4):
                        def mm(e, ps_, tc=tc, ot=ot, pi=pi, boff=boff):
                            for dc in range(NB):
                                e.matmul(ps_[:],
                                         xfm[:, dc * T + tc * 128: dc * T + (tc + 1) * 128],
                                         wt[:, dc * D + ot * 512: dc * D + (ot + 1) * 512],
                                         start=(dc == 0), stop=False)
                            e.matmul(ps_[:],
                                     zz[0:RANK, pi * T + tc * 128: pi * T + (tc + 1) * 128],
                                     aT[0:RANK, pi * D + ot * 512: pi * D + (ot + 1) * 512],
                                     start=False, stop=False)
                            return e.matmul(ps_[:],
                                            ones[0:1, 0:128],
                                            brow[0:1, boff + ot * 512: boff + (ot + 1) * 512],
                                            start=False, stop=True)

                        par = tc % 2
                        warv = last_store[par] if ot == 0 else None

                        def ev(e, ps_, fin, tc=tc, ot=ot, p=p, dst=dst, par=par, warv=warv):
                            if warv is not None:
                                e.wait_ge(mk.sem[f"ch{par}"], warv)
                            act = e.activation(out=rows[:, par * D + ot * 512: par * D + (ot + 1) * 512],
                                               in_=ps_[:], func=AF.Copy)
                            fin(act)
                            if ot == 3:
                                e.drain()
                                e.dma_start(
                                    out=dst[tc * 128:(tc + 1) * 128, :],
                                    in_=rows[:, par * D:(par + 1) * D]).then_inc(
                                        mk.sem[f"ch{par}"], 16)
                        groups.append((mm, ev))
                        if ot == 3:
                            mk.semn[f"ch{par}"] += 16
                            last_store[par] = mk.val(f"ch{par}")

            def pre(e, w_done=w_done):
                e.wait_ge(mk.sem["qG"], w_done)
                e.wait_ge(mk.sem["qS"], sync_done)
            run_groups(mk, groups, p_mm, pre=pre)
            if p == "v" and getattr(mk, "fused_mode", False):
                k0a = mk.dram("k0a", [2 * D, T], BF16)
                v0a = mk.dram("v0a", [SEQ, D], BF16)
                if not hasattr(mk, "exch_mark"):
                    mk.exch_mark = {}
                st0 = mk.val("ch0")
                st1 = mk.val("ch1")

                def gcoll(e):
                    e.wait_ge(mk.sem["ch0"], st0)
                    e.wait_ge(mk.sem["ch1"], st1)
                    mk.inc(e.collective_compute(
                        "AllGather", ALU.bypass,
                        replica_groups=[[0, 1], [2, 3], [4, 5], [6, 7]],
                        ins=[k0f.opt()], outs=[k0a.opt()]), "cc", 1)
                    mk.inc(e.collective_compute(
                        "AllGather", ALU.bypass,
                        replica_groups=[[0, 1], [2, 3], [4, 5], [6, 7]],
                        ins=[v0t.opt()], outs=[v0a.opt()]), "cc", 1)
                    mk.exch_mark["k0a"] = mk.exch_mark["v0a"] = mk.val("cc")
                blk.gpsimd(gcoll)

    mk.barrier()



def ph_attn0(mk):
    nc = mk.nc
    q0f = mk.dram("q0f", [D, T], BF16)
    k0a = mk.dram("k0a", [2 * D, T], BF16)
    v0a = mk.dram("v0a", [SEQ, D], BF16)
    ao_f = mk.dram("ao_f", [D, T], BF16)
    mk.ein("c_ones_bf", [128, 512], BF16)
    mk.ein("c_eye_f32", [128, 128], F32)
    NKC = SEQ // 128
    NKB = SEQ // 512

    es = ExitStack()
    sb = lambda n_, sh, dt: es.enter_context(nc.sbuf_tensor(n_, sh, dt))
    pst = lambda n_, sh, dt: es.enter_context(nc.psum_tensor(n_, sh, dt))
    with es:
        qh = sb("a2_qh", [128, 3 * T], BF16)
        kh = sb("a2_kh", [128, 3 * SEQ], BF16)
        vh4 = [sb(f"a2_vh{i}", [128, NKC * 512], BF16) for i in range(2)]
        pt = sb("a2_pt", [128, 2 * NKC * 512], BF16)
        mtmp = sb("a2_mtmp", [128, NKB], F32)
        mneg1 = sb("a2_mneg1", [128, 2], F32)
        m1sb = sb("a2_m1", [1, 2], BF16)
        mbias = sb("a2_mb", [128, 2], F32)
        lrec_f = sb("a2_lrecf", [1, 2 * 512], F32)
        lrec_b = sb("a2_lrecb", [1, 2 * 512], BF16)
        lb_f = sb("a2_lbf", [128, 2 * 512], F32)
        ao_t = sb("a2_aot", [128, 2 * 512], BF16)
        ones = sb("a2_ones", [128, 512], BF16)
        eye_f = sb("a2_eyef", [128, 128], F32)
        p_s = [pst(f"a2_ps{i}", [128, 512], F32) for i in range(2)]
        p_st = [pst(f"a2_pst{i}", [128, 512], F32) for i in range(2)]
        p_sm = pst("a2_psm", [1, 512], F32)
        p_mi = pst("a2_pmi", [128, 512], F32)
        p_lb = pst("a2_plb", [128, 512], F32)
        p_o = pst("a2_po", [128, 512], F32)

        sc = Sched(mk)
        sc.add("sync", lambda e: e.dma_start(out=ones[:], in_=mk.t["c_ones_bf"]), "qS", 16)
        sc.add("sync", lambda e: e.dma_start(out=eye_f[:], in_=mk.t["c_eye_f32"]), "qS", 16)
        sc.wait("tensor", "qS")

        iters = [(h, qt) for h in range(H0) for qt in range(4)]
        ld_marks = {}
        st_marks = {}

        prep_jobs = list(getattr(mk, "moeprep_jobs", []))
        prep_i = [0]

        def emit_prep_job():
            if prep_i[0] >= len(prep_jobs):
                return
            dst_ap, src_ap = prep_jobs[prep_i[0]]
            prep_i[0] += 1
            sc.add("gpsimd", lambda e, dst_ap=dst_ap, src_ap=src_ap:
                   e.dma_start(out=dst_ap, in_=src_ap), "bg0", 16)

        ccm = getattr(mk, "exch_mark", {}).get("v0a")

        vld = {}

        def emit_vload(g):
            buf = g % 2
            vch = f"ch{5 + buf}"
            if g == 0 and ccm is not None:
                sc.wait("sync", "cc", ccm)
            if g >= 2:
                sc.wait("sync", "ld1", 4 * g - 4)
            sc.add("sync", lambda e, g=g, buf=buf: e.dma_start(
                out=vh4[buf][:].rearrange("p (c f) -> p c f", c=NKC),
                in_=v0a.rearrange("(c p) d -> p c d", p=128)
                    [:, :, g * 512:(g + 1) * 512]), vch, 16)
            vld[g] = mk.val(vch)

        def emit_loads(h):
            par = h % 3
            lch = f"ch{2 + par}"
            if h % 4 == 0:
                emit_vload(h // 4)
            if h == 0 and ccm is not None:
                sc.wait("sync", "cc", ccm)
            if h >= 3:
                sc.wait("sync", "ld1", h - 2)
            sc.add("sync", lambda e, h=h, par=par: e.dma_start(
                out=qh[:, par * T:(par + 1) * T],
                in_=q0f[h * 128:(h + 1) * 128, :]), lch, 16)
            sc.add("sync", lambda e, h=h, par=par: e.dma_start(
                out=kh[:, par * SEQ: par * SEQ + T],
                in_=k0a[h * 128:(h + 1) * 128, :]), lch, 16)
            sc.add("sync", lambda e, h=h, par=par: e.dma_start(
                out=kh[:, par * SEQ + T:(par + 1) * SEQ],
                in_=k0a[D + h * 128: D + (h + 1) * 128, :]), lch, 16)
            ld_marks[h] = mk.val(lch)

        def stage_A(i):
            h, qt = iters[i]
            par, ip = h % 3, i % 2
            if qt == 0:
                sc.wait("tensor", f"ch{2 + par}", ld_marks[h])
            dvb = mk.val("dv0")
            for blk in range(NKB):
                if blk >= 2:
                    sc.wait("tensor", "dv0", dvb + blk - 1)
                sc.add("tensor", lambda e, h=h, par=par, qt=qt, blk=blk: e.matmul(
                    p_s[blk % 2][:],
                    qh[:, par * T + qt * 512: par * T + (qt + 1) * 512].rearrange(
                        "p (a b) -> p (a b)") if False else
                    qh[:, par * T + qt * 512: par * T + qt * 512 + 128],
                    kh[:, par * SEQ + blk * 512: par * SEQ + (blk + 1) * 512],
                    start=True, stop=True), "pe0", 1)
                sc.wait("vector", "pe0")
                sc.add("vector", lambda e, blk=blk: e.reduce_max(
                    out=mtmp[:, blk:blk + 1], in_=p_s[blk % 2][:],
                    axis=AX.X, negate=True), "dv0", 1)
            sc.on("vector", lambda e: e.drain())
            sc.add("vector", lambda e, ip=ip: e.tensor_reduce(
                out=mneg1[:, ip:ip + 1], in_=mtmp[:], axis=AX.X, op=ALU.min), None)
            sc.add("vector", lambda e: e.drain(), "dv1", 1)
            return mk.val("dv1")

        def stage_C(i, bmark):
            h, qt = iters[i]
            ip = i % 2
            sc.wait("tensor", "dv1", bmark)
            sc.wait("tensor", "sc1")
            sc.add("tensor", lambda e, ip=ip: e.transpose(
                p_mi[0:1, 0:128], mneg1[:, ip:ip + 1], eye_f[:]), "pe1", 1)
            sc.wait("vector", "pe1")
            sc.add("vector", lambda e, ip=ip: e.tensor_reduce(
                out=m1sb[0:1, ip:ip + 1], in_=p_mi[0:1, 0:128], axis=AX.X,
                op=ALU.min), None)
            sc.add("vector", lambda e: e.drain(), "dv1", 1)
            sc.wait("tensor", "dv1")
            sc.add("tensor", lambda e, ip=ip: e.matmul(
                p_mi[0:128, 256:257], ones[0:1, 0:128], m1sb[0:1, ip:ip + 1],
                start=True, stop=True), "pe1", 1)
            sc.wait("scalar", "pe1")
            sc.add("scalar", lambda e, ip=ip: e.activation(
                out=mbias[:, ip:ip + 1], in_=p_mi[0:128, 256:257], func=AF.Copy,
                scale=ATT_SCALE), None)
            sc.on("scalar", lambda e: e.drain())
            sc.add("scalar", lambda e: e.nop(), "sc1", 1)
            return mk.val("sc1")

        prev = None

        def emit_slot(i, cmark):
            nonlocal prev
            h, qt = iters[i]
            par, ip = h % 3, i % 2
            sc.wait("tensor", "sc1", cmark)
            ev_base = mk.val("ev0")
            if prev is not None:
                hp, qtp = iters[prev["i"]]
                parp, ipp = hp % 3, prev["i"] % 2
                sc.wait("tensor", "ev0", prev["evdone"])
                sc.wait("tensor", "dv0", prev["podone"])
            for k in range(NKC):
                if k >= 2:
                    sc.wait("tensor", "ev0", ev_base + k - 1)
                sc.add("tensor", lambda e, h=h, par=par, qt=qt, k=k: e.matmul(
                    p_st[k % 2][:],
                    kh[:, par * SEQ + k * 128: par * SEQ + (k + 1) * 128],
                    qh[:, par * T + qt * 512: par * T + (qt + 1) * 512],
                    start=True, stop=True), "pe0", 1)
                sc.wait("scalar", "pe0")
                sc.add("scalar", lambda e, k=k, ip=ip: e.activation(
                    out=pt[:, ip * NKC * 512 + k * 512: ip * NKC * 512 + (k + 1) * 512],
                    in_=p_st[k % 2][:], func=AF.Exp, scale=ATT_SCALE,
                    bias=mbias[:, ip:ip + 1]), "ev0", 1)
                if prev is not None:
                    if k < 16:
                        for j in (2 * k, 2 * k + 1):
                            sc.add("tensor", lambda e, j=j, ipp=ipp: e.matmul(
                                p_sm[0:1, 0:512], ones[0:128, 0:1],
                                pt[:, ipp * NKC * 512 + j * 512: ipp * NKC * 512 + (j + 1) * 512],
                                start=(j == 0), stop=(j == NKC - 1)),
                                "pe1" if j == NKC - 1 else None, 1)
                    else:
                        if k == 16:
                            sc.wait("vector", "pe1")
                            sc.add("vector", lambda e, ipp=ipp: e.reciprocal(
                                out=lrec_f[0:1, ipp * 512:(ipp + 1) * 512],
                                in_=p_sm[0:1, 0:512]), None)
                            sc.on("vector", lambda e: e.drain())
                            sc.add("vector", lambda e, ipp=ipp: e.tensor_copy(
                                out=lrec_b[0:1, ipp * 512:(ipp + 1) * 512],
                                in_=lrec_f[0:1, ipp * 512:(ipp + 1) * 512]), None)
                            sc.add("vector", lambda e: e.drain(), "dv1", 1)
                        for j in (2 * (k - 16), 2 * (k - 16) + 1):
                            gb = (hp // 4) % 2
                            if j == 0:
                                sc.wait("tensor", f"ch{5 + gb}", vld[hp // 4])
                            sc.add("tensor", lambda e, j=j, hp=hp, gb=gb, ipp=ipp: e.matmul(
                                p_o[:],
                                vh4[gb][:].rearrange("p (c f) -> p c f", c=NKC)
                                    [:, j, (hp % 4) * 128:(hp % 4) * 128 + 128],
                                pt[:, ipp * NKC * 512 + j * 512: ipp * NKC * 512 + (j + 1) * 512],
                                start=(j == 0), stop=(j == NKC - 1)),
                                "pe1" if j == NKC - 1 else None, 1)
            if prev is not None:
                sc.wait("tensor", "dv1")
                sc.add("tensor", lambda e, ipp=ipp: e.matmul(
                    p_lb[:], ones[0:1, 0:128],
                    lrec_b[0:1, ipp * 512:(ipp + 1) * 512],
                    start=True, stop=True), "pe2", 1)
                sc.wait("scalar", "pe2")
                sc.add("scalar", lambda e, ipp=ipp: e.activation(
                    out=lb_f[:, ipp * 512:(ipp + 1) * 512],
                    in_=p_lb[:], func=AF.Copy), "ev1", 1)
                io = prev["i"]
                opar = io % 2
                sc.wait("vector", "pe1")
                sc.wait("vector", "ev1")
                if io >= 2:
                    sc.wait("vector", f"st{opar}", st_marks[io - 2])
                sc.add("vector", lambda e, opar=opar: e.tensor_tensor(
                    out=ao_t[:, opar * 512:(opar + 1) * 512], in0=p_o[:],
                    in1=lb_f[:, opar * 512:(opar + 1) * 512], op=ALU.mult), None)
                if qtp == 3:
                    sc.add("vector", lambda e: e.drain(), "ld1", 1)
                else:
                    sc.on("vector", lambda e: e.drain())
                sc.add("vector", lambda e: e.engine_nop(), "dv0", 1)
                podone = mk.val("dv0")
                sc.wait("gpsimd", "dv0", podone)
                sc.add("gpsimd", lambda e, hp=hp, qtp=qtp, opar=opar: e.dma_start(
                    out=ao_f[hp * 128:(hp + 1) * 128, qtp * 512:(qtp + 1) * 512],
                    in_=ao_t[:, opar * 512:(opar + 1) * 512]), f"st{opar}", 16)
                st_marks[io] = mk.val(f"st{opar}")
                emit_prep_job()
            else:
                podone = 0
            prev_new = {"i": i, "evdone": mk.val("ev0"), "podone": podone}
            prev = prev_new

        emit_loads(0)
        emit_loads(1)
        emit_loads(2)
        bm = stage_A(0)
        cm = stage_C(0, bm)
        for i in range(len(iters)):
            emit_slot(i, cm)
            if i + 1 < len(iters):
                hn, qtn = iters[i + 1]
                if qtn == 0:
                    if hn + 2 < H0 and hn >= 1:
                        emit_loads(hn + 2)
                bm = stage_A(i + 1)
                cm = stage_C(i + 1, bm)
        i = len(iters)
        hp, qtp = iters[-1]
        ipp = (len(iters) - 1) % 2
        parp = hp % 3
        sc.wait("tensor", "ev0")
        sc.wait("tensor", "dv0")
        for j in range(NKC):
            sc.add("tensor", lambda e, j=j, ipp=ipp: e.matmul(
                p_sm[0:1, 0:512], ones[0:128, 0:1],
                pt[:, ipp * NKC * 512 + j * 512: ipp * NKC * 512 + (j + 1) * 512],
                start=(j == 0), stop=(j == NKC - 1)),
                "pe1" if j == NKC - 1 else None, 1)
        sc.wait("vector", "pe1")
        sc.add("vector", lambda e, ipp=ipp: e.reciprocal(
            out=lrec_f[0:1, ipp * 512:(ipp + 1) * 512], in_=p_sm[0:1, 0:512]), None)
        sc.on("vector", lambda e: e.drain())
        sc.add("vector", lambda e, ipp=ipp: e.tensor_copy(
            out=lrec_b[0:1, ipp * 512:(ipp + 1) * 512],
            in_=lrec_f[0:1, ipp * 512:(ipp + 1) * 512]), None)
        sc.add("vector", lambda e: e.drain(), "dv1", 1)
        sc.wait("tensor", "dv1")
        sc.add("tensor", lambda e, ipp=ipp: e.matmul(
            p_lb[:], ones[0:1, 0:128], lrec_b[0:1, ipp * 512:(ipp + 1) * 512],
            start=True, stop=True), "pe2", 1)
        sc.wait("scalar", "pe2")
        sc.add("scalar", lambda e, ipp=ipp: e.activation(
            out=lb_f[:, ipp * 512:(ipp + 1) * 512], in_=p_lb[:], func=AF.Copy),
            "ev1", 1)
        for j in range(NKC):
            gb = (hp // 4) % 2
            sc.add("tensor", lambda e, j=j, hp=hp, gb=gb, ipp=ipp: e.matmul(
                p_o[:],
                vh4[gb][:].rearrange("p (c f) -> p c f", c=NKC)
                    [:, j, (hp % 4) * 128:(hp % 4) * 128 + 128],
                pt[:, ipp * NKC * 512 + j * 512: ipp * NKC * 512 + (j + 1) * 512],
                start=(j == 0), stop=(j == NKC - 1)),
                "pe1" if j == NKC - 1 else None, 1)
        io = len(iters) - 1
        opar = io % 2
        sc.wait("vector", "pe1")
        sc.wait("vector", "ev1")
        sc.wait("vector", f"st{opar}", st_marks[io - 2])
        sc.add("vector", lambda e, opar=opar: e.tensor_tensor(
            out=ao_t[:, opar * 512:(opar + 1) * 512], in0=p_o[:],
            in1=lb_f[:, opar * 512:(opar + 1) * 512], op=ALU.mult), None)
        sc.add("vector", lambda e: e.drain(), "dv0", 1)
        sc.wait("gpsimd", "dv0")
        sc.add("gpsimd", lambda e, hp=hp, qtp=qtp, opar=opar: e.dma_start(
            out=ao_f[hp * 128:(hp + 1) * 128, qtp * 512:(qtp + 1) * 512],
            in_=ao_t[:, opar * 512:(opar + 1) * 512]), f"st{opar}", 16)
        while prep_i[0] < len(prep_jobs):
            emit_prep_job()
        sc.emit()
    mk.barrier()


def ln_row(sc, mk, row_ap, out_ap, sq, st1, st2, st3, n_free):
    inv_n = 1.0 / n_free
    sc.add("scalar", lambda e: e.activation(out=sq[:], in_=row_ap, func=AF.Square,
                                            accum_out=st1[:]), "sc0", 1)
    sc.wait("vector", "sc0")
    sc.add("vector", lambda e: e.reduce_sum(out=st2[:], in_=row_ap, axis=AX.X,
                                            negate=True), None)
    sc.on("vector", lambda e: e.drain())
    sc.on("vector", lambda e: e.tensor_scalar_mul(out=st2[:], in0=st2[:], scalar1=inv_n))
    sc.on("vector", lambda e: e.drain())
    sc.on("vector", lambda e: e.tensor_mul(out=st3[:], in0=st2[:], in1=st2[:]))
    sc.on("vector", lambda e: e.drain())
    sc.on("vector", lambda e: e.tensor_scalar(out=st1[:], in0=st1[:], scalar1=inv_n,
                                              scalar2=None, op0=ALU.mult))
    sc.on("vector", lambda e: e.drain())
    sc.on("vector", lambda e: e.tensor_sub(out=st1[:], in0=st1[:], in1=st3[:]))
    sc.on("vector", lambda e: e.drain())
    sc.on("vector", lambda e: e.tensor_scalar_add(out=st1[:], in0=st1[:], scalar1=EPS))
    sc.add("vector", lambda e: e.drain(), "dv1", 1)
    sc.wait("scalar", "dv1")
    sc.add("scalar", lambda e: e.activation(out=st3[:], in_=st1[:], func=AF.Sqrt), None)
    sc.on("scalar", lambda e: e.drain())
    sc.add("scalar", lambda e: e.nop(), "sc0", 1)
    sc.wait("vector", "sc0")
    sc.on("vector", lambda e: e.reciprocal(out=st3[:], in_=st3[:]))
    sc.on("vector", lambda e: e.drain())
    sc.add("vector", lambda e: e.tensor_scalar(out=out_ap, in0=row_ap, scalar1=st2[:],
                                               scalar2=st3[:], op0=ALU.add,
                                               op1=ALU.mult), None)
    sc.add("vector", lambda e: e.drain(), "dv0", 1)



def ph_wo0ln(mk):
    mk.ein("woT", [D, D], F32)
    mk.ein("BoT", [D, RANK], F32)
    mk.ein("AoT", [RANK, D], F32)
    mk.ein("bo_row", [1, D], F32)
    mk.ein("x_tm", [T, D], F32)
    mk.ein("c_ones_bf", [128, 512], BF16)
    mk.ein("c_eye_f32", [128, 128], F32)
    emit_oproj_ln(mk, "a3", mk.dram("ao_f", [D, T], BF16), "woT",
                  ("BoT", "AoT", "bo_row"), mk.t["x_tm"],
                  mk.dram("x1t", [T, D], F32), mk.dram("x1f", [D, T], F32))


def emit_oproj_ln(mk, tag, src_fm, w_name, lora, res_tm, xt, xf):
    nc = mk.nc
    x1t, x1f = xt, xf

    es = ExitStack()
    sb = lambda n_, sh, dt: es.enter_context(nc.sbuf_tensor(n_, sh, dt))
    pst = lambda n_, sh, dt: es.enter_context(nc.psum_tensor(n_, sh, dt))
    with es:
        afm = sb(f"{tag}_afm", [128, NB * T], BF16)
        wt = sb(f"{tag}_wt", [128, NB * D], BF16)
        ones = sb(f"{tag}_ones", [128, 512], BF16)
        eye_f = sb(f"{tag}_eyef", [128, 128], F32)
        xres = sb(f"{tag}_xres", [128, 2 * D], F32)
        prerow = sb(f"{tag}_pre", [128, 2 * D], F32)
        lnrow = sb(f"{tag}_ln", [128, 2 * D], F32)
        sq = sb(f"{tag}_sq", [128, D], F32)
        st1 = sb(f"{tag}_st1", [128, 1], F32)
        st2 = sb(f"{tag}_st2", [128, 1], F32)
        st3 = sb(f"{tag}_st3", [128, 1], F32)
        colb = sb(f"{tag}_colb", [128, D], F32)
        p_mm = [pst(f"{tag}_pmm{i}", [128, 512], F32) for i in range(3)]
        p_tr = [pst(f"{tag}_ptr{i}", [128, 128], F32) for i in range(2)]
        if lora is not None:
            aT = sb(f"{tag}_aT", [RANK, D], BF16)
            bT = sb(f"{tag}_bT", [128, NB * RANK], BF16)
            zz = sb(f"{tag}_z", [RANK, T], BF16)
            brow = sb(f"{tag}_brow", [1, D], BF16)
            p_z = [pst(f"{tag}_pz{i}", [RANK, 512], F32) for i in range(2)]

        sc = Sched(mk)
        for dc in range(NB):
            sc.add("sync", lambda e, dc=dc: e.dma_start(
                out=afm[:, dc * T:(dc + 1) * T],
                in_=src_fm[dc * 128:(dc + 1) * 128, :]), "qS", 16)
        sc.add("sync", lambda e: e.dma_start(out=ones[:], in_=mk.t["c_ones_bf"]), "qS", 16)
        sc.add("sync", lambda e: e.dma_start(out=eye_f[:], in_=mk.t["c_eye_f32"]), "qS", 16)
        for dc in range(NB):
            sc.add("gpsimd", lambda e, dc=dc: e.dma_start(
                out=wt[:, dc * D:(dc + 1) * D],
                in_=mk.t[w_name][dc * 128:(dc + 1) * 128, :]), "qG", 16)
        if lora is not None:
            bT_name, aT_name, brow_name = lora
            sc.add("gpsimd", lambda e: e.dma_start(out=aT[0:RANK, :], in_=mk.t[aT_name]), "qG", 16)
            sc.add("gpsimd", lambda e: e.dma_start(
                out=bT[:].rearrange("p (dc r) -> p dc r", dc=NB),
                in_=mk.t[bT_name].rearrange("(dc p) r -> p dc r", p=128)), "qG", 16)
            sc.add("gpsimd", lambda e: e.dma_start(out=brow[0:1, :], in_=mk.t[brow_name]), "qG", 16)
        lds = mk.val("qS")
        ldg = mk.val("qG")
        sc.wait("tensor", "qS", lds)
        sc.wait("tensor", "qG", ldg)

        if lora is not None:
            ev_base = mk.val("ev0")
            for tt in range(4):
                if tt >= 2:
                    sc.wait("tensor", "ev0", ev_base + tt - 1)

                def mmz(e, tt=tt):
                    for dc in range(NB):
                        last = e.matmul(p_z[tt % 2][:],
                                        bT[:, dc * RANK:(dc + 1) * RANK],
                                        afm[:, dc * T + tt * 512: dc * T + (tt + 1) * 512],
                                        start=(dc == 0), stop=(dc == NB - 1))
                    return last
                sc.add("tensor", mmz, "pe0", 1)
                sc.wait("scalar", "pe0")
                sc.add("scalar", lambda e, tt=tt: e.activation(
                    out=zz[0:RANK, tt * 512:(tt + 1) * 512], in_=p_z[tt % 2][:],
                    func=AF.Copy, scale=LORA_SCALE), "ev0", 1)

        xld = {}
        st_x1t = {}
        st_x1f = {}
        ttmark = {}

        def mmw(e, tc, ot):
            last = None
            for dc in range(NB):
                last = e.matmul(p_mm[(tc * 4 + ot) % 3][:],
                                afm[:, dc * T + tc * 128: dc * T + (tc + 1) * 128],
                                wt[:, dc * D + ot * 512: dc * D + (ot + 1) * 512],
                                start=(dc == 0),
                                stop=(lora is None and dc == NB - 1))
            if lora is not None:
                e.matmul(p_mm[(tc * 4 + ot) % 3][:],
                         zz[0:RANK, tc * 128:(tc + 1) * 128],
                         aT[0:RANK, ot * 512:(ot + 1) * 512],
                         start=False, stop=False)
                last = e.matmul(p_mm[(tc * 4 + ot) % 3][:], ones[0:1, 0:128],
                                brow[0:1, ot * 512:(ot + 1) * 512],
                                start=False, stop=True)
            return last

        tadd_mark = {}
        row_mark = {}
        for tc in range(NT):
            par = tc % 2
            lch = f"ch{2 + par}"
            if tc >= 2:
                sc.wait("sync", "dv0", row_mark[tc - 2])
            sc.add("sync", lambda e, tc=tc, par=par: e.dma_start(
                out=xres[:, par * D:(par + 1) * D],
                in_=res_tm[tc * 128:(tc + 1) * 128, :]), lch, 16)
            xld[tc] = mk.val(lch)
            for ot in range(4):
                i = tc * 4 + ot
                if i >= 3:
                    sc.wait("tensor", "dv0", tadd_mark[i - 3])
                sc.add("tensor", lambda e, tc=tc, ot=ot: mmw(e, tc, ot), "pe0", 1)
                sc.wait("vector", "pe0")
                sc.wait("vector", lch, xld[tc])
                sc.add("vector", lambda e, tc=tc, ot=ot, par=par, i=i: e.tensor_add(
                    out=prerow[:, par * D + ot * 512: par * D + (ot + 1) * 512],
                    in0=p_mm[i % 3][:],
                    in1=xres[:, par * D + ot * 512: par * D + (ot + 1) * 512]), "dv0", 1)
                tadd_mark[i] = mk.val("dv0")
            row_mark[tc] = mk.val("dv0")
            sc.on("vector", lambda e: e.drain())
            sc.wait("scalar", "dv0")
            ln_row(sc, mk, prerow[:, par * D:(par + 1) * D],
                   lnrow[:, par * D:(par + 1) * D], sq, st1, st2, st3, D)
            och = f"ch{4 + par}"
            sc.wait("scalar", "dv0")
            sc.add("scalar", lambda e, tc=tc, par=par, och=och: e.dma_start(
                out=x1t[tc * 128:(tc + 1) * 128, :],
                in_=lnrow[:, par * D:(par + 1) * D]), och, 16)
            st_x1t[tc] = mk.val(och)
            if tc >= 1:
                sc.wait("scalar", "ch6", st_x1f[tc - 1])
            ev1_base = mk.val("ev1")
            for dc in range(NB):
                if ev1_base + dc - 1 > 0:
                    sc.wait("tensor", "ev1", ev1_base + dc - 1)
                sc.wait("tensor", "dv0")
                sc.add("tensor", lambda e, tc=tc, par=par, dc=dc: e.transpose(
                    p_tr[dc % 2][:],
                    lnrow[:, par * D + dc * 128: par * D + (dc + 1) * 128],
                    eye_f[:]), "pe1", 1)
                sc.wait("scalar", "pe1")
                sc.add("scalar", lambda e, tc=tc, par=par, dc=dc: e.activation(
                    out=colb[:, dc * 128:(dc + 1) * 128],
                    in_=p_tr[dc % 2][:], func=AF.Copy), "ev1", 1)
            sc.on("scalar", lambda e: e.drain())
            sc.add("scalar", lambda e, tc=tc: e.dma_start(
                out=x1f.rearrange("(fc p) t -> p fc t", p=128)[:, :, tc * 128:(tc + 1) * 128],
                in_=colb[:].rearrange("p (fc q) -> p fc q", fc=NB)),
                "ch6", 16)
            st_x1f[tc] = mk.val("ch6")
            if tc >= 1:
                sc.wait("vector", f"ch{4 + (tc - 1) % 2}", st_x1t[tc - 1])

        sc.emit()
    mk.barrier()



def ph_moeprep(mk):
    nc = mk.nc
    mk.ein("wgTe", [NEXP, D, FMOE], F32)
    mk.ein("wuTe", [NEXP, D, FMOE], F32)
    mk.ein("wdTe", [NEXP, FMOE, D], F32)
    mk.ein("shgT", [D, FSH], F32)
    mk.ein("shuT", [D, FSH], F32)
    mk.ein("shdT", [FSH, D], F32)
    w8g = mk.dram("w8g", [NEXP, D, FMOE], BF16)
    w8u = mk.dram("w8u", [NEXP, D, FMOE], BF16)
    w8d = mk.dram("w8d", [NEXP, FMOE, D], BF16)
    w8sg = mk.dram("w8sg", [D, FSH], BF16)
    w8su = mk.dram("w8su", [D, FSH], BF16)
    w8sd = mk.dram("w8sd", [FSH, D], BF16)
    sc = Sched(mk)
    paced = hasattr(mk, "exch_mark") and "v0a" in mk.exch_mark
    jobs = []
    for src, dst in (("wgTe", w8g), ("wuTe", w8u), ("wdTe", w8d)):
        for n in range(NEXP):
            rows = dst.shape[1]
            for r0 in range(0, rows, 1024):
                r1 = min(rows, r0 + 1024)
                jobs.append((dst[n][r0:r1], mk.t[src][n][r0:r1]))
    for src, dst in (("shgT", w8sg), ("shuT", w8su), ("shdT", w8sd)):
        rows = dst.shape[0]
        for r0 in range(0, rows, 1024):
            r1 = min(rows, r0 + 1024)
            jobs.append((dst[r0:r1], mk.t[src][r0:r1]))
    if paced:
        mk.moeprep_jobs = list(jobs)
        return
    base = mk.val("bg0")
    for i, (dst_ap, src_ap) in enumerate(jobs):
        if i >= 2:
            sc.wait("gpsimd", "bg0", base + 16 * (i - 1))
        sc.add("gpsimd", lambda e, dst_ap=dst_ap, src_ap=src_ap:
               e.dma_start(out=dst_ap, in_=src_ap), "bg0", 16)
    sc.emit()


def ph_moe(mk):
    nc = mk.nc
    x1f = mk.dram("x1f", [D, T], F32)
    x1t = mk.dram("x1t", [T, D], F32)
    mk.ein("gwT", [D, NEXP], F32)
    mk.ein("wgTe", [NEXP, D, FMOE], F32)
    mk.ein("wuTe", [NEXP, D, FMOE], F32)
    mk.ein("wdTe", [NEXP, FMOE, D], F32)
    mk.ein("shgT", [D, FSH], F32)
    mk.ein("shuT", [D, FSH], F32)
    mk.ein("shdT", [FSH, D], F32)
    mk.ein("c_ones_bf", [128, 512], BF16)
    mk.ein("c_eye_f32", [128, 128], F32)
    x2t = mk.dram("x2t", [T, D], F32)
    x2f = mk.dram("x2f", [D, T], F32)

    es0 = ExitStack()
    sb0 = lambda n_, sh, dt: es0.enter_context(nc.sbuf_tensor(n_, sh, dt))
    with es0:
        wmat = sb0("b_wmat", [128, NT * NEXP], F32)
        acc = sb0("b_acc", [128, 4 * D], F32)
        eye_f = sb0("b_eyef", [128, 128], F32)

        esR = ExitStack()
        sbR = lambda n_, sh, dt: esR.enter_context(nc.sbuf_tensor(n_, sh, dt))
        with esR:
            gw_sb = sbR("b_gw", [128, NB * NEXP], F32)
            xcol = sbR("b_xcol", [128, 2 * D], F32)
            sm = sbR("b_sm", [128, NEXP], F32)
            g4 = sbR("b_g4", [128, 4], F32)
            pr2 = sbR("b_pr2", [128, 2], F32)
            pr2b = sbR("b_pr2b", [128, 2], F32)
            s1 = sbR("b_s1", [128, 1], F32)
            s2 = sbR("b_s2", [128, 1], F32)
            s3 = sbR("b_s3", [128, 1], F32)
            msk = sbR("b_msk", [128, NEXP], F32)
            p_log = esR.enter_context(nc.psum_tensor("b_plog", [128, NEXP], F32))

            sc = Sched(mk)
            sc.add("sync", lambda e: e.dma_start(
                out=gw_sb[:].rearrange("p (dc x) -> p dc x", dc=NB),
                in_=mk.t["gwT"].rearrange("(dc p) x -> p dc x", p=128)), "qS", 16)
            sc.add("sync", lambda e: e.dma_start(out=eye_f[:], in_=mk.t["c_eye_f32"]),
                   "qS", 16)
            xc_mark = {}
            dvR = {}
            gw_mark = mk.val("qS")
            sc.wait("tensor", "qS", gw_mark)
            for tc in range(NT):
                par = tc % 2
                lch = f"ch{2 + par}"
                if tc >= 2:
                    sc.wait("sync", "pe0", dvR[tc - 2])
                sc.add("sync", lambda e, tc=tc, par=par: e.dma_start(
                    out=xcol[:, par * D:(par + 1) * D].rearrange("p (dc q) -> p dc q", dc=NB),
                    in_=x1f.rearrange("(dc p) t -> p dc t", p=128)[:, :, tc * 128:(tc + 1) * 128]),
                    lch, 16)
                xc_mark[tc] = mk.val(lch)
                sc.wait("tensor", lch, xc_mark[tc])
                if tc >= 1:
                    sc.wait("tensor", "dv1")

                def mml(e, tc=tc, par=par):
                    for dc in range(NB):
                        last = e.matmul(p_log[:],
                                        xcol[:, par * D + dc * 128: par * D + (dc + 1) * 128],
                                        gw_sb[:, dc * NEXP:(dc + 1) * NEXP],
                                        start=(dc == 0), stop=(dc == NB - 1))
                    return last
                sc.add("tensor", mml, "pe0", 1)
                dvR[tc] = mk.val("pe0")
                sc.wait("vector", "pe0")
                sc.add("vector", lambda e: e.reduce_max(out=s1[:], in_=p_log[:],
                                                        axis=AX.X, negate=True), None)
                sc.add("vector", lambda e: e.drain(), "dv0", 1)
                sc.wait("scalar", "dv0")
                sc.add("scalar", lambda e: e.activation(out=sm[:], in_=p_log[:],
                                                        func=AF.Exp, bias=s1[:],
                                                        accum_out=s2[:]), None)
                sc.on("scalar", lambda e: e.drain())
                sc.add("scalar", lambda e: e.nop(), "sc0", 1)
                sc.wait("vector", "sc0")
                sc.add("vector", lambda e: e.drain(), "dv1", 1)
                V = lambda f: (sc.on("vector", f), sc.on("vector", lambda e: e.drain()))
                V(lambda e: e.reciprocal(out=s3[:], in_=s2[:]))
                V(lambda e: e.tensor_scalar_mul(out=sm[:], in0=sm[:], scalar1=s3[:]))
                smv = sm[:].rearrange("p (g u) -> p g u", g=4)
                V(lambda e: e.tensor_tensor(out=g4[:], in0=smv[:, :, 0], in1=smv[:, :, 1],
                                            op=ALU.max))
                g4v = g4[:].rearrange("p (u v) -> p u v", u=2)
                V(lambda e: e.tensor_tensor(out=pr2[:], in0=g4v[:, :, 0], in1=g4v[:, :, 1],
                                            op=ALU.max))
                V(lambda e: e.tensor_tensor(out=pr2b[:], in0=g4v[:, :, 0], in1=g4v[:, :, 1],
                                            op=ALU.min))
                V(lambda e: e.tensor_tensor(out=s1[:], in0=pr2[:, 0:1], in1=pr2[:, 1:2],
                                            op=ALU.min))
                V(lambda e: e.tensor_tensor(out=s2[:], in0=pr2b[:, 0:1], in1=pr2b[:, 1:2],
                                            op=ALU.max))
                V(lambda e: e.tensor_tensor(out=s1[:], in0=s1[:], in1=s2[:], op=ALU.max))
                V(lambda e: e.tensor_single_scalar(out=g4[:], in_=g4[:], scalar=s1[:],
                                                   op=ALU.is_ge))
                V(lambda e: e.tensor_tensor(
                    out=smv, in0=smv,
                    in1=g4[:].unsqueeze(2).broadcast_to([128, 4, 2]), op=ALU.mult))
                V(lambda e: e.reduce_max(out=s1[:], in_=sm[:], axis=AX.X))
                V(lambda e: e.tensor_single_scalar(out=msk[:], in_=sm[:], scalar=s1[:],
                                                   op=ALU.is_lt))
                V(lambda e: e.tensor_tensor(out=msk[:], in0=sm[:], in1=msk[:], op=ALU.mult))
                V(lambda e: e.reduce_max(out=s2[:], in_=msk[:], axis=AX.X))
                V(lambda e: e.tensor_single_scalar(out=msk[:], in_=sm[:], scalar=s2[:],
                                                   op=ALU.is_ge))
                sc.add("vector", lambda e, tc=tc: e.tensor_tensor(
                    out=wmat[:, tc * NEXP:(tc + 1) * NEXP], in0=sm[:], in1=msk[:],
                    op=ALU.mult), None)
                sc.on("vector", lambda e: e.drain())
            sc.emit()
        mk.barrier()

        for tt in range(4):
            esE = ExitStack()
            sbE = lambda n_, sh, dt: esE.enter_context(nc.sbuf_tensor(n_, sh, dt))
            with esE:
                xtt = sbE(f"bE_x{tt}", [128, NB * 512], BF16)
                wbuf = [sbE(f"bE_w{tt}_{i}", [128, NB * FMOE], BF16) for i in range(2)]
                g_sb = sbE(f"bE_g{tt}", [128, 22 * 512], BF16)
                gated = sbE(f"bE_gt{tt}", [128, 22 * 512], BF16)
                u_t = sbE(f"bE_u{tt}", [128, 2 * 512], BF16)
                p_g = [esE.enter_context(nc.psum_tensor(f"bE_pg{tt}_{i}", [128, 512], F32))
                       for i in range(3)]
                p_d = [esE.enter_context(nc.psum_tensor(f"bE_pd{tt}_{i}", [128, 512], F32))
                       for i in range(3)]

                sc = Sched(mk)
                for dc in range(NB):
                    sc.add("gpsimd", lambda e, dc=dc, tt=tt: e.dma_start(
                        out=xtt[:, dc * 512:(dc + 1) * 512],
                        in_=x1f[dc * 128:(dc + 1) * 128, tt * 512:(tt + 1) * 512]),
                        "qG", 16)
                x_mark = mk.val("qG")
                sc.wait("tensor", "qG", x_mark)
                sc.wait("gpsimd", "bg0", mk.val("bg0"))

                buf_free = [0, 0]
                pg_last = [None, None, None]
                pd_last = [None, None, None]
                u_last = [None, None]
                gidx = [0]

                def load_stage(buf, src_slabs):
                    wch = f"wb{buf}"
                    sc.wait("gpsimd", "pe1", buf_free[buf])
                    for nch, chw, src in src_slabs:
                        sc.add("gpsimd", lambda e, nch=nch, chw=chw, src=src, buf=buf:
                               e.dma_start(
                                   out=wbuf[buf][:, 0:nch * chw]
                                       .rearrange("p (c w) -> p c w", c=nch),
                                   in_=src), wch, 16)
                    return (wch, mk.val(wch))

                def wg_slabs(w, n):
                    w8 = {"wgTe": "w8g", "wuTe": "w8u"}[w]
                    return [(NB, FMOE,
                             mk.t[w8][n].rearrange("(dc p) f -> p dc f", p=128))]

                def wd_slabs(n):
                    return [(NFC, D,
                             mk.t["w8d"][n].rearrange("(fc p) h -> p fc h", p=128))]

                def sh_slabs(w, half):
                    w8 = {"shgT": "w8sg", "shuT": "w8su"}[w]
                    return [(NB, FMOE,
                             mk.t[w8].rearrange("(dc p) f -> p dc f", p=128)
                                 [:, :, half * FMOE:(half + 1) * FMOE])]

                def shd_slabs(half):
                    return [(NFC, D,
                             mk.t["w8sd"][half * FMOE:(half + 1) * FMOE]
                                 .rearrange("(fc p) h -> p fc h", p=128))]

                def gate_stage(buf, mark, nfc, dst, func):
                    sc.wait("tensor", mark[0], mark[1])
                    for fc in range(nfc):
                        if pg_last[fc % 3] is not None:
                            sc.wait("tensor", *pg_last[fc % 3])

                        def mmg(e, buf=buf, fc=fc):
                            boff = (fc % NFC) * 128 if nfc == NFC else 0
                            for dc in range(NB):
                                last = e.matmul(
                                    p_g[fc % 3][:],
                                    wbuf[buf][:, dc * FMOE + (fc % NFC) * 128: dc * FMOE + (fc % NFC) * 128 + 128]
                                    if nfc == NFC else
                                    wbuf[buf][:, dc * FMOE + fc * 128: dc * FMOE + fc * 128 + 128],
                                    xtt[:, dc * 512:(dc + 1) * 512],
                                    start=(dc == 0), stop=(dc == NB - 1))
                            return last
                        sc.add("tensor", mmg, "pe1", 1)
                        sc.wait("scalar", "pe1")
                        sc.add("scalar", lambda e, fc=fc, dst=dst, func=func: e.activation(
                            out=dst[:, fc * 512:(fc + 1) * 512], in_=p_g[fc % 3][:],
                            func=func), "ev0", 1)
                        pg_last[fc % 3] = ("ev0", mk.val("ev0"))
                    buf_free[buf] = mk.val("pe1")
                    return mk.val("ev0")

                def up_fuse_stage(buf, mark, nfc, goff):
                    sc.wait("tensor", mark[0], mark[1])
                    for fc in range(nfc):
                        if pg_last[fc % 3] is not None:
                            sc.wait("tensor", *pg_last[fc % 3])
                        if u_last[fc % 2] is not None:
                            sc.wait("tensor", *u_last[fc % 2])

                        def mmu(e, buf=buf, fc=fc):
                            for dc in range(NB):
                                last = e.matmul(
                                    p_g[fc % 3][:],
                                    wbuf[buf][:, dc * FMOE + fc * 128: dc * FMOE + fc * 128 + 128],
                                    xtt[:, dc * 512:(dc + 1) * 512],
                                    start=(dc == 0), stop=(dc == NB - 1))
                            return last
                        sc.add("tensor", mmu, "pe1", 1)
                        sc.wait("scalar", "pe1")
                        sc.add("scalar", lambda e, fc=fc: e.activation(
                            out=u_t[:, (fc % 2) * 512:((fc % 2) + 1) * 512],
                            in_=p_g[fc % 3][:], func=AF.Copy), "ev0", 1)
                        sc.wait("vector", "ev0")
                        sc.add("vector", lambda e, fc=fc, goff=goff: e.tensor_tensor(
                            out=gated[:, (goff + fc) * 512:(goff + fc + 1) * 512],
                            in0=g_sb[:, (goff + fc) * 512:(goff + fc + 1) * 512],
                            in1=u_t[:, (fc % 2) * 512:((fc % 2) + 1) * 512],
                            op=ALU.mult), "dv0", 1)
                        pg_last[fc % 3] = ("ev0", mk.val("ev0"))
                        u_last[fc % 2] = ("dv0", mk.val("dv0"))
                    buf_free[buf] = mk.val("pe1")
                    return mk.val("dv0")

                def down_stage(bufs, mark, nfc, goff, n, first):
                    sc.wait("tensor", mark[0], mark[1])
                    sc.wait("tensor", "dv0")
                    i = 0
                    for tc4 in range(4):
                        tcg = tt * 4 + tc4
                        for ht in range(4):
                            if pd_last[i % 3] is not None:
                                sc.wait("tensor", *pd_last[i % 3])

                            def mmd(e, tc4=tc4, ht=ht, i=i, bufs=bufs, nfc=nfc, goff=goff):
                                for fc in range(nfc):
                                    buf = bufs[fc // NFC]
                                    last = e.matmul(
                                        p_d[i % 3][:],
                                        gated[:, (goff + fc) * 512 + tc4 * 128:
                                              (goff + fc) * 512 + tc4 * 128 + 128],
                                        wbuf[buf][:, (fc % NFC) * D + ht * 512:
                                                  (fc % NFC) * D + ht * 512 + 512],
                                        start=(fc == 0), stop=(fc == nfc - 1))
                                return last
                            sc.add("tensor", mmd, "pe1", 1)
                            sc.wait("vector", "pe1")
                            if n is None:
                                sc.add("vector", lambda e, tc4=tc4, ht=ht, i=i: e.scalar_tensor_tensor(
                                    out=acc[:, tc4 * D + ht * 512: tc4 * D + (ht + 1) * 512],
                                    in0=p_d[i % 3][:], scalar=1.0,
                                    in1=acc[:, tc4 * D + ht * 512: tc4 * D + (ht + 1) * 512],
                                    op0=ALU.mult, op1=ALU.add), "dv1", 1)
                            elif first:
                                sc.add("vector", lambda e, tc4=tc4, ht=ht, i=i, tcg=tcg, n=n:
                                       e.tensor_scalar_mul(
                                           out=acc[:, tc4 * D + ht * 512: tc4 * D + (ht + 1) * 512],
                                           in0=p_d[i % 3][:],
                                           scalar1=wmat[:, tcg * NEXP + n: tcg * NEXP + n + 1]),
                                       "dv1", 1)
                            else:
                                sc.add("vector", lambda e, tc4=tc4, ht=ht, i=i, tcg=tcg, n=n:
                                       e.scalar_tensor_tensor(
                                           out=acc[:, tc4 * D + ht * 512: tc4 * D + (ht + 1) * 512],
                                           in0=p_d[i % 3][:],
                                           scalar=wmat[:, tcg * NEXP + n: tcg * NEXP + n + 1],
                                           in1=acc[:, tc4 * D + ht * 512: tc4 * D + (ht + 1) * 512],
                                           op0=ALU.mult, op1=ALU.add), "dv1", 1)
                            pd_last[i % 3] = ("dv1", mk.val("dv1"))
                            i += 1
                    for b in bufs:
                        buf_free[b] = mk.val("pe1")

                for n in range(NEXP):
                    mkg = load_stage(n % 2, wg_slabs("wgTe", n))
                    gate_stage(n % 2, mkg, NFC, g_sb, AF.Silu)
                    mku = load_stage((n + 1) % 2, wg_slabs("wuTe", n))
                    up_fuse_stage((n + 1) % 2, mku, NFC, 0)
                    mkd = load_stage(n % 2, wd_slabs(n))
                    down_stage([n % 2], mkd, NFC, 0, n, first=(n == 0))
                m0 = load_stage(0, sh_slabs("shgT", 0))
                m1 = load_stage(1, sh_slabs("shgT", 1))
                gate_stage(0, m0, NFC, g_sb, AF.Silu)
                sc.wait("tensor", m1[0], m1[1])
                for fc in range(NFC):
                    if pg_last[fc % 3] is not None:
                        sc.wait("tensor", *pg_last[fc % 3])

                    def mmg2(e, fc=fc):
                        for dc in range(NB):
                            last = e.matmul(
                                p_g[fc % 3][:],
                                wbuf[1][:, dc * FMOE + fc * 128: dc * FMOE + fc * 128 + 128],
                                xtt[:, dc * 512:(dc + 1) * 512],
                                start=(dc == 0), stop=(dc == NB - 1))
                        return last
                    sc.add("tensor", mmg2, "pe1", 1)
                    sc.wait("scalar", "pe1")
                    sc.add("scalar", lambda e, fc=fc: e.activation(
                        out=g_sb[:, (NFC + fc) * 512:(NFC + fc + 1) * 512],
                        in_=p_g[fc % 3][:], func=AF.Silu), "ev0", 1)
                    pg_last[fc % 3] = ("ev0", mk.val("ev0"))
                buf_free[1] = mk.val("pe1")
                m0 = load_stage(0, sh_slabs("shuT", 0))
                m1 = load_stage(1, sh_slabs("shuT", 1))
                up_fuse_stage(0, m0, NFC, 0)
                sc.wait("tensor", m1[0], m1[1])
                for fc in range(NFC):
                    if pg_last[fc % 3] is not None:
                        sc.wait("tensor", *pg_last[fc % 3])
                    if u_last[fc % 2] is not None:
                        sc.wait("tensor", *u_last[fc % 2])

                    def mmu2(e, fc=fc):
                        for dc in range(NB):
                            last = e.matmul(
                                p_g[fc % 3][:],
                                wbuf[1][:, dc * FMOE + fc * 128: dc * FMOE + fc * 128 + 128],
                                xtt[:, dc * 512:(dc + 1) * 512],
                                start=(dc == 0), stop=(dc == NB - 1))
                        return last
                    sc.add("tensor", mmu2, "pe1", 1)
                    sc.wait("scalar", "pe1")
                    sc.add("scalar", lambda e, fc=fc: e.activation(
                        out=u_t[:, (fc % 2) * 512:((fc % 2) + 1) * 512],
                        in_=p_g[fc % 3][:], func=AF.Copy), "ev0", 1)
                    sc.wait("vector", "ev0")
                    sc.add("vector", lambda e, fc=fc: e.tensor_tensor(
                        out=gated[:, (NFC + fc) * 512:(NFC + fc + 1) * 512],
                        in0=g_sb[:, (NFC + fc) * 512:(NFC + fc + 1) * 512],
                        in1=u_t[:, (fc % 2) * 512:((fc % 2) + 1) * 512],
                        op=ALU.mult), "dv0", 1)
                    pg_last[fc % 3] = ("ev0", mk.val("ev0"))
                    u_last[fc % 2] = ("dv0", mk.val("dv0"))
                buf_free[1] = mk.val("pe1")
                m0 = load_stage(0, shd_slabs(0))
                m1 = load_stage(1, shd_slabs(1))
                sc.wait("tensor", m1[0], m1[1])
                down_stage([0, 1], m0, 22, 0, None, first=False)
                sc.emit()
            mk.barrier()

            esL = ExitStack()
            sbL = lambda n_, sh, dt: esL.enter_context(nc.sbuf_tensor(n_, sh, dt))
            with esL:
                xres = sbL(f"bL_xres{tt}", [128, 2 * D], F32)
                lnrow = sbL(f"bL_ln{tt}", [128, 2 * D], F32)
                sq = sbL(f"bL_sq{tt}", [128, D], F32)
                s1 = sbL(f"bL_s1{tt}", [128, 1], F32)
                s2 = sbL(f"bL_s2{tt}", [128, 1], F32)
                s3 = sbL(f"bL_s3{tt}", [128, 1], F32)
                colb = sbL(f"bL_colb{tt}", [128, D], F32)
                p_tr = [esL.enter_context(nc.psum_tensor(f"bL_ptr{tt}_{i}", [128, 128], F32))
                        for i in range(2)]
                sc = Sched(mk)
                stf = {}
                stt_m = {}
                ev1_m = {}
                ln_mark = {}
                for tc4 in range(4):
                    tcg = tt * 4 + tc4
                    par = tc4 % 2
                    lch = f"ch{2 + par}"
                    if tc4 >= 2:
                        sc.wait("sync", "dv0", ln_mark[tc4 - 2])
                    sc.add("sync", lambda e, tcg=tcg, par=par: e.dma_start(
                        out=xres[:, par * D:(par + 1) * D],
                        in_=x1t[tcg * 128:(tcg + 1) * 128, :]), lch, 16)
                    sc.wait("vector", lch)
                    if tc4 >= 2:
                        sc.wait("vector", "ch6", stf[tc4 - 2])
                        sc.wait("vector", f"ch{4 + par}", stt_m[tc4 - 2])
                        sc.wait("vector", "ev1", ev1_m[tc4 - 2])
                    sc.add("vector", lambda e, tc4=tc4, par=par: e.tensor_add(
                        out=xres[:, par * D:(par + 1) * D],
                        in0=acc[:, tc4 * D:(tc4 + 1) * D],
                        in1=xres[:, par * D:(par + 1) * D]), "dv0", 1)
                    sc.on("vector", lambda e: e.drain())
                    sc.wait("scalar", "dv0")
                    ln_row(sc, mk, xres[:, par * D:(par + 1) * D],
                           lnrow[:, par * D:(par + 1) * D], sq, s1, s2, s3, D)
                    ln_mark[tc4] = mk.val("dv0")
                    sc.wait("scalar", "dv0")
                    sc.add("scalar", lambda e, tcg=tcg, par=par: e.dma_start(
                        out=x2t[tcg * 128:(tcg + 1) * 128, :],
                        in_=lnrow[:, par * D:(par + 1) * D]), f"ch{4 + par}", 16)
                    stt_m[tc4] = mk.val(f"ch{4 + par}")
                    if tc4 >= 1:
                        sc.wait("scalar", "ch6", stf[tc4 - 1])
                    ev1b = mk.val("ev1")
                    for dc in range(NB):
                        if ev1b + dc - 1 > 0:
                            sc.wait("tensor", "ev1", ev1b + dc - 1)
                        sc.wait("tensor", "dv0")
                        sc.add("tensor", lambda e, par=par, dc=dc: e.transpose(
                            p_tr[dc % 2][:],
                            lnrow[:, par * D + dc * 128: par * D + (dc + 1) * 128],
                            eye_f[:]), "pe0", 1)
                        sc.wait("scalar", "pe0")
                        sc.add("scalar", lambda e, dc=dc: e.activation(
                            out=colb[:, dc * 128:(dc + 1) * 128],
                            in_=p_tr[dc % 2][:], func=AF.Copy), "ev1", 1)
                    sc.on("scalar", lambda e: e.drain())
                    sc.add("scalar", lambda e, tcg=tcg: e.dma_start(
                        out=x2f.rearrange("(fc p) t -> p fc t", p=128)[:, :, tcg * 128:(tcg + 1) * 128],
                        in_=colb[:].rearrange("p (fc q) -> p fc q", fc=NB)), "ch6", 16)
                    stf[tc4] = mk.val("ch6")
                    ev1_m[tc4] = mk.val("ev1")
                sc.emit()
            mk.barrier()



def ph_qkv1(mk):
    nc = mk.nc
    x2f = mk.dram("x2f", [D, T], F32)
    mk.ein("wqkvT", [D, 3 * D], F32)
    for t_ in ("c_cq", "c_sq", "c_ck", "c_sk"):
        mk.ein(t_, [T, 32], F32)
    q1t = mk.dram("q1t", [T, D], BF16)
    k1t = mk.dram("k1t", [T, D], BF16)
    v1t = mk.dram("v1t", [T, D], BF16)
    SEC = 1536

    es = ExitStack()
    sb = lambda n_, sh, dt: es.enter_context(nc.sbuf_tensor(n_, sh, dt))
    with es:
        xfm = sb("c1_xfm", [128, NB * T], BF16)
        wsec = [sb(f"c1_w{i}", [128, NB * SEC], BF16) for i in range(2)]
        rows = sb("c1_rows", [128, 2 * SEC], BF16)
        rot = {t_: sb(f"c1_{t_}", [128, NT * 32], F32)
               for t_ in ("c_cq", "c_sq", "c_ck", "c_sk")}
        qa = sb("c1_qa", [128, 256], BF16)
        rh = sb("c1_rh", [128, 256], BF16)
        p_mm = [es.enter_context(nc.psum_tensor(f"c1_pmm{i}", [128, 512], F32))
                for i in range(3)]

        sc = Sched(mk)
        for dc in range(NB):
            sc.add("gpsimd", lambda e, dc=dc: e.dma_start(
                out=xfm[:, dc * T:(dc + 1) * T],
                in_=x2f[dc * 128:(dc + 1) * 128, :]), "qG", 16)
        x_mark = mk.val("qG")
        for t_ in ("c_cq", "c_sq", "c_ck", "c_sk"):
            sc.add("sync", lambda e, t_=t_: e.dma_start(
                out=rot[t_][:].rearrange("p (tc r) -> p tc r", tc=NT),
                in_=mk.t[t_].rearrange("(tc p) r -> p tc r", p=128)), "qS", 16)
        rot_mark = mk.val("qS")
        sc.wait("tensor", "qG", x_mark)
        sc.wait("vector", "qS", rot_mark)

        w_free = [0, 0]
        st_mark = {0: None, 1: None}
        dvm = {}
        evm = {}
        for sec in range(4):
            par = sec % 2
            wch = f"wb{par}"
            sc.wait("gpsimd", "pe0", w_free[par])
            for dc in range(NB):
                sc.add("gpsimd", lambda e, sec=sec, dc=dc, par=par: e.dma_start(
                    out=wsec[par][:, dc * SEC:(dc + 1) * SEC],
                    in_=mk.t["wqkvT"][dc * 128:(dc + 1) * 128, sec * SEC:(sec + 1) * SEC]),
                    wch, 16)
            sc.wait("tensor", wch)
            for tc in range(NT):
                rpar = tc % 2
                evb = mk.val("ev0")
                for ot in range(3):
                    i = sec * 48 + tc * 3 + ot
                    if i >= 3:
                        sc.wait("tensor", "ev0", evm[i - 3])
                    sc.add("tensor", lambda e, sec=sec, tc=tc, ot=ot, par=par, i=i: [
                        e.matmul(p_mm[i % 3][:],
                                 xfm[:, dc_ * T + tc * 128: dc_ * T + (tc + 1) * 128],
                                 wsec[par][:, dc_ * SEC + ot * 512: dc_ * SEC + (ot + 1) * 512],
                                 start=(dc_ == 0), stop=(dc_ == NB - 1))
                        for dc_ in range(NB)][-1], "pe0", 1)
                    sc.wait("scalar", "pe0")
                    if ot == 0 and st_mark[rpar] is not None:
                        sc.wait("scalar", f"ch{4 + rpar}", st_mark[rpar])
                    sc.add("scalar", lambda e, ot=ot, rpar=rpar, i=i: e.activation(
                        out=rows[:, rpar * SEC + ot * 512: rpar * SEC + (ot + 1) * 512],
                        in_=p_mm[i % 3][:], func=AF.Copy), "ev0", 1)
                    evm[i] = mk.val("ev0")
                sc.wait("vector", "ev0")
                rv = rows[:, rpar * SEC:(rpar + 1) * SEC].rearrange(
                    "p (h three hd) -> p h three hd", h=8, three=3)
                qav = qa[:].rearrange("p (h r) -> p h r", h=8)
                rhv = rh[:].rearrange("p (h r) -> p h r", h=8)
                for part, ct, st_ in ((0, "c_cq", "c_sq"), (1, "c_ck", "c_sk")):
                    cb = rot[ct][:, tc * 32:(tc + 1) * 32].unsqueeze(1).broadcast_to([128, 8, 32])
                    sb_ = rot[st_][:, tc * 32:(tc + 1) * 32].unsqueeze(1).broadcast_to([128, 8, 32])
                    src = rv[:, :, part, 0:32]
                    sc.on("vector", lambda e, src=src, cb=cb: e.tensor_tensor(
                        out=qav, in0=src, in1=cb, op=ALU.mult))
                    sc.on("vector", lambda e, src=src: e.tensor_scalar_mul(
                        out=rhv[:, :, 0:32:2], in0=src[:, :, 1:32:2], scalar1=-1.0))
                    sc.on("vector", lambda e, src=src: e.tensor_copy(
                        out=rhv[:, :, 1:32:2], in_=src[:, :, 0:32:2]))
                    sc.on("vector", lambda e: e.drain())
                    sc.on("vector", lambda e, sb_=sb_: e.tensor_tensor(
                        out=rhv, in0=rhv, in1=sb_, op=ALU.mult))
                    sc.on("vector", lambda e: e.drain())
                    sc.on("vector", lambda e, src=src: e.tensor_tensor(
                        out=src, in0=qav, in1=rhv, op=ALU.add))
                    sc.on("vector", lambda e: e.drain())
                sc.add("vector", lambda e: e.engine_nop(), "dv0", 1)
                dvm[(sec, tc)] = mk.val("dv0")
                sc.wait("scalar", "dv0", dvm[(sec, tc)])
                for three, dst in ((0, q1t), (1, k1t), (2, v1t)):
                    sc.add("scalar", lambda e, sec=sec, tc=tc, rpar=rpar, three=three, dst=dst:
                           e.dma_start(
                               out=dst[tc * 128:(tc + 1) * 128, sec * 512:(sec + 1) * 512],
                               in_=rows[:, rpar * SEC:(rpar + 1) * SEC].rearrange(
                                   "p (h three hd) -> p h three hd", h=8, three=3)[:, :, three, :]),
                           f"ch{4 + rpar}", 16)
                st_mark[rpar] = mk.val(f"ch{4 + rpar}")
            w_free[par] = mk.val("pe0")
        sc.emit()
    mk.barrier()



def ph_ret(mk):
    nc = mk.nc
    q1t = mk.dram("q1t", [T, D], BF16)
    k1t = mk.dram("k1t", [T, D], BF16)
    v1t = mk.dram("v1t", [T, D], BF16)
    k1a = mk.dram("k1a", [SEQ, D], BF16)
    v1a = mk.dram("v1a", [SEQ, D], BF16)
    for t_ in ("c_d0", "c_qdec", "c_kdec", "c_kdec_prev"):
        mk.ein(t_, [128, H1 * 128] if t_ == "c_d0" else [128, D], F32)
    mk.ein("c_sdec", [64, H1 * HD1], F32)
    mk.ein("c_halfpick", [128, 2], F32)
    mk.ein("c_eye_bf", [128, 128], BF16)
    ret_bf = mk.dram("ret_bf", [D, T], BF16)
    st4 = mk.dram("st4", [2, 32], F32)

    es = ExitStack()
    sb = lambda n_, sh, dt: es.enter_context(nc.sbuf_tensor(n_, sh, dt))
    with es:
        krow = sb("c3_k", [128, 2 * D], BF16)
        vrow = sb("c3_v", [128, 2 * D], BF16)
        qrow = sb("c3_q", [128, 2 * D], BF16)
        k1s = sb("c3_ks", [128, D], BF16)
        q1s = sb("c3_qs", [128, D], BF16)
        kfm = sb("c3_kfm", [64, H1 * 128], BF16)
        qfm = sb("c3_qfm", [64, H1 * 128], BF16)
        pt = [sb(f"c3_pt{i}", [128, 128], BF16) for i in range(2)]
        state = sb("c3_S", [64, H1 * HD1], F32)
        state_bf = sb("c3_Sb", [64, H1 * HD1], BF16)
        retc = sb("c3_retc", [128, 2 * NB * 128], BF16)
        cd0 = sb("c3_cd0", [128, H1 * 128], F32)
        cqd = sb("c3_cqd", [128, D], F32)
        ckd = sb("c3_ckd", [128, D], F32)
        ckp = sb("c3_ckp", [128, D], F32)
        csd = sb("c3_csd", [64, H1 * HD1], F32)
        hp = sb("c3_hp", [128, 2], F32)
        eye = sb("c3_eye", [128, 128], BF16)
        sums = sb("c3_sums", [128, 2 * NB], F32)
        sqs = sb("c3_sq", [128, D], F32)
        slab = sb("c3_slab", [128, D], BF16)
        stats_sb = sb("c3_stats", [2, 32], F32)
        p_tr = es.enter_context(nc.psum_tensor("c3_ptr", [64, 16 * 128], BF16))
        p_s = [es.enter_context(nc.psum_tensor(f"c3_ps{i}", [128, 128], F32))
               for i in range(2)]
        p_big = es.enter_context(nc.psum_tensor("c3_pbig", [64, 2048], F32))

        sc = Sched(mk)
        for dst, src in ((cd0, "c_d0"), (cqd, "c_qdec"), (ckd, "c_kdec"),
                         (ckp, "c_kdec_prev"), (csd, "c_sdec"), (hp, "c_halfpick"),
                         (eye, "c_eye_bf")):
            sc.add("sync", lambda e, dst=dst, src=src: e.dma_start(
                out=dst[0:dst.shape[0], :], in_=mk.t[src]), "qS", 16)
        const_mark = mk.val("qS")
        sc.wait("vector", "qS", const_mark)
        sc.on("vector", lambda e: e.memset(state[:], 0.0))
        sc.on("vector", lambda e: e.memset(state_bf[:], 0.0))
        sc.add("vector", lambda e: e.drain(), "dv0", 1)
        sc.wait("tensor", "qS", const_mark)

        ccm = getattr(mk, "exch_mark", {}).get("v1a")
        if ccm is not None:
            sc.wait("sync", "cc", ccm)
        ld_m = {0: None, 1: None}
        up_m = {}
        for c in range(NT):
            par = c % 2
            lch = f"ch{2 + par}"
            if c >= 2:
                sc.wait("sync", "dv0", up_m[c - 2])
            sc.add("sync", lambda e, c=c, par=par: e.dma_start(
                out=krow[:, par * D:(par + 1) * D], in_=k1a[c * 128:(c + 1) * 128, :]),
                lch, 16)
            sc.add("sync", lambda e, c=c, par=par: e.dma_start(
                out=vrow[:, par * D:(par + 1) * D], in_=v1a[c * 128:(c + 1) * 128, :]),
                lch, 16)
            ld = mk.val(lch)
            sc.wait("vector", lch, ld)
            sc.add("vector", lambda e, par=par: e.tensor_tensor(
                out=k1s[:], in0=krow[:, par * D:(par + 1) * D], in1=ckp[:],
                op=ALU.mult), None)
            sc.add("vector", lambda e: e.drain(), "dv1", 1)
            sc.wait("tensor", "dv1")
            sc.wait("tensor", "dv0")
            for h in range(H1):
                sc.add("tensor", lambda e, h=h, par=par: e.matmul(
                    p_big[0:64, h * HD1:(h + 1) * HD1],
                    k1s[:, h * HD1:(h + 1) * HD1],
                    vrow[:, par * D + h * HD1: par * D + (h + 1) * HD1],
                    start=True, stop=True), "pe0" if h == H1 - 1 else None, 1)
            sc.wait("vector", "pe0")
            sc.on("vector", lambda e: e.tensor_tensor(out=state[:], in0=state[:],
                                                      in1=csd[:], op=ALU.mult))
            sc.on("vector", lambda e: e.drain())
            sc.add("vector", lambda e: e.tensor_add(out=state[:], in0=state[:],
                                                    in1=p_big[0:64, :]), None)
            sc.on("vector", lambda e: e.drain())
            sc.add("vector", lambda e: e.tensor_copy(out=state_bf[:], in_=state[:]), None)
            sc.add("vector", lambda e: e.drain(), "dv0", 1)
            up_m[c] = mk.val("dv0")

        st_m = {0: None, 1: None}
        for c in range(NT):
            par = c % 2
            lch = f"ch{2 + par}"
            sc.wait("sync", "dv0", up_m[NT + c - 2])
            for buf, src in ((qrow, q1t), (krow, k1t), (vrow, v1t)):
                sc.add("sync", lambda e, c=c, par=par, buf=buf, src=src: e.dma_start(
                    out=buf[:, par * D:(par + 1) * D], in_=src[c * 128:(c + 1) * 128, :]),
                    lch, 16)
            ld = mk.val(lch)
            sc.wait("vector", lch, ld)
            sc.add("vector", lambda e, par=par: e.tensor_tensor(
                out=k1s[:], in0=krow[:, par * D:(par + 1) * D], in1=ckd[:],
                op=ALU.mult), None)
            sc.add("vector", lambda e, par=par: e.tensor_tensor(
                out=q1s[:], in0=qrow[:, par * D:(par + 1) * D], in1=cqd[:],
                op=ALU.mult), None)
            sc.add("vector", lambda e: e.drain(), "dv1", 1)
            sc.wait("tensor", "dv1")
            sc.wait("tensor", "sc1")
            for gi, (src_sb, dst_sb) in enumerate(
                    ((q1s, qfm), (q1s, qfm), (krow, kfm), (krow, kfm))):
                half = gi % 2
                for hh in range(16):
                    h = half * 16 + hh
                    off = par * D if src_sb is krow else 0
                    sc.add("tensor", lambda e, h=h, hh=hh, src_sb=src_sb, off=off: e.transpose(
                        p_tr[0:64, hh * 128:(hh + 1) * 128],
                        src_sb[:, off + h * HD1: off + (h + 1) * HD1],
                        eye[:]), "pe1" if hh == 15 else None, 1)
                sc.wait("scalar", "pe1")
                sc.add("scalar", lambda e, half=half, dst_sb=dst_sb: e.activation(
                    out=dst_sb[0:64, half * 2048:(half + 1) * 2048],
                    in_=p_tr[0:64, :], func=AF.Copy), "sc1", 1)
                sc.wait("tensor", "sc1")
            tr_done = mk.val("sc1")
            sc.wait("tensor", "sc1", tr_done)
            if st_m[par] is not None:
                sc.wait("scalar", f"ch{4 + par}", st_m[par])
            dvb = mk.val("dv1")
            for grp in range(2):
                sc.wait("tensor", "ev1")
                sc.wait("tensor", "dv0")
                for hh in range(16):
                    h = grp * 16 + hh
                    if hh >= 2:
                        sc.wait("tensor", "dv1", dvb + grp * 16 + hh - 1)
                    sc.add("tensor", lambda e, h=h: e.matmul(
                        p_s[h % 2][:],
                        kfm[0:64, h * 128:(h + 1) * 128],
                        qfm[0:64, h * 128:(h + 1) * 128],
                        start=True, stop=True), "pe1", 1)
                    sc.wait("vector", "pe1")
                    sc.add("vector", lambda e, h=h: e.tensor_tensor(
                        out=pt[h % 2][:], in0=p_s[h % 2][:],
                        in1=cd0[:, h * 128:(h + 1) * 128], op=ALU.mult), "dv1", 1)
                    sc.wait("tensor", "dv1")
                    sc.add("tensor", lambda e, h=h, hh=hh, par=par: [
                        e.matmul(p_big[0:64, hh * 128:(hh + 1) * 128],
                                 vrow[:, par * D + h * HD1: par * D + (h + 1) * HD1],
                                 pt[h % 2][:], start=True, stop=False),
                        e.matmul(p_big[0:64, hh * 128:(hh + 1) * 128],
                                 state_bf[0:64, h * HD1:(h + 1) * HD1],
                                 qfm[0:64, h * 128:(h + 1) * 128],
                                 start=False, stop=True)][-1], "pe1", 1)
                sc.wait("scalar", "pe1")
                rv = retc[:, par * NB * 128:(par + 1) * NB * 128].rearrange(
                    "p (fc q) -> p fc q", fc=NB)
                pv = p_big[0:64, :].rearrange("p (hh q) -> p hh q", hh=16)
                sc.add("scalar", lambda e, grp=grp, rv=rv, pv=pv: e.activation(
                    out=rv[0:64, grp * 8:(grp + 1) * 8, :], in_=pv[:, 0::2, :],
                    func=AF.Copy), None)
                sc.add("scalar", lambda e, grp=grp, rv=rv, pv=pv: e.activation(
                    out=rv[64:128, grp * 8:(grp + 1) * 8, :], in_=pv[:, 1::2, :],
                    func=AF.Copy), "ev1", 1)
            sc.on("scalar", lambda e: e.drain())
            sc.add("scalar", lambda e, c=c, par=par: e.dma_start(
                out=ret_bf.rearrange("(fc p) t -> p fc t", p=128)[:, :, c * 128:(c + 1) * 128],
                in_=retc[:, par * NB * 128:(par + 1) * NB * 128]
                    .rearrange("p (fc q) -> p fc q", fc=NB)), f"ch{4 + par}", 16)
            st_m[par] = mk.val(f"ch{4 + par}")
            sc.wait("tensor", "dv1")
            sc.wait("tensor", "dv0")
            sc.wait("tensor", "ev1")
            for h in range(H1):
                sc.add("tensor", lambda e, h=h, par=par: e.matmul(
                    p_big[0:64, h * HD1:(h + 1) * HD1],
                    k1s[:, h * HD1:(h + 1) * HD1],
                    vrow[:, par * D + h * HD1: par * D + (h + 1) * HD1],
                    start=True, stop=True), "pe0" if h == H1 - 1 else None, 1)
            sc.wait("vector", "pe0")
            sc.on("vector", lambda e: e.tensor_tensor(out=state[:], in0=state[:],
                                                      in1=csd[:], op=ALU.mult))
            sc.on("vector", lambda e: e.drain())
            sc.add("vector", lambda e: e.tensor_add(out=state[:], in0=state[:],
                                                    in1=p_big[0:64, :]), None)
            sc.on("vector", lambda e: e.drain())
            sc.add("vector", lambda e: e.tensor_copy(out=state_bf[:], in_=state[:]), None)
            sc.add("vector", lambda e: e.drain(), "dv0", 1)
            up_m[NT + c] = mk.val("dv0")

        sc.wait("sync", "ch4")
        sc.wait("sync", "ch5")
        for fc in range(NB):
            lch = f"ch{2 + fc % 2}"
            if fc >= 1:
                sc.wait("sync", "sc0")
            sc.add("sync", lambda e, fc=fc: e.dma_start(
                out=slab[:], in_=ret_bf[fc * 128:(fc + 1) * 128, :]), lch, 16)
            sc.wait("vector", lch)
            sc.add("vector", lambda e, fc=fc: e.reduce_sum(
                out=sums[:, fc:fc + 1], in_=slab[:], axis=AX.X), None)
            sc.add("vector", lambda e: e.drain(), "dv1", 1)
            sc.wait("scalar", "dv1")
            sc.add("scalar", lambda e, fc=fc: e.activation(
                out=sqs[:], in_=slab[:], func=AF.Square,
                accum_out=sums[:, NB + fc:NB + fc + 1]), None)
            sc.on("scalar", lambda e: e.drain())
            sc.add("scalar", lambda e: e.nop(), "sc0", 1)
        sc.wait("tensor", "sc0")
        sc.wait("tensor", "dv1")
        sc.add("tensor", lambda e: e.matmul(
            p_s[0][0:2, 0:16], hp[:, 0:2], sums[:, 0:NB], start=True, stop=True), None)
        sc.add("tensor", lambda e: e.matmul(
            p_s[0][0:2, 16:32], hp[:, 0:2], sums[:, NB:2 * NB], start=True, stop=True),
            "pe1", 1)
        sc.wait("scalar", "pe1")
        sc.add("scalar", lambda e: e.activation(out=stats_sb[0:2, :], in_=p_s[0][0:2, 0:32],
                                                func=AF.Copy), None)
        sc.on("scalar", lambda e: e.drain())
        sc.add("scalar", lambda e: e.dma_start(out=st4, in_=stats_sb[0:2, :]), "qA", 16)
        sc.emit()
    mk.barrier()



def ph_gw1(mk):
    nc = mk.nc
    ret_bf = mk.dram("ret_bf", [D, T], BF16)
    st4a = mk.dram("st4a", [4, 32], F32)
    x2f = mk.dram("x2f", [D, T], F32)
    x2t = mk.dram("x2t", [T, D], F32)
    mk.ein("gnw2", [2, 16], F32)
    mk.ein("gnb2", [2, 16], F32)
    mk.ein("wg1T", [D, D], F32)
    mk.ein("wo1T", [D, D], F32)
    mk.ein("c_hpT", [2, 128], F32)
    mk.ein("c_ones_bf", [128, 512], BF16)
    mk.ein("c_eye_f32", [128, 128], F32)
    gated_b = mk.dram("gated_b", [D, T], BF16)
    x3t = mk.dram("x3t", [T, D], F32)
    x3f = mk.dram("x3f", [D, T], F32)
    NTOK = float(SEQ * HD1)

    es = ExitStack()
    sb = lambda n_, sh, dt: es.enter_context(nc.sbuf_tensor(n_, sh, dt))
    with es:
        xfm = sb("c5_xfm", [128, NB * T], BF16)
        wgt = sb("c5_wgt", [128, NB * D], BF16)
        retslab = sb("c5_ret", [128, 2 * T], BF16)
        grow = sb("c5_g", [128, 2 * T], F32)
        t1 = sb("c5_t1", [128, T], F32)
        t2 = sb("c5_t2", [128, T], F32)
        spb = sb("c5_sp", [128, T], F32)
        yrow = sb("c5_y", [128, T], F32)
        gb = sb("c5_gb", [128, 2 * T], BF16)
        slo = sb("c5_slo", [2, 32], F32)
        shi = sb("c5_shi", [2, 32], F32)
        a2 = sb("c5_a2", [2, 16], F32)
        b2 = sb("c5_b2", [2, 16], F32)
        gw2 = sb("c5_gw2", [2, 16], F32)
        gb2 = sb("c5_gb2", [2, 16], F32)
        hpT = sb("c5_hpT", [2, 128], F32)
        acol = sb("c5_acol", [128, 16], F32)
        bcol = sb("c5_bcol", [128, 16], F32)
        p_g = [es.enter_context(nc.psum_tensor(f"c5_pg{i}", [128, 512], F32))
               for i in range(3)]
        p_ab = es.enter_context(nc.psum_tensor("c5_pab", [128, 32], F32))

        sc = Sched(mk)
        for dc in range(NB):
            sc.add("gpsimd", lambda e, dc=dc: e.dma_start(
                out=xfm[:, dc * T:(dc + 1) * T],
                in_=x2f[dc * 128:(dc + 1) * 128, :]), "qG", 16)
            sc.add("gpsimd", lambda e, dc=dc: e.dma_start(
                out=wgt[:, dc * D:(dc + 1) * D],
                in_=mk.t["wg1T"][dc * 128:(dc + 1) * 128, :]), "qG", 16)
        ldg = mk.val("qG")
        ccm = getattr(mk, "exch_mark", {}).get("st4a")
        if ccm is not None:
            sc.wait("sync", "cc", ccm)
        sc.add("sync", lambda e: e.dma_start(out=slo[0:2, :], in_=st4a[0:2, :]), "qS", 16)
        sc.add("sync", lambda e: e.dma_start(out=shi[0:2, :], in_=st4a[2:4, :]), "qS", 16)
        sc.add("sync", lambda e: e.dma_start(out=gw2[0:2, :], in_=mk.t["gnw2"]), "qS", 16)
        sc.add("sync", lambda e: e.dma_start(out=gb2[0:2, :], in_=mk.t["gnb2"]), "qS", 16)
        sc.add("sync", lambda e: e.dma_start(out=hpT[0:2, :], in_=mk.t["c_hpT"]), "qS", 16)
        lds = mk.val("qS")

        sc.wait("vector", "qS", lds)
        V = lambda f: (sc.on("vector", f), sc.on("vector", lambda e: e.drain()))
        V(lambda e: e.tensor_add(out=slo[0:2, :], in0=slo[0:2, :], in1=shi[0:2, :]))
        V(lambda e: e.tensor_scalar_mul(out=slo[0:2, :], in0=slo[0:2, :],
                                        scalar1=1.0 / NTOK))
        V(lambda e: e.tensor_mul(out=a2[0:2, :], in0=slo[0:2, 0:16], in1=slo[0:2, 0:16]))
        V(lambda e: e.tensor_sub(out=b2[0:2, :], in0=slo[0:2, 16:32], in1=a2[0:2, :]))
        V(lambda e: e.tensor_scalar_add(out=b2[0:2, :], in0=b2[0:2, :], scalar1=EPS))
        sc.add("vector", lambda e: e.engine_nop(), "dv1", 1)
        sc.wait("scalar", "dv1")
        sc.add("scalar", lambda e: e.activation(out=b2[0:2, :], in_=b2[0:2, :],
                                                func=AF.Sqrt), None)
        sc.on("scalar", lambda e: e.drain())
        sc.add("scalar", lambda e: e.nop(), "sc0", 1)
        sc.wait("vector", "sc0")
        V(lambda e: e.reciprocal(out=b2[0:2, :], in_=b2[0:2, :]))
        V(lambda e: e.tensor_mul(out=a2[0:2, :], in0=gw2[0:2, :], in1=b2[0:2, :]))
        V(lambda e: e.tensor_mul(out=b2[0:2, :], in0=slo[0:2, 0:16], in1=a2[0:2, :]))
        V(lambda e: e.tensor_sub(out=b2[0:2, :], in0=gb2[0:2, :], in1=b2[0:2, :]))
        sc.add("vector", lambda e: e.engine_nop(), "dv1", 1)
        sc.wait("tensor", "dv1")
        sc.add("tensor", lambda e: e.matmul(p_ab[:, 0:16], hpT[0:2, :], a2[0:2, :],
                                            start=True, stop=True), None)
        sc.add("tensor", lambda e: e.matmul(p_ab[:, 16:32], hpT[0:2, :], b2[0:2, :],
                                            start=True, stop=True), "pe1", 1)
        sc.wait("scalar", "pe1")
        sc.add("scalar", lambda e: e.activation(out=acol[:], in_=p_ab[:, 0:16],
                                                func=AF.Copy), None)
        sc.add("scalar", lambda e: e.activation(out=bcol[:], in_=p_ab[:, 16:32],
                                                func=AF.Copy), None)
        sc.on("scalar", lambda e: e.drain())
        sc.add("scalar", lambda e: e.nop(), "sc1", 1)

        sc.wait("tensor", "qG", ldg)
        gdone = {}
        for oc in range(NB):
            par = oc % 2
            lch = f"ch{2 + par}"
            if oc >= 2:
                sc.wait("sync", "dv0", gdone[oc - 2])
            sc.add("sync", lambda e, oc=oc, par=par: e.dma_start(
                out=retslab[:, par * T:(par + 1) * T],
                in_=ret_bf[oc * 128:(oc + 1) * 128, :]), lch, 16)
            rld = mk.val(lch)
            evb = mk.val("ev0")
            for tt in range(4):
                i = oc * 4 + tt
                if i >= 3:
                    sc.wait("tensor", "ev0", evb + tt - 3 if tt >= 3 else None) \
                        if False else sc.wait("tensor", "ev0",
                                              mk.val("ev0") - 2 if mk.val("ev0") >= 2 else 0)
                sc.add("tensor", lambda e, oc=oc, tt=tt, i=i: [
                    e.matmul(p_g[i % 3][:],
                             wgt[:, dc_ * D + oc * 128: dc_ * D + (oc + 1) * 128],
                             xfm[:, dc_ * T + tt * 512: dc_ * T + (tt + 1) * 512],
                             start=(dc_ == 0), stop=(dc_ == NB - 1))
                    for dc_ in range(NB)][-1], "pe0", 1)
                sc.wait("scalar", "pe0")
                if tt == 0 and oc >= 2:
                    sc.wait("scalar", "dv0", gdone[oc - 2])
                sc.add("scalar", lambda e, tt=tt, par=par, i=i: e.activation(
                    out=grow[:, par * T + tt * 512: par * T + (tt + 1) * 512],
                    in_=p_g[i % 3][:], func=AF.Copy), "ev0", 1)
            gslice = lambda par=par: grow[:, par * T:(par + 1) * T]
            sc.on("scalar", lambda e: e.drain())
            if oc >= 1:
                sc.wait("scalar", "dv0", gdone.get(oc - 1, 0))
            sc.add("scalar", lambda e, par=par: e.activation(
                out=t1[:], in_=grow[:, par * T:(par + 1) * T], func=AF.Abs), None)
            sc.on("scalar", lambda e: e.drain())
            sc.add("scalar", lambda e: e.activation(out=t1[:], in_=t1[:], func=AF.Exp,
                                                    scale=-1.0), None)
            sc.on("scalar", lambda e: e.drain())
            sc.add("scalar", lambda e: e.activation(out=t1[:], in_=t1[:], func=AF.Ln,
                                                    bias=1.0), None)
            sc.add("scalar", lambda e, par=par: e.activation(
                out=t2[:], in_=grow[:, par * T:(par + 1) * T], func=AF.Relu), None)
            sc.on("scalar", lambda e: e.drain())
            sc.add("scalar", lambda e: e.nop(), "sc0", 1)
            sc.wait("vector", "sc0")
            sc.add("vector", lambda e: e.tensor_add(out=spb[:], in0=t1[:], in1=t2[:]),
                   None)
            sc.add("vector", lambda e: e.drain(), "dv1", 1)
            sc.wait("scalar", "dv1")
            sc.add("scalar", lambda e: e.activation(out=spb[:], in_=spb[:],
                                                    func=AF.Tanh), None)
            sc.on("scalar", lambda e: e.drain())
            sc.add("scalar", lambda e: e.nop(), "sc0", 1)
            sc.wait("vector", "sc0")
            sc.wait("vector", "sc1")
            sc.wait("vector", lch, rld)
            sc.add("vector", lambda e, oc=oc, par=par: e.tensor_scalar(
                out=yrow[:], in0=retslab[:, par * T:(par + 1) * T],
                scalar1=acol[:, oc:oc + 1], scalar2=bcol[:, oc:oc + 1],
                op0=ALU.mult, op1=ALU.add), None)
            sc.add("vector", lambda e, par=par: e.tensor_mul(
                out=t2[:], in0=grow[:, par * T:(par + 1) * T], in1=spb[:]), None)
            sc.on("vector", lambda e: e.drain())
            if oc >= 2:
                sc.wait("vector", f"ch{4 + par}")
            sc.add("vector", lambda e, par=par: e.tensor_mul(
                out=gb[:, par * T:(par + 1) * T], in0=t2[:], in1=yrow[:]), None)
            sc.add("vector", lambda e: e.drain(), "dv0", 1)
            gdone[oc] = mk.val("dv0")
            sc.wait("scalar", "dv0", gdone[oc])
            sc.add("scalar", lambda e, oc=oc, par=par: e.dma_start(
                out=gated_b[oc * 128:(oc + 1) * 128, :],
                in_=gb[:, par * T:(par + 1) * T]), f"ch{4 + par}", 16)
        sc.emit()
    mk.barrier()
    emit_oproj_ln(mk, "c5b", gated_b, "wo1T", None, x2t, x3t, x3f)



def ph_ffn1(mk):
    nc = mk.nc
    x3f = mk.dram("x3f", [D, T], F32)
    x3t = mk.dram("x3t", [T, D], F32)
    mk.ein("f1g", [D, FF1], I8)
    mk.ein("f1u", [D, FF1], I8)
    mk.ein("f1d", [FF1, D], I8)
    mk.ein("sgc", [128, 64], F32)
    mk.ein("suc", [128, 64], F32)
    mk.ein("sdc", [128, 16], F32)
    mk.ein("c_eye_f32", [128, 128], F32)
    out_t = mk.dram("out_t", [T, D], F32)
    NSEC = 8
    OSEC = FF1 // NSEC

    es0 = ExitStack()
    sb0 = lambda n_, sh, dt: es0.enter_context(nc.sbuf_tensor(n_, sh, dt))
    with es0:
        x4pre = sb0("d_x4", [128, NB * 512], F32)
        sg_sb = sb0("d_sg", [128, 64], F32)
        su_sb = sb0("d_su", [128, 64], F32)
        sd_sb = sb0("d_sd", [128, 16], F32)
        eye_f = sb0("d_eye", [128, 128], F32)

        sc = Sched(mk)
        sc.add("sync", lambda e: e.dma_start(out=sg_sb[:], in_=mk.t["sgc"]), "qS", 16)
        sc.add("sync", lambda e: e.dma_start(out=su_sb[:], in_=mk.t["suc"]), "qS", 16)
        sc.add("sync", lambda e: e.dma_start(out=sd_sb[:], in_=mk.t["sdc"]), "qS", 16)
        sc.add("sync", lambda e: e.dma_start(out=eye_f[:], in_=mk.t["c_eye_f32"]), "qS", 16)
        sc.wait("scalar", "qS")
        sc.wait("vector", "qS")
        sc.wait("tensor", "qS")
        sc.emit()
        mk.barrier()

        for tt in range(4):
            esT = ExitStack()
            sbT = lambda n_, sh, dt: esT.enter_context(nc.sbuf_tensor(n_, sh, dt))
            with esT:
                fuse = sbT(f"d_fu{tt}", [128, 64 * 512], BF16)
                xtt = sbT(f"d_x{tt}", [128, NB * 512], BF16)

                esG = ExitStack()
                sbG = lambda n_, sh, dt: esG.enter_context(nc.sbuf_tensor(n_, sh, dt))
                with esG:
                    wsec = [sbG(f"d_w{tt}_{i}", [128, NB * OSEC], BF16) for i in range(2)]
                    utmp = sbG(f"d_ut{tt}", [128, 2 * 512], F32)
                    p_g = [esG.enter_context(nc.psum_tensor(f"d_pg{tt}_{i}", [128, 512], F32))
                           for i in range(3)]
                    sc = Sched(mk)
                    for dc in range(NB):
                        sc.add("gpsimd", lambda e, dc=dc, tt=tt: e.dma_start(
                            out=xtt[:, dc * 512:(dc + 1) * 512],
                            in_=x3f[dc * 128:(dc + 1) * 128, tt * 512:(tt + 1) * 512]),
                            "qG", 16)
                    x_mark = mk.val("qG")
                    sc.wait("tensor", "qG", x_mark)
                    w_free = [0, 0]
                    stage = 0
                    pvm = {}
                    for mat, wname in ((0, "f1g"), (1, "f1u")):
                        for sec in range(NSEC):
                            buf = stage % 2
                            wch = f"wb{buf}"
                            sc.wait("gpsimd", "pe0", w_free[buf])
                            sc.add("gpsimd", lambda e, wname=wname, sec=sec, buf=buf:
                                   e.dma_start(
                                       out=wsec[buf][:].rearrange("p (hc o) -> p hc o", hc=NB),
                                       in_=mk.t[wname].rearrange("(hc p) o -> p hc o", p=128)
                                           [:, :, sec * OSEC:(sec + 1) * OSEC]), wch, 16)
                            wmk = mk.val(wch)
                            sc.wait("tensor", wch, wmk)
                            for oi in range(8):
                                ocg = sec * 8 + oi
                                i = stage * 8 + oi
                                if i >= 3:
                                    psem, pval = pvm[i - 3]
                                    sc.wait("tensor", psem, pval)
                                sc.add("tensor", lambda e, oi=oi, buf=buf, i=i: [
                                    e.matmul(p_g[i % 3][:],
                                             wsec[buf][:, dc_ * OSEC + oi * 128:
                                                       dc_ * OSEC + oi * 128 + 128],
                                             xtt[:, dc_ * 512:(dc_ + 1) * 512],
                                             start=(dc_ == 0), stop=(dc_ == NB - 1))
                                    for dc_ in range(NB)][-1], "pe0", 1)
                                if mat == 0:
                                    sc.wait("scalar", "pe0")
                                    sc.add("scalar", lambda e, ocg=ocg, i=i: e.activation(
                                        out=fuse[:, ocg * 512:(ocg + 1) * 512],
                                        in_=p_g[i % 3][:], func=AF.Silu,
                                        scale=sg_sb[:, ocg:ocg + 1]), "sc1", 1)
                                else:
                                    sc.wait("vector", "pe0")
                                    sc.add("vector", lambda e, ocg=ocg, i=i, oi=oi: e.tensor_scalar(
                                        out=utmp[:, (oi % 2) * 512:((oi % 2) + 1) * 512],
                                        in0=p_g[i % 3][:], scalar1=su_sb[:, ocg:ocg + 1],
                                        scalar2=None, op0=ALU.mult), None)
                                    sc.on("vector", lambda e: e.drain())
                                    sc.add("vector", lambda e, ocg=ocg, oi=oi: e.tensor_mul(
                                        out=fuse[:, ocg * 512:(ocg + 1) * 512],
                                        in0=fuse[:, ocg * 512:(ocg + 1) * 512],
                                        in1=utmp[:, (oi % 2) * 512:((oi % 2) + 1) * 512]),
                                        "dv1", 1)
                                pvm[i] = (("sc1", mk.val("sc1")) if mat == 0
                                          else ("dv1", mk.val("dv1")))
                            w_free[buf] = mk.val("pe0")
                            stage += 1
                    sc.emit()
                mk.barrier()

                esD = ExitStack()
                sbD = lambda n_, sh, dt: esD.enter_context(nc.sbuf_tensor(n_, sh, dt))
                with esD:
                    wdn = [sbD(f"d_wd{tt}_{i}", [128, 8 * D], BF16) for i in range(2)]
                    p_d = [esD.enter_context(nc.psum_tensor(f"d_pd{tt}_{i}", [128, 512], F32))
                           for i in range(8)]
                    sc = Sched(mk)
                    for hg in range(2):
                        dvb = mk.val("dv1")
                        w_free = [mk.val("pe0")] * 2
                        for og in range(8):
                            buf = og % 2
                            wch = f"wb{buf}"
                            sc.wait("gpsimd", "pe0", w_free[buf])
                            sc.add("gpsimd", lambda e, og=og, buf=buf: e.dma_start(
                                out=wdn[buf][:].rearrange("p (oc h) -> p oc h", oc=8),
                                in_=mk.t["f1d"].rearrange("(oc p) h -> p oc h", p=128)
                                    [:, og * 8:(og + 1) * 8, :]), wch, 16)
                            wmk = mk.val(wch)
                            sc.wait("tensor", wch, wmk)
                            if og == 0 and hg == 0:
                                sc.wait("tensor", "dv1", dvb)
                            if og == 0 and hg == 1:
                                sc.wait("tensor", "dv1", dvb)
                            for oi in range(8):
                                oc = og * 8 + oi

                                def mmd(e, oc=oc, oi=oi, buf=buf, hg=hg):
                                    for hh in range(8):
                                        last = e.matmul(
                                            p_d[hh][:],
                                            wdn[buf][:, oi * D + (hg * 8 + hh) * 128:
                                                     oi * D + (hg * 8 + hh) * 128 + 128],
                                            fuse[:, oc * 512:(oc + 1) * 512],
                                            start=(oc == 0), stop=(oc == 63))
                                    return last
                                sc.add("tensor", mmd, "pe0" if oi == 7 else None, 1)
                            w_free[buf] = mk.val("pe0")
                        sc.wait("vector", "pe0")
                        for hh in range(8):
                            hc = hg * 8 + hh
                            sc.add("vector", lambda e, hh=hh, hc=hc: e.tensor_scalar(
                                out=x4pre[:, hc * 512:(hc + 1) * 512], in0=p_d[hh][:],
                                scalar1=sd_sb[:, hc:hc + 1], scalar2=None,
                                op0=ALU.mult), None)
                        sc.add("vector", lambda e: e.drain(), "dv1", 1)
                    sc.emit()
                mk.barrier()

            esL = ExitStack()
            sbL = lambda n_, sh, dt: esL.enter_context(nc.sbuf_tensor(n_, sh, dt))
            with esL:
                trow = sbL(f"dL_tr{tt}", [128, 2 * D], F32)
                xres = sbL(f"dL_xr{tt}", [128, 2 * D], F32)
                lnrow = sbL(f"dL_ln{tt}", [128, 2 * D], F32)
                sq = sbL(f"dL_sq{tt}", [128, D], F32)
                s1 = sbL(f"dL_s1{tt}", [128, 1], F32)
                s2 = sbL(f"dL_s2{tt}", [128, 1], F32)
                s3 = sbL(f"dL_s3{tt}", [128, 1], F32)
                p_tr = [esL.enter_context(nc.psum_tensor(f"dL_pt{tt}_{i}", [128, 128], F32))
                        for i in range(2)]
                sc = Sched(mk)
                st_m = {0: None, 1: None}
                dv_m = {}
                for tc4 in range(4):
                    tcg = tt * 4 + tc4
                    par = tc4 % 2
                    lch = f"ch{2 + par}"
                    if tc4 >= 2:
                        sc.wait("sync", "dv0", dv_m[tc4 - 2])
                    sc.add("sync", lambda e, tcg=tcg, par=par: e.dma_start(
                        out=xres[:, par * D:(par + 1) * D],
                        in_=x3t[tcg * 128:(tcg + 1) * 128, :]), lch, 16)
                    xmk = mk.val(lch)
                    ev1b = mk.val("ev1")
                    if st_m[par] is not None:
                        sc.wait("scalar", f"ch{4 + par}", st_m[par])
                    for hc in range(NB):
                        if ev1b + hc - 1 > 0:
                            sc.wait("tensor", "ev1", ev1b + hc - 1)
                        sc.add("tensor", lambda e, hc=hc, tc4=tc4: e.transpose(
                            p_tr[hc % 2][:],
                            x4pre[:, hc * 512 + tc4 * 128: hc * 512 + (tc4 + 1) * 128],
                            eye_f[:]), "pe1", 1)
                        sc.wait("scalar", "pe1")
                        sc.add("scalar", lambda e, hc=hc, par=par: e.activation(
                            out=trow[:, par * D + hc * 128: par * D + (hc + 1) * 128],
                            in_=p_tr[hc % 2][:], func=AF.Copy), "ev1", 1)
                    sc.on("scalar", lambda e: e.drain())
                    sc.add("scalar", lambda e: e.nop(), "sc1", 1)
                    sc.wait("vector", "sc1")
                    sc.wait("vector", lch, xmk)
                    sc.add("vector", lambda e, par=par: e.tensor_add(
                        out=xres[:, par * D:(par + 1) * D],
                        in0=trow[:, par * D:(par + 1) * D],
                        in1=xres[:, par * D:(par + 1) * D]), "dv0", 1)
                    sc.on("vector", lambda e: e.drain())
                    sc.wait("scalar", "dv0")
                    ln_row(sc, mk, xres[:, par * D:(par + 1) * D],
                           lnrow[:, par * D:(par + 1) * D], sq, s1, s2, s3, D)
                    dv_m[tc4] = mk.val("dv0")
                    sc.wait("scalar", "dv0")
                    sc.add("scalar", lambda e, tcg=tcg, par=par: e.dma_start(
                        out=out_t[tcg * 128:(tcg + 1) * 128, :],
                        in_=lnrow[:, par * D:(par + 1) * D]), f"ch{4 + par}", 16)
                    st_m[par] = mk.val(f"ch{4 + par}")
                sc.emit()
            mk.barrier()



PHASES = {
    "qkv0": (ph_qkv0, [], ["q0f", "k0f", "v0t"]),
    "attn0": (ph_attn0, ["q0f", "k0a", "v0a"], ["ao_f"]),
    "wo0ln": (ph_wo0ln, ["ao_f"], ["x1t", "x1f"]),
    "moe": (lambda mk: (ph_moeprep(mk), ph_moe(mk)), ["x1f", "x1t"], ["x2t", "x2f"]),
    "qkv1": (ph_qkv1, ["x2f"], ["q1t", "k1t", "v1t"]),
    "ret": (ph_ret, ["q1t", "k1t", "v1t", "k1a", "v1a"], ["ret_bf", "st4"]),
    "gw1": (ph_gw1, ["ret_bf", "st4a", "x2f", "x2t"], ["x3t", "x3f"]),
    "ffn1": (ph_ffn1, ["x3f", "x3t"], ["out_t"]),
}


def build_phase(name):
    fn, pins, pouts = PHASES[name]
    mk = MK(f"mk_{name}", ext_in=set(pins), ext_out=set(pouts))
    mk.begin()
    fn(mk)
    mk.finish()
    return mk


LAUNCH_ORDER = ["qkv0", "attn0", "wo0ln", "moe", "qkv1", "ret", "gw1", "ffn1"]


def ph_exchange(mk, pairs):
    if not hasattr(mk, "exch_mark"):
        mk.exch_mark = {}
    for src, dst in pairs:
        sap = mk.t[src]
        mk.dram(dst, [2 * sap.shape[0]] + list(sap.shape[1:]), sap.dtype)
    sc = Sched(mk)
    for src, dst in pairs:
        sc.add("gpsimd", lambda e, src=src, dst=dst: e.collective_compute(
            "AllGather", ALU.bypass,
            replica_groups=[[0, 1], [2, 3], [4, 5], [6, 7]],
            ins=[mk.t[src].opt()], outs=[mk.t[dst].opt()]), "cc", 1)
        mk.exch_mark[dst] = mk.val("cc")
    sc.emit()


def build_fused():
    mk = MK("mk_fused", ext_in=(), ext_out=("out_t",))
    mk.fused_mode = True
    mk.begin()
    steps = [("qkv0", ph_qkv0),
             ("moeprep", ph_moeprep),
             ("attn0", ph_attn0),
             ("wo0ln", ph_wo0ln),
             ("moe", ph_moe),
             ("qkv1", ph_qkv1),
             ("exch1", lambda m: ph_exchange(m, [("k1t", "k1a"), ("v1t", "v1a")])),
             ("ret", ph_ret),
             ("exch2", lambda m: ph_exchange(m, [("st4", "st4a")])),
             ("gw1", ph_gw1),
             ("ffn1", ph_ffn1)]
    for nm, fn in steps:
        with mk.nc.named_scope(nm):
            fn(mk)
    mk.finish()
    return mk


def kernel_fused(inp):
    sh, pc = host_prep(inp)
    mk = build_fused()
    in_maps = []
    for c in range(8):
        feed = {}
        for tn in mk.input_names:
            if tn in sh:
                feed[tn] = sh[tn]
            elif tn in pc[c]:
                feed[tn] = pc[c][tn]
            else:
                raise KeyError(f"fused: unfed input {tn}")
        in_maps.append(feed)
    res = run_bass_kernel_spmd(mk.nc, in_maps, core_ids=list(range(8)))
    out = np.empty((4, 4096, 2048), np.float32)
    for c in range(8):
        b, hf = c // 2, c % 2
        out[b, hf * T:(hf + 1) * T, :] = res.results[c]["out_t"]
    return out


def kernel(**inp):
    inp = {k: np.asarray(v) for k, v in inp.items()}
    if FUSED:
        return kernel_fused(inp)
    sh, pc = host_prep(inp)
    inter = [dict() for _ in range(8)]

    for name in LAUNCH_ORDER:
        mk = build_phase(name)
        _, pins, pouts = PHASES[name]
        in_maps = []
        for c in range(8):
            feed = {}
            for tn in mk.input_names:
                if tn in inter[c]:
                    feed[tn] = inter[c][tn]
                elif tn in sh:
                    feed[tn] = sh[tn]
                elif tn in pc[c]:
                    feed[tn] = pc[c][tn]
                else:
                    raise KeyError(f"{name}: unfed input {tn}")
            in_maps.append(feed)
        res = run_bass_kernel_spmd(mk.nc, in_maps, core_ids=list(range(8)))
        for c in range(8):
            for po in pouts:
                inter[c][po] = res.results[c][po]
        if name == "qkv0":
            for c in range(8):
                p0, p1 = (c // 2) * 2, (c // 2) * 2 + 1
                inter[c]["k0a"] = np.concatenate(
                    [inter[p0]["k0f"], inter[p1]["k0f"]], axis=0)
                inter[c]["v0a"] = np.concatenate(
                    [inter[p0]["v0t"], inter[p1]["v0t"]], axis=0)
        if name == "qkv1":
            for c in range(8):
                p0, p1 = (c // 2) * 2, (c // 2) * 2 + 1
                inter[c]["k1a"] = np.concatenate(
                    [inter[p0]["k1t"], inter[p1]["k1t"]], axis=0)
                inter[c]["v1a"] = np.concatenate(
                    [inter[p0]["v1t"], inter[p1]["v1t"]], axis=0)
        if name == "ret":
            for c in range(8):
                p0, p1 = (c // 2) * 2, (c // 2) * 2 + 1
                inter[c]["st4a"] = np.concatenate(
                    [inter[p0]["st4"], inter[p1]["st4"]], axis=0)

    out = np.empty((4, 4096, 2048), np.float32)
    for c in range(8):
        b, hf = c // 2, c % 2
        out[b, hf * T:(hf + 1) * T, :] = inter[c]["out_t"]
    return out
